# version 08
# baseline (speedup 1.0000x reference)
import contextlib
import numpy as np
import concourse.bass as bass
import concourse.mybir as mybir
from concourse.bass_utils import run_bass_kernel_spmd

F32 = mybir.dt.float32
F16 = mybir.dt.float16
AF = mybir.ActivationFunctionType
ALU = mybir.AluOpType
AX = mybir.AxisListType

NCORES = 8
N = 4096
E = 1024
H = 16
M = 256
HID = 4096
T = 512
NT = N // T
TC = T // 128
EC = E // 128
HC = HID // 128
CSTAB = 20.0
KEPS = 1e-4
LNEPS = 1e-5
CQ = float(64.0 ** -0.25)

ENGINES = ("sync", "vector", "scalar", "tensor", "gpsimd")


class Sched:
    DSTREAMS = ("dxa", "dwt", "dwq", "dsb", "dst", "dgo", "dpr", "dscalar")

    def __init__(self):
        self.prog = {e: [] for e in ENGINES}
        self.cnt = {e: 0 for e in ENGINES}
        self.dma = {k: 0 for k in self.DSTREAMS}
        self.cc = 0

    def _rw(self, waits):
        out = []
        for w in waits:
            if not w:
                continue
            nm, v = w
            if nm in self.dma:
                v = self.dma[nm]
                if v == 0:
                    continue
            out.append((nm, v))
        return out

    def stage(self, eng, fns, waits=()):
        w = self._rw(waits)
        if self.cnt[eng] > 0:
            w.append((eng, self.cnt[eng]))
        self.cnt[eng] += 1
        kind = "pestage" if eng == "tensor" else "stage"
        self.prog[eng].append((kind, w, fns, (eng, self.cnt[eng])))
        return (eng, self.cnt[eng])

    def dmas(self, eng, fns, waits=(), stream=None):
        nm = stream or ("dscalar" if eng == "scalar" else
                        ("dst" if eng == "gpsimd" else "dwt"))
        w = self._rw(waits)
        self.dma[nm] += 16 * len(fns)
        self.prog[eng].append(("dma", w, fns, (nm, self.dma[nm])))
        return (nm, self.dma[nm])

    def collective(self, fn, waits=()):
        self.cc += 1
        self.prog["gpsimd"].append(("ccstage", self._rw(waits), [fn],
                                    ("cc", self.cc)))
        return ("cc", self.cc)

    def mark(self, name):
        self.marks = getattr(self, "marks", {})
        self.marks[name] = {e: len(self.prog[e]) for e in ENGINES}

    def truncate(self, name):
        for e in ENGINES:
            self.prog[e] = self.prog[e][: self.marks[name][e]]
        self.cnt = {e: 0 for e in ENGINES}
        self.dma = {k: 0 for k in self.DSTREAMS}
        self.cc = 0
        for e in ENGINES:
            for kind, waits, fns, inc in self.prog[e]:
                if inc is None:
                    continue
                nm, v = inc
                if nm in self.dma:
                    self.dma[nm] = max(self.dma[nm], v)
                elif nm == "cc":
                    self.cc = max(self.cc, v)
                else:
                    self.cnt[nm] = max(self.cnt[nm], v)

    def barrier(self):
        w = [(e, self.cnt[e]) for e in ENGINES if self.cnt[e] > 0]
        w += [(k, v) for k, v in self.dma.items() if v > 0]
        if self.cc:
            w.append(("cc", self.cc))
        return w


def build_program():
    nc = bass.Bass("TRN2", target_bir_lowering=False, debug=False)

    xin = nc.dram_tensor("x", [N, E], F32, kind="ExternalInput").ap()
    Wq = nc.dram_tensor("Wq", [E, E], F32, kind="ExternalInput").ap()
    Wk = nc.dram_tensor("Wk", [E, E], F32, kind="ExternalInput").ap()
    Wv = nc.dram_tensor("Wv", [E, E], F32, kind="ExternalInput").ap()
    Wo = nc.dram_tensor("Wo", [E, E], F32, kind="ExternalInput").ap()
    bqd = nc.dram_tensor("bq", [E], F32, kind="ExternalInput").ap()
    bkd = nc.dram_tensor("bk", [E], F32, kind="ExternalInput").ap()
    bvd = nc.dram_tensor("bv", [E], F32, kind="ExternalInput").ap()
    bod = nc.dram_tensor("bo", [E], F32, kind="ExternalInput").ap()
    projd = nc.dram_tensor("proj", [M, 64], F32, kind="ExternalInput").ap()
    w1d = nc.dram_tensor("w1", [HID, E], F32, kind="ExternalInput").ap()
    b1d = nc.dram_tensor("b1", [HID], F32, kind="ExternalInput").ap()
    w2d = nc.dram_tensor("w2", [E, HID], F32, kind="ExternalInput").ap()
    b2d = nc.dram_tensor("b2", [E], F32, kind="ExternalInput").ap()
    gammad = nc.dram_tensor("gamma", [E], F32, kind="ExternalInput").ap()
    betad = nc.dram_tensor("beta", [E], F32, kind="ExternalInput").ap()
    outd = nc.dram_tensor("out", [N, E], F32, kind="ExternalOutput").ap()

    x2s = nc.dram_tensor("x2s", [N, E], F32)
    wsw = nc.dram_tensor("wsw", [4 * E, E], F16)
    wsp = nc.dram_tensor("wsp", [M, 128], F16)
    ws1 = nc.dram_tensor("ws1", [HID, E], F16)
    ws2 = nc.dram_tensor("ws2", [E, HID], F16)
    rmio = nc.dram_tensor("rmio", [128, 16], F32)
    rmo = nc.dram_tensor("rmo", [128, 16], F32)
    srow = nc.dram_tensor("srow", [128, 1], F32)

    identd = nc.inline_tensor(np.eye(128, dtype=np.float16), "identc")
    ident32d = nc.inline_tensor(np.eye(128, dtype=np.float32), "ident32c")
    ind_np = np.zeros((128, 2), dtype=np.float16)
    ind_np[0:64, 0] = 1.0
    ind_np[64:128, 1] = 1.0
    indd = nc.inline_tensor(ind_np, "indc")

    s = Sched()
    es = contextlib.ExitStack()
    with es:
        sbuf = lambda n_, sh, dt_: es.enter_context(nc.sbuf_tensor(n_, sh, dt_))
        WP = sbuf("WP", [128, 65536], F16)
        HP = sbuf("HP", [128, 12320], F32)
        TP = sbuf("TP", [128, 9472], F16)
        MS = sbuf("MS", [128, 872], F32)
        CXP = sbuf("CXP", [128, 2080], F16)
        BR = sbuf("BR", [128, 2193], F16)
        IDN = sbuf("IDN", [128, 128], F16)
        IDN32 = sbuf("IDN32", [128, 128], F32)
        IND = sbuf("IND", [128, 2], F16)
        PJ = sbuf("PJ", [128, 256], F16)
        KC = sbuf("KC", [128, 8], F32)

        PAB_A = es.enter_context(nc.psum_tensor("PAB_A", [128, 512], F32))
        PAB_B = es.enter_context(nc.psum_tensor("PAB_B", [128, 512], F32))
        DDP = es.enter_context(nc.psum_tensor("DDP", [128, 2048], F32))
        CTP = es.enter_context(nc.psum_tensor("CTP", [128, 1024], F32))
        PA16 = PAB_A.bitcast(F16)
        PB16 = PAB_B.bitcast(F16)

        sems = {}
        for nm in ("sync", "vector", "scalar", "tensor", "gpsimd",
                   "dxa", "dwt", "dwq", "dsb", "dst", "dgo", "dpr", "dscalar", "cc"):
            sems[nm] = es.enter_context(nc.semaphore(nm))

        TPf = TP.bitcast(F32)
        HPf = HP.bitcast(F16)

        RA, RB, RXL, RC, RD = 0, 8192, 16384, 49152, 57344
        KN = 4352

        def w2t_region(kc):
            return [RA, RB, RC, RD][kc // 8] + (kc % 8) * 1024

        m_bn = MS[:, 0:12]
        m_mv = MS[:, 16:24]
        m_sd = MS[:, 24:28]
        m_rstd = MS[:, 28:32]
        m_negm = MS[:, 32:36]
        m_cbk = MS[:, 36:44]
        m_cbq = MS[:, 44:52]
        m_b1h = MS[:, 52:84]
        m_RM = MS[:, 84:100]
        m_diag = MS[:, 100:164]
        m_asum = MS[:, 800:864]
        m_nst = MS[:, 164:228]
        m_edb = MS[:, 228:292]
        m_stab = MS[:, 292:293]
        m_cstab = MS[:, 293:294]
        m_red2 = MS[:, 296:298]
        m_dinv = MS[:, 300:304]
        m_wred = MS[:, 304:306]
        m_sca = MS[0:1, 306:316]
        m_rx1 = MS[:, 320:321]
        m_mx1 = MS[:, 321:322]
        m_s1c = MS[:, 322:323]
        m_s1i = MS[:, 324:328]
        m_rx2 = MS[:, 328:332]
        m_mx2 = MS[:, 332:336]
        m_s2c = MS[:, 336:340]
        m_s2i = MS[:, 340:344]
        m_f2 = MS[:, 344:348]
        m_sw1b = MS[:, 348:349]
        m_sw2b = MS[:, 349:350]
        m_mw1db = MS[:, 350:351]
        m_mw2db = MS[:, 351:352]
        o32row = MS[0:1, 416:544]
        m_bn2 = MS[:, 352:376]
        m_mv2 = MS[:, 376:380]
        m_rstd2 = MS[:, 380:382]
        m_sd2 = MS[:, 390:392]
        m_rx1b = MS[:, 382:384]
        m_mx1b = MS[:, 384:386]
        m_s1cb = MS[:, 386:388]
        m_mw1d = MS[:, 392:393]
        m_mw2d = MS[:, 393:394]
        m_L1col = MS[:, 394:395]
        m_L1 = MS[0:1, 395:396]
        m_cLt = MS[:, 396:397]
        m_mLt = MS[:, 397:398]
        m_eD = MS[:, 398:399]
        m_Lprev = MS[:, 399:400]
        m_mS = MS[:, 400:401]
        m_f = MS[:, 401:402]
        m_sx = MS[:, 402:410]
        m_dinv8 = MS[:, 864:872]
        m_srow = MS[0:1, 544:672]
        m_s1row = MS[0:1, 544:800]

        r_bv = BR[0:1, 0:1024]
        ones16row = BR[0:1, 1024:1152]
        r_bo = BR[32:33, 0:1024]
        ones16row32 = BR[32:33, 1024:1152]
        r_f2 = BR[64:65, 0:512]
        r_b2 = BR[64:65, 512:1536]
        r_eps = BR[32:33, 1152:2192]
        ones16col = BR[:, 2192:2193]
        k4096h = BR[0:1, 1536:1537]

        k1536 = KC[:, 0:1]
        kmC = KC[:, 1:2]
        kpC = KC[:, 2:3]
        kCQ = KC[:, 3:4]
        kmhalf = KC[:, 4:5]
        kones32 = KC[:, 5:6]
        kLNEPS = KC[:, 6:7]

        def dma1(eng, outap, inap, waits=(), stream=None, **kw):
            return s.dmas(eng, [lambda e, o=outap, i=inap, kw=kw:
                                e.dma_start(out=o, in_=i, **kw)], waits=waits,
                          stream=stream)

        s.dmas("sync", [
            lambda e: e.dma_start(out=IDN[:], in_=identd.ap()),
            lambda e: e.dma_start(out=IDN32[:], in_=ident32d.ap()),
            lambda e: e.dma_start(out=IND[:], in_=indd.ap()),
        ])
        st_ones = s.stage("vector", [
            lambda e: e.memset(KC[:, 0:1], 1536.0),
            lambda e: e.memset(KC[:, 1:2], -CSTAB),
            lambda e: e.memset(KC[:, 2:3], CSTAB),
            lambda e: e.memset(KC[:, 3:4], CQ),
            lambda e: e.memset(KC[:, 4:5], -0.5),
            lambda e: e.memset(KC[:, 5:6], 1.0),
            lambda e: e.memset(KC[:, 6:7], LNEPS),
            lambda e: e.memset(o32row, 1.0),
            lambda e: e.memset(ones16row, 1.0),
            lambda e: e.memset(ones16row32, 1.0),
            lambda e: e.memset(ones16col, 1.0),
            lambda e: e.memset(m_RM, -3.0e38),
            lambda e: e.memset(KC[:, 7:8], 4096.0),
            lambda e: e.memset(k4096h, 4096.0),
            lambda e: e.memset(m_Lprev, -1.0e30),
        ])

        row_ld = s.dmas("gpsimd", [
            lambda e: e.dma_start(out=r_bv, in_=bvd[None, :]),
            lambda e: e.dma_start(out=r_bo, in_=bod[None, :]),
        ], stream="dpr")
        wlist = [(Wk, RA), (Wv, RB), (Wq, RC), (Wo, RD)]
        cast_done = []
        for wi, (wap, dst) in enumerate(wlist):
            cast_done.append(dma1("gpsimd", wsw[wi * E:(wi + 1) * E, :], wap, stream="dpr"))
        proj_cast = s.dmas("gpsimd", [
            lambda e: e.dma_start(out=wsp[:, 0:64], in_=projd),
            lambda e: e.dma_start(out=wsp[:, 64:128], in_=projd),
        ], stream="dpr")
        b2_cast = dma1("gpsimd", r_b2, b2d[None, :], stream="dpr")
        wt_loads = []
        for wi, (wap, dst) in enumerate(wlist):
            fns = []
            for kc in range(EC):
                fns.append(lambda e, wi=wi, kc=kc, dst=dst: e.dma_start(
                    out=WP[:, dst + kc * 1024: dst + (kc + 1) * 1024],
                    in_=wsw[wi * E:(wi + 1) * E, kc * 128:(kc + 1) * 128],
                    transpose=True))
            wt_loads.append(s.dmas("sync", fns, waits=[cast_done[wi]]))
        pj_load = s.dmas("sync", [
            lambda e: e.dma_start(out=PJ[:, :], in_=wsp[:, :], transpose=True),
        ], waits=[proj_cast])

        bias_ld = s.dmas("sync", [
            lambda e: e.dma_start(out=m_cbk, in_=bkd.rearrange("(c p) -> p c", p=128),
                                  allow_slow_non_contiguous=True),
            lambda e: e.dma_start(out=m_cbq, in_=bqd.rearrange("(c p) -> p c", p=128),
                                  allow_slow_non_contiguous=True),
            lambda e: e.dma_start(out=m_b1h, in_=b1d.rearrange("(c p) -> p c", p=128),
                                  allow_slow_non_contiguous=True),
        ])
        sc_bias = s.stage("vector", [
            lambda e: e.tensor_scalar_mul(m_cbk, m_cbk, CQ),
            lambda e: e.tensor_scalar_mul(m_cbq, m_cbq, CQ),
        ], waits=[bias_ld])

        def emit_ln(src_of_tc, dst_of_tc, wait_load, dst_w=()):
            for tc in range(TC):
                s.stage("vector", [
                    (lambda e, tc=tc: e.bn_stats(m_bn[:, 0:6], src_of_tc(tc)[:, 0:512])),
                    (lambda e, tc=tc: e.bn_stats(m_bn[:, 6:12], src_of_tc(tc)[:, 512:1024])),
                ], waits=[wait_load] if tc == 0 else [])
                s.stage("vector", [
                    (lambda e, tc=tc: e.bn_aggr(m_mv[:, 2 * tc:2 * tc + 2], m_bn[:, 0:12])),
                ])
            veps = s.stage("vector", [
                lambda e: e.tensor_scalar_add(m_sd, m_mv[:, 1:8:2], LNEPS)])
            ssq = s.stage("scalar", [
                lambda e: e.activation(m_sd, m_sd, AF.Sqrt)], waits=[veps])
            s.stage("vector", [
                lambda e: e.reciprocal(m_rstd, m_sd)], waits=[ssq])
            vnm = s.stage("vector", [
                lambda e: e.scalar_tensor_tensor(m_negm, m_mv[:, 0:8:2], -1.0,
                                                 m_rstd, ALU.mult, ALU.mult)])
            last = None
            for tc in range(TC):
                last = s.stage("scalar", [
                    (lambda e, tc=tc: e.activation(
                        dst_of_tc(tc), src_of_tc(tc), AF.Identity,
                        bias=m_negm[:, tc:tc + 1], scale=m_rstd[:, tc:tc + 1])),
                ], waits=[vnm] + list(dst_w) if tc == 0 else [])
            return last

        def emit_tblock(src_of_k, dst_of_k, nk, wait_src, ps16, extra_w=()):
            last_ev = None
            for g0 in range(0, nk, 8):
                gn = min(8, nk - g0)
                w = [wait_src] + list(extra_w)
                if last_ev is not None:
                    w.append(last_ev)
                tp_ = s.stage("tensor", [
                    (lambda e, k=g0 + i, sl=i: e.transpose(
                        ps16[:, sl * 128:(sl + 1) * 128], src_of_k(k), IDN[:]))
                    for i in range(gn)
                ], waits=w)
                last_ev = s.stage("vector", [
                    (lambda e, k=g0 + i, sl=i: e.tensor_copy(
                        dst_of_k(k), ps16[:, sl * 128:(sl + 1) * 128]))
                    for i in range(gn)
                ], waits=[tp_])
            return last_ev

        def emit_proj(wreg, cb_col, t, dst_of_ec, waits):
            evs = []
            for ec in range(EC):
                pab = PAB_A if ec % 2 == 0 else PAB_B
                w = list(waits) if ec <= 1 else []
                if len(evs) >= 2:
                    w.append(evs[-2])
                mm = s.stage("tensor", [
                    (lambda e, kc=kc, ec=ec, pab=pab: e.matmul(
                        pab[:, 0:512],
                        WP[:, wreg + kc * 1024 + ec * 128: wreg + kc * 1024 + ec * 128 + 128],
                        WP[:, RXL + kc * 4096 + t * 512: RXL + kc * 4096 + (t + 1) * 512],
                        start=(kc == 0), stop=(kc == EC - 1)))
                    for kc in range(EC)
                ], waits=w)
                evs.append(s.stage("scalar", [
                    (lambda e, ec=ec, pab=pab: e.activation(
                        dst_of_ec(ec), pab[:, 0:512], AF.Identity,
                        bias=cb_col[:, ec:ec + 1], scale=kCQ)),
                ], waits=[mm]))
            return evs[-1]

        def emit_diag(src_off, wait_kn, extra_first=()):
            dmm = []
            for ec in range(EC):
                p = ec % 2
                wsq = [wait_kn] if ec == 0 else []
                if len(dmm) >= 2:
                    wsq.append(dmm[ec - 2])
                sq_ = s.stage("scalar", [
                    (lambda e, ec=ec, p=p: e.activation(
                        TP[:, 8448 + p * 512: 8448 + (p + 1) * 512],
                        TP[:, src_off + ec * 512: src_off + (ec + 1) * 512],
                        AF.Square)),
                ], waits=wsq)
                wm = [sq_]
                if ec == 0:
                    wm += list(extra_first)
                dmm.append(s.stage("tensor", [
                    (lambda e, ec=ec, tc=tc, p=p: e.matmul(
                        DDP[:, tc * 16 + 2 * ec: tc * 16 + 2 * ec + 2],
                        TP[:, 8448 + p * 512 + tc * 128: 8448 + p * 512 + tc * 128 + 128],
                        IND[:], start=(ec == 0 and tc == 0),
                        stop=(ec == EC - 1 and tc == TC - 1)))
                    for tc in range(TC)
                ], waits=wm))
            return s.stage("vector", [
                lambda e: e.tensor_copy(m_diag, DDP[:, 0:64])], waits=[dmm[-1]])

        wq_items = ([("w1", ch, None) for ch in range(HC)] +
                    [("w2", ch, ss) for ch in range(EC) for ss in range(4)])
        wq1 = {"i": 0, "last": None}
        wq2 = {"i": 0, "last": None, "mean": None}
        wq_stg = HP[:, 4096:5120]
        wq_ta = HP[:, 5120:6144]
        wq_tb = HP[:, 6144:7168]
        wq_o16 = HPf[:, 14336:15360]

        def wq_src(it):
            wname, ch, ss = it
            if wname == "w1":
                return w1d[ch * 128:(ch + 1) * 128, :], m_asum[:, ch:ch + 1], m_sw1b
            return (w2d[ch * 128:(ch + 1) * 128, ss * 1024:(ss + 1) * 1024],
                    m_asum[:, 32 + ch * 4 + ss: 33 + ch * 4 + ss], m_sw2b)

        def wq_dst(it):
            wname, ch, ss = it
            if wname == "w1":
                return ws1[ch * 128:(ch + 1) * 128, :]
            return ws2[ch * 128:(ch + 1) * 128, ss * 1024:(ss + 1) * 1024]

        def emit_wq_pass1(n):
            for _ in range(n):
                i = wq1["i"]
                if i >= len(wq_items):
                    return
                wq1["i"] += 1
                src_, acol, _ = wq_src(wq_items[i])
                w = [wq1["last"]] if wq1["last"] else []
                ld = dma1("sync", wq_stg, src_, waits=w, stream="dwq")
                wq1["last"] = s.stage("scalar", [
                    (lambda e, acol=acol: e.activation(
                        wq_stg, wq_stg, AF.Abs, accum_out=acol)),
                ], waits=[ld])

        def emit_wq_mean():
            red = s.stage("vector", [
                lambda e: e.reduce_sum(m_wred[:, 0:1], m_asum[:, 0:32], AX.X),
                lambda e: e.reduce_sum(m_wred[:, 1:2], m_asum[:, 32:64], AX.X),
            ], waits=[wq1["last"]])
            pm = s.stage("tensor", [
                lambda e: e.matmul(CTP[0:1, 0:1], kones32, m_wred[:, 0:1],
                                   start=True, stop=False),
                lambda e: e.matmul(CTP[0:1, 1:2], kones32, m_wred[:, 1:2],
                                   start=False, stop=True),
            ], waits=[red])
            s.stage("vector", [
                lambda e: e.tensor_scalar(m_sca[:, 0:2], CTP[0:1, 0:2],
                                          1.0 / (HID * E), 1e-5, ALU.mult, ALU.max),
            ], waits=[pm])
            sc = s.stage("vector", [
                lambda e: e.reciprocal(m_sca[:, 2:4], m_sca[:, 0:2]),
                lambda e: e.tensor_scalar_mul(m_sca[:, 4:6], m_sca[:, 0:2], 1.0 / 127.0),
            ])
            bc = s.stage("tensor", [
                lambda e: e.matmul(CTP[:, 0:1], o32row, m_sca[:, 2:3],
                                   start=True, stop=False),
                lambda e: e.matmul(CTP[:, 1:2], o32row, m_sca[:, 3:4],
                                   start=False, stop=False),
                lambda e: e.matmul(CTP[:, 2:3], o32row, m_sca[:, 4:5],
                                   start=False, stop=False),
                lambda e: e.matmul(CTP[:, 3:4], o32row, m_sca[:, 5:6],
                                   start=False, stop=False),
                lambda e: e.matmul(CTP[:, 4:5], o32row, m_sca[:, 0:1],
                                   start=False, stop=False),
                lambda e: e.matmul(CTP[:, 5:6], o32row, m_sca[:, 1:2],
                                   start=False, stop=True),
            ], waits=[sc])
            wq2["mean"] = s.stage("vector", [
                lambda e: e.tensor_copy(m_sw1b, CTP[:, 0:1]),
                lambda e: e.tensor_copy(m_sw2b, CTP[:, 1:2]),
                lambda e: e.tensor_copy(m_mw1db, CTP[:, 2:3]),
                lambda e: e.tensor_copy(m_mw2db, CTP[:, 3:4]),
                lambda e: e.tensor_copy(m_mw1d, CTP[:, 4:5]),
                lambda e: e.tensor_copy(m_mw2d, CTP[:, 5:6]),
            ], waits=[bc])

        def emit_wq_pass2(n):
            for _ in range(n):
                i = wq2["i"]
                if i >= len(wq_items):
                    return
                wq2["i"] += 1
                it = wq_items[i]
                src_, _, swb = wq_src(it)
                w = [wq2["mean"]]
                if wq2["last"]:
                    w.append(wq2["last"])
                ld = dma1("sync", wq_stg, src_, waits=w, stream="dwq")
                g1 = s.stage("vector", [
                    (lambda e, swb=swb: e.tensor_scalar(
                        wq_ta, wq_stg, swb[:, 0:1], 0.5, ALU.mult, ALU.is_gt)),
                    (lambda e, swb=swb: e.tensor_scalar(
                        wq_tb, wq_stg, swb[:, 0:1], -0.5, ALU.mult, ALU.is_lt)),
                ], waits=[ld] + ([wq2["last"]] if wq2["last"] else []))
                g2 = s.stage("vector", [
                    lambda e: e.tensor_sub(wq_o16, wq_ta, wq_tb),
                ])
                wq2["last"] = dma1("gpsimd", wq_dst(it), wq_o16, waits=[g2],
                                   stream="dgo")

        s.mark("prologue")
        prol_w = s.barrier()
        ctx_init = s.stage("vector", [
            lambda e: e.memset(HP[:, 8192:10272], 0.0),
        ], waits=prol_w)
        ctx_last = None
        last_xuse = None
        for t in range(NT):
            w = list(prol_w) if t == 0 else []
            if last_xuse is not None:
                w.append(last_xuse)
            xld = s.dmas("sync", [
                (lambda e, t=t: e.dma_start(
                    out=HP[:, 0:4096].rearrange("p (a m) -> p a m", a=TC),
                    in_=xin[t * 512:(t + 1) * 512, :].rearrange("(a p) m -> p a m", p=128)))
            ], waits=w, stream="dxa")
            lnd = emit_ln(lambda tc: HP[:, tc * 1024:(tc + 1) * 1024],
                          lambda tc: TP[:, tc * 1024:(tc + 1) * 1024], xld,
                          dst_w=[ctx_last] if ctx_last else [])
            last_xuse = lnd
            trd = emit_tblock(
                lambda k: TP[:, (k // EC) * 1024 + (k % EC) * 128:
                             (k // EC) * 1024 + (k % EC) * 128 + 128],
                lambda k, t=t: WP[:, RXL + (k % EC) * 4096 + t * 512 + (k // EC) * 128:
                                  RXL + (k % EC) * 4096 + t * 512 + (k // EC) * 128 + 128],
                TC * EC, lnd, PA16)
            if t == 0:
                s.mark("p1tr")
            kev = emit_proj(RA, m_cbk, t,
                            lambda ec: TP[:, KN + ec * 512: KN + (ec + 1) * 512],
                            [trd, wt_loads[0], sc_bias, st_ones] if t == 0
                            else [trd])
            if t == 0:
                s.mark("p1kp")
            vev = []
            for tc in range(TC):
                for half in range(2):
                    pab = PAB_A if half == 0 else PAB_B
                    wv = [kev] if tc == 0 else []
                    if t == 0 and tc == 0 and half == 0:
                        wv += [wt_loads[1], row_ld]
                    if len(vev) >= 2:
                        wv.append(vev[-2])
                    mmv = s.stage("tensor", [
                        (lambda e, kc=kc, tc=tc, half=half, pab=pab, t=t: e.matmul(
                            pab[:, 0:512],
                            WP[:, RXL + kc * 4096 + t * 512 + tc * 128:
                               RXL + kc * 4096 + t * 512 + tc * 128 + 128],
                            WP[:, RB + kc * 1024 + half * 512: RB + kc * 1024 + (half + 1) * 512],
                            start=(kc == 0), stop=False))
                        for kc in range(EC)
                    ] + [
                        (lambda e, half=half, pab=pab: e.matmul(
                            pab[:, 0:512], ones16row,
                            BR[0:1, half * 512:(half + 1) * 512],
                            start=False, stop=True)),
                    ], waits=wv)
                    vev.append(s.stage("scalar", [
                        (lambda e, tc=tc, half=half, pab=pab: e.activation(
                            TP[:, tc * 1040 + half * 520: tc * 1040 + half * 520 + 520]
                            .rearrange("p (h d) -> p h d", d=65)[:, :, 0:64],
                            pab[:, 0:512].rearrange("p (h d) -> p h d", d=64),
                            AF.Copy)),
                    ], waits=[mmv]))
            ones_v = s.stage("vector", [
                lambda e: e.memset(
                    TP[:, 0:4160].rearrange("p (a h d) -> p a h d", a=TC, d=65)
                    [:, :, :, 64:65], 1.0),
            ], waits=[vev[-1]])
            dgd = emit_diag(KN, kev,
                            extra_first=[ctx_last] if ctx_last else list(prol_w))
            prev_red = None
            for hp in range(8):
                wm = [kev, dgd] if hp == 0 else [prev_red]
                if t == 0 and hp == 0:
                    wm.append(pj_load)
                mm = s.stage("tensor", [
                    mmf
                    for tc in range(TC)
                    for mmf in [
                        (lambda e, hp=hp, tc=tc: e.matmul(
                            DDP[:, tc * 256: tc * 256 + 256],
                            TP[0:64, KN + hp * 512 + tc * 128: KN + hp * 512 + tc * 128 + 128],
                            PJ[0:64, :], start=(tc % 2 == 0), stop=(tc % 2 == 1),
                            tile_position=(0, 0))),
                        (lambda e, hp=hp, tc=tc: e.matmul(
                            DDP[:, 1024 + tc * 256: 1024 + tc * 256 + 256],
                            TP[64:128, KN + hp * 512 + tc * 128: KN + hp * 512 + tc * 128 + 128],
                            PJ[64:128, :], start=(tc % 2 == 0), stop=(tc % 2 == 1),
                            tile_position=(64, 0))),
                    ]
                ], waits=wm)
                if t == 0 and hp == 0:
                    s.mark("p1d1")
                prev_red = s.stage("vector", [
                    lambda e: e.reduce_max(
                        m_red2,
                        DDP[:, 0:2048].rearrange("p (h m) -> p h m", h=2),
                        AX.X),
                ], waits=[mm])
                rm_done = s.stage("vector", [
                    (lambda e, hp=hp: e.tensor_max(
                        m_RM[:, 2 * hp:2 * hp + 2], m_RM[:, 2 * hp:2 * hp + 2],
                        m_red2)),
                ])
            if t == 0:
                s.mark("p1dd")
            ra = s.stage("vector", [
                lambda e: e.reduce_max(m_L1col, m_RM, AX.X)])
            ra_last = ra
            rowm = s.stage("tensor", [
                lambda e: e.matmul(CTP[0:1, 0:128], m_L1col, IDN32[:],
                                   start=True, stop=True)], waits=[ra])
            rL = s.stage("vector", [
                lambda e: e.reduce_max(m_L1, CTP[0:1, 0:128], AX.X)], waits=[rowm])
            bcm = s.stage("tensor", [
                lambda e: e.matmul(CTP[:, 0:1], o32row, m_L1,
                                   start=True, stop=True)], waits=[rL])
            vL = s.stage("vector", [
                lambda e: e.tensor_scalar(m_cLt, CTP[:, 0:1], -1.0, CSTAB,
                                          ALU.mult, ALU.add),
                lambda e: e.tensor_scalar_mul(m_mLt, CTP[:, 0:1], -1.0),
            ], waits=[bcm])
            eD = s.stage("scalar", [
                lambda e: e.activation(m_eD, m_Lprev, AF.Exp, bias=m_mLt[:, 0:1]),
            ], waits=[vL])
            s.stage("vector", [
                lambda e: e.tensor_scalar_mul(m_Lprev, m_mLt, -1.0),
            ], waits=[eD])
            edb = s.stage("scalar", [
                lambda e: e.activation(m_edb, m_diag, AF.Exp,
                                       bias=m_cLt[:, 0:1], scale=kmhalf),
            ], waits=[dgd])
            prev_ctx_ev = None
            prev_readers = [prev_red]
            last_kf = {0: None, 1: None}
            last_cmmz = {0: None, 1: None}
            cmm = None
            for hp in range(8):
                wm = [kev, edb] + prev_readers
                hmm = s.stage("tensor", [
                    mmf
                    for tc in range(TC)
                    for mmf in [
                        (lambda e, hp=hp, tc=tc: e.matmul(
                            DDP[:, tc * 256: tc * 256 + 256],
                            TP[0:64, KN + hp * 512 + tc * 128: KN + hp * 512 + tc * 128 + 128],
                            PJ[0:64, :], start=(tc % 2 == 0), stop=(tc % 2 == 1),
                            tile_position=(0, 0))),
                        (lambda e, hp=hp, tc=tc: e.matmul(
                            DDP[:, 1024 + tc * 256: 1024 + tc * 256 + 256],
                            TP[64:128, KN + hp * 512 + tc * 128: KN + hp * 512 + tc * 128 + 128],
                            PJ[64:128, :], start=(tc % 2 == 0), stop=(tc % 2 == 1),
                            tile_position=(64, 0))),
                    ]
                ], waits=wm)
                prev_readers = []
                exw = [hmm]
                for sub_ in range(2):
                    if last_kf.get(sub_) is not None:
                        exw.append(last_kf[sub_])
                exb = s.stage("scalar", [
                    (lambda e, sub=sub: e.activation(
                        HP[:, 10272 + sub * 1024: 10272 + (sub + 1) * 1024],
                        DDP[:, sub * 1024:(sub + 1) * 1024], AF.Exp, bias=kmC))
                    for sub in range(2)
                ], waits=exw)
                prev_readers.append(exb)
                for sub in range(2):
                    h = 2 * hp + sub
                    kz = TP[:, 8448:9472] if sub == 0 else HPf[:, 15360:16384]
                    kfw = [exb, ones_v]
                    if last_cmmz.get(sub) is not None:
                        kfw.append(last_cmmz[sub])
                    kf = s.stage("vector", [
                        (lambda e, sub=sub, h=h, tc=tc, kz=kz: e.tensor_single_scalar(
                            kz[:, tc * 256:(tc + 1) * 256],
                            HP[:, 10272 + sub * 1024 + tc * 256:
                               10272 + sub * 1024 + (tc + 1) * 256],
                            m_edb[:, tc * 16 + h: tc * 16 + h + 1],
                            ALU.mult))
                        for tc in range(TC)
                    ], waits=kfw)
                    last_kf[sub] = kf
                    prev_readers.append(kf)
                    zb = h % 2
                    wcm = [kf]
                    if prev_ctx_ev is not None and zb == 0:
                        wcm.append(prev_ctx_ev)
                    cmm = s.stage("tensor", [
                        (lambda e, ms=ms, tc=tc, h=h, zb=zb, kz=kz: e.matmul(
                            CTP[:, zb * 512 + ms * 65: zb * 512 + ms * 65 + 65],
                            kz[:, tc * 256 + ms * 128: tc * 256 + ms * 128 + 128],
                            TP[:, tc * 1040 + h * 65: tc * 1040 + h * 65 + 65],
                            start=(ms == 0 and tc == 0), stop=(ms == 1 and tc == TC - 1)))
                        for ms in range(2) for tc in range(TC)
                    ], waits=wcm)
                    last_cmmz[sub] = cmm
                prev_ctx_ev = s.stage("vector", [
                    (lambda e, hp=hp: e.scalar_tensor_tensor(
                        HP[:, 8192 + hp * 260: 8192 + (hp + 1) * 260]
                        .rearrange("p (z c) -> p z c", c=130),
                        HP[:, 8192 + hp * 260: 8192 + (hp + 1) * 260]
                        .rearrange("p (z c) -> p z c", c=130),
                        m_eD[:, 0:1],
                        CTP[:, 0:1024].rearrange("p (z c) -> p z c", z=2)[:, :, 0:130],
                        ALU.mult, ALU.add)),
                ], waits=[cmm, ctx_init])
                prev_readers.append(prev_ctx_ev)
            ctx_last = prev_ctx_ev
            if t < 3:
                emit_wq_pass1(22)
            elif t == 3:
                emit_wq_pass1(len(wq_items))
                emit_wq_mean()
                emit_wq_pass2(13)
            else:
                emit_wq_pass2(13)
        emit_wq_pass2(len(wq_items))
        wq_final = wq2["last"]
        s.mark("p1")

        ar_o = dma1("gpsimd", rmio.ap(), m_RM, waits=[rm_done], stream="dgo")
        import os as _os
        if _os.environ.get("MK_NOCC"):
            cc1 = dma1("gpsimd", rmo.ap(), rmio.ap(), waits=[ar_o], stream="dgo")
        else:
            cc1 = s.collective(
                lambda e: e.collective_compute(
                    "AllReduce", ALU.max, replica_groups=[list(range(NCORES))],
                    ins=[rmio.ap().opt()], outs=[rmo.ap().opt()]),
                waits=[ar_o])
        ar_i = dma1("gpsimd", m_RM, rmo.ap(), waits=[cc1, ra_last], stream="dgo")
        rr = s.stage("vector", [
            lambda e: e.reduce_max(m_red2[:, 0:1], m_RM, AX.X)], waits=[ar_i])
        sb_o = dma1("gpsimd", srow.ap(), m_red2[:, 0:1], waits=[rr], stream="dgo")
        sb_i = dma1("sync", m_srow, srow.ap().rearrange("a b -> b a"),
                    waits=[sb_o], stream="dsb", allow_slow_non_contiguous=True)
        sr = s.stage("vector", [
            lambda e: e.reduce_max(m_sca[:, 6:7], m_srow, AX.X)], waits=[sb_i])
        sbm = s.stage("tensor", [
            lambda e: e.matmul(CTP[:, 0:1], o32row, m_sca[:, 6:7],
                               start=True, stop=True)], waits=[sr, ctx_last])
        sev = s.stage("vector", [
            lambda e: e.tensor_copy(m_stab, CTP[:, 0:1]),
            lambda e: e.tensor_scalar_mul(m_mS, CTP[:, 0:1], -1.0),
        ], waits=[sbm])
        fst = s.stage("scalar", [
            lambda e: e.activation(m_f, m_Lprev, AF.Exp, bias=m_mS[:, 0:1]),
        ], waits=[sev])
        s.mark("ar")

        sxr = s.stage("vector", [
            (lambda e, kc=kc: e.reduce_sum(
                m_sx[:, kc:kc + 1],
                WP[:, RXL + kc * 4096: RXL + (kc + 1) * 4096], AX.X))
            for kc in range(EC)
        ])
        sxc = s.stage("vector", [
            lambda e: e.tensor_copy(TP[:, 8448:8456], m_sx),
        ])
        vsm = s.stage("tensor", [
            (lambda e, kc=kc, half=half: e.matmul(
                CTP[0:1, half * 512:(half + 1) * 512],
                TP[:, 8448 + kc: 8449 + kc],
                WP[:, RB + kc * 1024 + half * 512: RB + kc * 1024 + (half + 1) * 512],
                start=(kc == 0), stop=False))
            for half in range(2) for kc in range(EC)
        ] + [
            (lambda e, half=half: e.matmul(
                CTP[0:1, half * 512:(half + 1) * 512],
                k4096h, r_bv[:, half * 512:(half + 1) * 512],
                start=False, stop=True))
            for half in range(2)
        ], waits=[sxc, ctx_last])
        vse = s.stage("vector", [
            lambda e: e.tensor_copy(HP[0:1, 0:1024], CTP[0:1, 0:1024]),
        ], waits=[vsm])
        v2a = s.stage("vector", [
            (lambda e, ms=ms: e.tensor_scalar_mul(
                HP[0:1, 1024:3104].rearrange("a (h m c) -> a h m c", m=2, c=65)
                [:, :, ms:ms + 1, 0:64],
                HP[0:1, 0:1024].rearrange("a (h one d) -> a h one d", one=1, d=64),
                KEPS))
            for ms in range(2)
        ] + [
            lambda e: e.memset(
                HP[0:1, 1024:3104].rearrange("a (h m c) -> a h m c", m=2, c=65)
                [:, :, :, 64:65], KEPS * 4096.0),
        ], waits=[vse])
        vsb = s.stage("tensor", [
            (lambda e, c=c: e.matmul(
                DDP[:, c * 512:(c + 1) * 512], o32row,
                HP[0:1, 1024 + c * 512: 1024 + (c + 1) * 512],
                start=True, stop=True))
            for c in range(4)
        ] + [
            lambda e: e.matmul(CTP[:, 0:32], o32row, HP[0:1, 3072:3104],
                               start=True, stop=True),
        ], waits=[v2a])
        vsc = s.stage("vector", [
            (lambda e, c=c: e.tensor_copy(HP[:, c * 512:(c + 1) * 512],
                                          DDP[:, c * 512:(c + 1) * 512]))
            for c in range(4)
        ] + [
            lambda e: e.tensor_copy(HP[:, 2048:2080], CTP[:, 0:32]),
        ], waits=[vsb])
        cxc = s.stage("vector", [
            lambda e: e.scalar_tensor_tensor(
                CXP[:], HP[:, 8192:10272], m_f[:, 0:1], HP[:, 0:2080],
                ALU.mult, ALU.add),
        ], waits=[vsc, fst])
        ebm = s.stage("tensor", [
            (lambda e, h=h, ms=ms: e.matmul(
                DDP[0:1, h * 128: h * 128 + 65], ones16col,
                CXP[:, h * 130 + ms * 65: h * 130 + ms * 65 + 65],
                start=(ms == 0 and h % 4 == 0), stop=(ms == 1 and h % 4 == 3)))
            for h in range(H) for ms in range(2)
        ], waits=[cxc])
        ebv = s.stage("vector", [
            lambda e: e.tensor_scalar_mul(
                r_eps.rearrange("a (h c) -> a h c", c=65),
                DDP[0:1, 0:2048].rearrange("p (h c) -> p h c", h=16)[:, :, 0:65], KEPS),
        ], waits=[ebm])

        w2ab = s.dmas("sync", [
            (lambda e, kc=kc: e.dma_start(
                out=WP[:, w2t_region(kc): w2t_region(kc) + 1024],
                in_=ws2[:, kc * 128:(kc + 1) * 128], transpose=True))
            for kc in range(16)
        ], waits=[wq_final, vsm])

        s.mark("p2")
        x2st = [None] * NT
        p3a_last = None
        for t in range(NT):
            w0 = [ctx_last] if t == 0 else [p3a_last, x2st[t - 1]]
            xl = s.dmas("sync", [
                (lambda e, t=t: e.dma_start(
                    out=HP[:, 0:4096].rearrange("p (a m) -> p a m", a=TC),
                    in_=xin[t * 512:(t + 1) * 512, :].rearrange("(a p) m -> p a m", p=128)))
            ], waits=w0 + ([cxc] if t == 0 else []), stream="dxa")
            qev = emit_proj(RC, m_cbq, t,
                            lambda ec: TP[:, KN + ec * 512: KN + (ec + 1) * 512], w0)
            dgd = emit_diag(KN, qev,
                            extra_first=w0 + ([ebv] if t == 0 else []))
            prev_out_ev = None
            prev_readers = [dgd]
            last_qf = {0: None, 1: None}
            last_eqf = {0: None, 1: None}
            last_ommz = {0: None, 1: None}
            for hp in range(8):
                wm = [qev] + prev_readers
                hmm = s.stage("tensor", [
                    mmf
                    for tc in range(TC)
                    for mmf in [
                        (lambda e, hp=hp, tc=tc: e.matmul(
                            DDP[:, tc * 256: tc * 256 + 256],
                            TP[0:64, KN + hp * 512 + tc * 128: KN + hp * 512 + tc * 128 + 128],
                            PJ[0:64, :], start=(tc % 2 == 0), stop=(tc % 2 == 1),
                            tile_position=(0, 0))),
                        (lambda e, hp=hp, tc=tc: e.matmul(
                            DDP[:, 1024 + tc * 256: 1024 + tc * 256 + 256],
                            TP[64:128, KN + hp * 512 + tc * 128: KN + hp * 512 + tc * 128 + 128],
                            PJ[64:128, :], start=(tc % 2 == 0), stop=(tc % 2 == 1),
                            tile_position=(64, 0))),
                    ]
                ], waits=wm)
                nst = s.stage("vector", [
                    (lambda e, sub=sub, tc=tc, hp=hp: e.reduce_max(
                        m_nst[:, tc * 16 + 2 * hp + sub: tc * 16 + 2 * hp + sub + 1],
                        DDP[:, sub * 1024 + tc * 256: sub * 1024 + (tc + 1) * 256],
                        AX.X, negate=True))
                    for sub in range(2) for tc in range(TC)
                ], waits=[hmm])
                edq = s.stage("vector", [
                    (lambda e, hp=hp: e.scalar_tensor_tensor(
                        m_edb.rearrange("p (a h) -> p a h", a=TC)[:, :, 2 * hp:2 * hp + 2],
                        m_diag.rearrange("p (a h) -> p a h", a=TC)[:, :, 2 * hp:2 * hp + 2],
                        -0.5,
                        m_nst.rearrange("p (a h) -> p a h", a=TC)[:, :, 2 * hp:2 * hp + 2],
                        ALU.mult, ALU.add)),
                ], waits=[nst, dgd])
                edq2 = s.stage("scalar", [
                    (lambda e, hp=hp: e.activation(
                        m_edb.rearrange("p (a h) -> p a h", a=TC)[:, :, 2 * hp:2 * hp + 2],
                        m_edb.rearrange("p (a h) -> p a h", a=TC)[:, :, 2 * hp:2 * hp + 2],
                        AF.Exp, bias=kpC)),
                ], waits=[edq])
                prev_readers = [nst]
                exw = [hmm, edq2]
                for sub in range(2):
                    if last_qf.get(sub) is not None:
                        exw.append(last_qf[sub])
                exb = s.stage("scalar", [
                    (lambda e, sub=sub: e.activation(
                        HP[:, 10272 + sub * 1024: 10272 + (sub + 1) * 1024],
                        DDP[:, sub * 1024:(sub + 1) * 1024], AF.Exp, bias=kmC))
                    for sub in range(2)
                ], waits=exw)
                prev_readers.append(exb)
                ozoneP = [[DDP[:, z * 512: z * 512 + 65] for z in range(4)],
                          [DDP[:, z * 512 + 128: z * 512 + 193] for z in range(4)]]
                ozcolP = [[DDP[:, z * 512 + 64: z * 512 + 65] for z in range(4)],
                          [DDP[:, z * 512 + 192: z * 512 + 193] for z in range(4)]]
                oznumP = [[DDP[:, z * 512: z * 512 + 64] for z in range(4)],
                          [DDP[:, z * 512 + 128: z * 512 + 192] for z in range(4)]]
                omms = []
                for sub in range(2):
                    h = 2 * hp + sub
                    par = sub
                    qz = TP[:, 8448:9472] if par == 0 else HPf[:, 16384:18432]
                    pz = PA16 if par == 0 else PB16
                    ozone = ozoneP[par]
                    qfw = [exb]
                    if last_ommz.get(par) is not None:
                        qfw.append(last_ommz[par])
                    qf = s.stage("vector", [
                        (lambda e, sub=sub, tc=tc, h=h, qz=qz: e.tensor_scalar(
                            qz[:, tc * 256:(tc + 1) * 256],
                            HP[:, 10272 + sub * 1024 + tc * 256:
                               10272 + sub * 1024 + (tc + 1) * 256],
                            m_edb[:, tc * 16 + h: tc * 16 + h + 1], KEPS,
                            ALU.mult, ALU.add))
                        for tc in range(TC)
                    ], waits=qfw)
                    last_qf[sub] = qf
                    tqw = [qf]
                    if last_eqf.get(par) is not None:
                        tqw.append(last_eqf[par])
                    tqf = s.stage("tensor", [
                        (lambda e, k=k, qz=qz, pz=pz: e.transpose(
                            pz[:, k * 128:(k + 1) * 128],
                            qz[:, k * 128:(k + 1) * 128],
                            IDN[:]))
                        for k in range(8)
                    ], waits=tqw)
                    eqf = s.stage("scalar" if par == 0 else "vector", [
                        (lambda e, k=k, qz=qz, pz=pz, par=par: (
                            e.activation(qz[:, k * 128:(k + 1) * 128],
                                         pz[:, k * 128:(k + 1) * 128], AF.Copy)
                            if par == 0 else
                            e.tensor_copy(qz[:, k * 128:(k + 1) * 128],
                                          pz[:, k * 128:(k + 1) * 128])))
                        for k in range(8)
                    ], waits=[tqf])
                    last_eqf[par] = eqf
                    ommw = [eqf, exb, cxc, ebv]
                    if sub == 0 and prev_out_ev is not None:
                        ommw.append(prev_out_ev)
                    omm = s.stage("tensor", [
                        mmf
                        for tc in range(TC)
                        for mmf in [
                            (lambda e, tc=tc, h=h, qz=qz, ozone=ozone, sub=sub: e.matmul(
                                ozone[tc],
                                qz[:, tc * 256: tc * 256 + 128],
                                CXP[:, h * 130: h * 130 + 65],
                                start=(sub == 0), stop=False)),
                            (lambda e, tc=tc, h=h, qz=qz, ozone=ozone: e.matmul(
                                ozone[tc],
                                qz[:, tc * 256 + 128: tc * 256 + 256],
                                CXP[:, h * 130 + 65: h * 130 + 130],
                                start=False, stop=False)),
                            (lambda e, tc=tc, h=h, ozone=ozone, sub=sub: e.matmul(
                                ozone[tc],
                                ones16row32, r_eps[:, h * 65: h * 65 + 65],
                                start=False, stop=(sub == 1))),
                        ]
                    ], waits=ommw)
                    last_ommz[par] = omm
                    omms.append(omm)
                dvs = s.stage("vector", [
                    (lambda e, sub=sub, tc=tc: e.reciprocal(
                        m_dinv8[:, sub * 4 + tc: sub * 4 + tc + 1],
                        ozcolP[sub][tc]))
                    for sub in range(2) for tc in range(TC)
                ], waits=[omms[1]])
                prev_out_ev = s.stage("scalar", [
                    (lambda e, sub=sub, tc=tc, h=2 * hp + sub: e.activation(
                        TP[:, tc * 1024 + h * 64: tc * 1024 + h * 64 + 64],
                        oznumP[sub][tc], AF.Copy,
                        scale=m_dinv8[:, sub * 4 + tc: sub * 4 + tc + 1]))
                    for sub in range(2) for tc in range(TC)
                ], waits=[dvs])
                prev_readers.append(prev_out_ev)
            trd = emit_tblock(
                lambda k: TP[:, (k // EC) * 1024 + (k % EC) * 128:
                             (k // EC) * 1024 + (k % EC) * 128 + 128],
                lambda k: TP[:, KN + (k % EC) * 512 + (k // EC) * 128:
                             KN + (k % EC) * 512 + (k // EC) * 128 + 128],
                TC * EC, prev_out_ev, PB16)
            x2ev = []
            for tc in range(TC):
                for half in range(2):
                    pab = PAB_A if half == 0 else PAB_B
                    wv = [trd, xl] if tc == 0 else []
                    if len(x2ev) >= 2:
                        wv.append(x2ev[-2])
                    mmo = s.stage("tensor", [
                        (lambda e, kc=kc, tc=tc, half=half, pab=pab: e.matmul(
                            pab[:, 0:512],
                            TP[:, KN + kc * 512 + tc * 128: KN + kc * 512 + tc * 128 + 128],
                            WP[:, RD + kc * 1024 + half * 512: RD + kc * 1024 + (half + 1) * 512],
                            start=(kc == 0), stop=False))
                        for kc in range(EC)
                    ] + [
                        (lambda e, half=half, pab=pab: e.matmul(
                            pab[:, 0:512], ones16row32,
                            BR[32:33, half * 512:(half + 1) * 512],
                            start=False, stop=True)),
                    ], waits=wv)
                    x2ev.append(s.stage("vector", [
                        (lambda e, tc=tc, half=half, pab=pab: e.tensor_add(
                            HP[:, 4096 + tc * 1024 + half * 512:
                               4096 + tc * 1024 + (half + 1) * 512],
                            pab[:, 0:512],
                            HP[:, tc * 1024 + half * 512: tc * 1024 + (half + 1) * 512])),
                    ], waits=[mmo]))
            p3a_last = x2ev[-1]
            x2st[t] = s.dmas("gpsimd", [
                (lambda e, t=t, tc=tc: e.dma_start(
                    out=x2s[(t * TC + tc) * 128:(t * TC + tc + 1) * 128, :],
                    in_=HP[:, 4096 + tc * 1024: 4096 + (tc + 1) * 1024]))
                for tc in range(TC)
            ], waits=[x2ev[-1]])

        s.mark("p3a")
        p3ab = s.barrier()
        w1ld = s.dmas("sync", [
            (lambda e, fc=fc: e.dma_start(
                out=WP[:, RXL + fc * 4096: RXL + (fc + 1) * 4096],
                in_=ws1[:, fc * 128:(fc + 1) * 128], transpose=True))
            for fc in range(EC)
        ], waits=p3ab)
        w2cd = s.dmas("scalar", [
            (lambda e, kc=kc: e.dma_start(
                out=WP[:, w2t_region(kc): w2t_region(kc) + 1024],
                in_=ws2[:, kc * 128:(kc + 1) * 128], transpose=True))
            for kc in range(16, 32)
        ], waits=p3ab)
        gb_last = None
        for idx, vecd in enumerate([gammad, betad]):
            for hlf in range(4):
                gl = dma1("sync", m_s1row,
                          vecd[None, hlf * 256:(hlf + 1) * 256],
                          waits=list(p3ab) + ([gb_last] if gb_last else []))
                gm = s.stage("tensor", [
                    lambda e: e.matmul(CTP[:, 0:256], o32row, m_s1row,
                                       start=True, stop=True),
                ], waits=[gl] + ([gb_last] if gb_last else []))
                gb_last = s.stage("vector", [
                    (lambda e, idx=idx, hlf=hlf: e.tensor_copy(
                        CXP[:, idx * 1024 + hlf * 256: idx * 1024 + (hlf + 1) * 256],
                        CTP[:, 0:256])),
                ], waits=[gm])

        s.mark("bnd")
        T2 = 256
        NT2 = N // T2
        TC2 = 2
        p3bw = s.barrier()
        prev_store = None
        prev_oev = None
        for t2 in range(NT2):
            w0 = list(p3bw) if t2 == 0 else [prev_store]
            xq_done = []
            x2la = s.dmas("sync", [
                (lambda e, g=t2 * TC2: e.dma_start(
                    out=TPf[:, 2048:3072],
                    in_=x2s[g * 128:(g + 1) * 128, 0:1024])),
            ], waits=list(w0) + [x2st[t2 // 2]], stream="dxa")
            x2lb = s.dmas("sync", [
                (lambda e, g=t2 * TC2 + 1: e.dma_start(
                    out=TPf[:, 3072:4096],
                    in_=x2s[g * 128:(g + 1) * 128, 0:1024])),
            ], waits=[], stream="dxa")
            s.stage("vector", [
                lambda e: e.bn_stats(m_bn2[:, 0:6], TPf[:, 2048:2560]),
                lambda e: e.bn_stats(m_bn2[:, 6:12], TPf[:, 2560:3072]),
                lambda e: e.bn_stats(m_bn2[:, 12:18], TPf[:, 3072:3584]),
                lambda e: e.bn_stats(m_bn2[:, 18:24], TPf[:, 3584:4096]),
            ], waits=[x2la, x2lb])
            ag = s.stage("vector", [
                lambda e: e.bn_aggr(m_mv2[:, 0:2], m_bn2[:, 0:12]),
                lambda e: e.bn_aggr(m_mv2[:, 2:4], m_bn2[:, 12:24]),
            ])
            rs = s.stage("scalar", [
                lambda e: e.activation(m_sd2[:, 0:1], m_mv2[:, 1:2],
                                       AF.Sqrt, bias=kLNEPS),
                lambda e: e.activation(m_sd2[:, 1:2], m_mv2[:, 3:4],
                                       AF.Sqrt, bias=kLNEPS),
            ], waits=[ag])
            s.stage("vector", [
                (lambda e, tc=tc: e.scalar_tensor_tensor(
                    HP[:, tc * 1024:(tc + 1) * 1024],
                    TPf[:, 2048 + tc * 1024: 2048 + (tc + 1) * 1024],
                    m_mv2[:, 2 * tc: 2 * tc + 1],
                    CXP[:, 0:1024], ALU.subtract, ALU.mult))
                for tc in range(TC2)
            ])
            s.stage("vector", [
                lambda e: e.reciprocal(m_rstd2[:, 0:2], m_sd2[:, 0:2]),
            ], waits=[rs])
            s.stage("vector", [
                (lambda e, tc=tc: e.scalar_tensor_tensor(
                    HP[:, 2048 + tc * 1024: 2048 + (tc + 1) * 1024],
                    HP[:, tc * 1024:(tc + 1) * 1024],
                    m_rstd2[:, tc:tc + 1],
                    CXP[:, 1024:2048], ALU.mult, ALU.add))
                for tc in range(TC2)
            ])
            s.stage("vector", [
                (lambda e, tc=tc: e.reduce_max(
                    m_rx1b[:, tc:tc + 1],
                    HP[:, 2048 + tc * 1024: 2048 + (tc + 1) * 1024],
                    AX.X, apply_absolute_value=True))
                for tc in range(TC2)
            ])
            s.stage("vector", [
                lambda e: e.tensor_scalar(m_mx1b[:, 0:2], m_rx1b[:, 0:2],
                                          1e-5, 1.0 / 127.0, ALU.max, ALU.mult),
            ])
            rcp = s.stage("vector", [
                lambda e: e.reciprocal(m_s1cb[:, 0:2], m_mx1b[:, 0:2]),
                lambda e: e.tensor_single_scalar(m_s1i[:, 0:2], m_mx1b[:, 0:2],
                                                 m_mw1d[:, 0:1], ALU.mult),
            ])
            u1 = s.stage("scalar", [
                (lambda e, tc=tc: e.activation(
                    TP[:, tc * 1024:(tc + 1) * 1024],
                    HP[:, 2048 + tc * 1024: 2048 + (tc + 1) * 1024],
                    AF.Copy, bias=1536.0, scale=m_s1cb[:, tc:tc + 1]))
                for tc in range(TC2)
            ], waits=[rcp])
            for tc in range(TC2):
                tp_ = s.stage("tensor", [
                    (lambda e, fc=i, tc=tc: e.transpose(
                        PA16[:, fc * 128:(fc + 1) * 128],
                        TP[:, tc * 1024 + fc * 128: tc * 1024 + fc * 128 + 128],
                        IDN[:]))
                    for i in range(8)
                ], waits=[u1] + ([xq_done[-1]] if xq_done else []))
                xq_done.append(s.stage("vector", [
                    (lambda e, fc=i, tc=tc: e.tensor_scalar_add(
                        TP[:, 2048 + fc * 256 + tc * 128: 2048 + fc * 256 + tc * 128 + 128],
                        PA16[:, fc * 128:(fc + 1) * 128], -1536.0))
                    for i in range(8)
                ], waits=[tp_]))
            s1o = s.dmas("gpsimd", [
                lambda e: e.dma_start(out=rmio[:, 0:2], in_=m_s1i[:, 0:2]),
            ], waits=[xq_done[-1]], stream="dgo")
            s1i_ = s.dmas("sync", [
                lambda e: e.dma_start(out=m_s1row[:, 0:256].rearrange("a (c p) -> a c p", p=128),
                                      in_=rmio[:, 0:2].rearrange("p c -> c p")[None],
                                      allow_slow_non_contiguous=True),
            ], waits=[s1o], stream="dsb")
            sbm1 = s.stage("tensor", [
                (lambda e, tc=tc: e.matmul(
                    CTP[:, tc * 128:(tc + 1) * 128], o32row,
                    m_s1row[:, tc * 128:(tc + 1) * 128],
                    start=(tc == 0), stop=(tc == TC2 - 1)))
                for tc in range(TC2)
            ], waits=[s1i_])
            sbe1 = s.stage("vector", [
                lambda e: e.tensor_copy(TPf[:, 0:256], CTP[:, 0:256]),
            ], waits=[sbm1])
            sc1_hist = []
            m1_done = None
            for r in range(4):
                if r % 2 == 0:
                    zones = [DDP[:, 0:512], DDP[:, 512:1024],
                             DDP[:, 1024:1536], DDP[:, 1536:2048]]
                else:
                    zones = [PAB_A[:, 0:512], PAB_B[:, 0:512],
                             CTP[:, 0:512], CTP[:, 512:1024]]
                if r == 0:
                    wv = [xq_done[-1], w1ld]
                elif r == 1:
                    wv = [sbe1]
                else:
                    wv = [sc1_hist[r - 2]]
                mmh = s.stage("tensor", [
                    (lambda e, fc=fc, hc=r * 8 + j, z=zones[j // 2],
                     o=(j % 2) * 256,
                     st=(j % 2 == 0 and fc == 0),
                     sp=(j % 2 == 1 and fc == EC - 1): e.matmul(
                        z[:, o:o + 256],
                        WP[:, RXL + fc * 4096 + hc * 128: RXL + fc * 4096 + hc * 128 + 128],
                        TP[:, 2048 + fc * 256: 2048 + (fc + 1) * 256],
                        start=st, stop=sp))
                    for j in range(8)
                    for fc in range(EC)
                ], waits=wv)
                sc1 = s.stage("vector", [
                    (lambda e, hc=r * 8 + j, z=zones[j // 2], o=(j % 2) * 256:
                     e.tensor_mul(
                        HP[:, hc * 256:(hc + 1) * 256], z[:, o:o + 256],
                        TPf[:, 0:256]))
                    for j in range(8)
                ], waits=[mmh])
                sc1_hist.append(sc1)
                gl = s.stage("scalar", [
                    (lambda e, hc=r * 8 + j: e.activation(
                        HP[:, hc * 256:(hc + 1) * 256],
                        HP[:, hc * 256:(hc + 1) * 256],
                        AF.Gelu, bias=m_b1h[:, hc:hc + 1]))
                    for j in range(8)
                ], waits=[sc1])
                if r == 0:
                    m1_done = s.stage("vector", [
                        lambda e: e.tensor_copy(HP[:, 8192:10240], HP[:, 0:2048]),
                    ], waits=[gl])
                else:
                    m1_done = s.stage("vector", [
                        (lambda e, r=r: e.tensor_max(
                            HP[:, 8192:10240], HP[:, 8192:10240],
                            HP[:, r * 2048:(r + 1) * 2048])),
                    ], waits=[gl, m1_done])
            m1_done = s.stage("vector", [
                lambda e: e.tensor_max(HP[:, 8192:9216], HP[:, 8192:9216],
                                       HP[:, 9216:10240]),
            ], waits=[m1_done])
            m1_done = s.stage("vector", [
                lambda e: e.tensor_max(HP[:, 8192:8704], HP[:, 8192:8704],
                                       HP[:, 8704:9216]),
            ])
            m1_done = s.stage("vector", [
                lambda e: e.tensor_max(HP[:, 8192:8448], HP[:, 8192:8448],
                                       HP[:, 8448:8704]),
            ])
            m132 = s.stage("tensor", [
                (lambda e, tc=tc: e.transpose(
                    CTP[:, tc * 128:(tc + 1) * 128],
                    HP[:, 8192 + tc * 128: 8192 + (tc + 1) * 128], IDN32[:]))
                for tc in range(TC2)
            ], waits=[m1_done])
            s.stage("vector", [
                (lambda e, tc=tc: e.reduce_max(
                    m_rx2[:, tc:tc + 1], CTP[:, tc * 128:(tc + 1) * 128], AX.X))
                for tc in range(TC2)
            ], waits=[m132])
            s.stage("vector", [
                lambda e: e.tensor_scalar(m_mx2[:, 0:2], m_rx2[:, 0:2],
                                          1e-5, 1.0 / 127.0, ALU.max, ALU.mult),
            ])
            s.stage("vector", [
                lambda e: e.reciprocal(m_s2c[:, 0:2], m_mx2[:, 0:2]),
                lambda e: e.tensor_single_scalar(m_s2i[:, 0:2], m_mx2[:, 0:2],
                                                 m_mw2d[:, 0:1], ALU.mult),
            ])
            rx2 = s.stage("vector", [
                lambda e: e.tensor_single_scalar(m_f2[:, 0:2], m_s2c[:, 0:2],
                                                 m_sw2b[:, 0:1], ALU.mult),
            ])
            s2o = s.dmas("gpsimd", [
                lambda e: e.dma_start(out=rmio[:, 0:2], in_=m_s2c[:, 0:2]),
                lambda e: e.dma_start(out=rmio[:, 2:4], in_=m_f2[:, 0:2]),
            ], waits=[rx2], stream="dgo")
            s2i_ = s.dmas("sync", [
                lambda e: e.dma_start(out=m_s1row[:, 0:256].rearrange("a (c p) -> a c p", p=128),
                                      in_=rmio[:, 0:2].rearrange("p c -> c p")[None],
                                      allow_slow_non_contiguous=True),
            ], waits=[s2o], stream="dsb")
            f2i_ = s.dmas("gpsimd", [
                lambda e: e.dma_start(out=r_f2[:, 0:256].rearrange("a (c p) -> a c p", p=128),
                                      in_=rmio[:, 2:4].rearrange("p c -> c p")[None],
                                      allow_slow_non_contiguous=True),
            ], waits=[s2o], stream="dgo")
            sbm2 = s.stage("tensor", [
                (lambda e, tc=tc: e.matmul(
                    CTP[:, tc * 128:(tc + 1) * 128], o32row,
                    m_s1row[:, tc * 128:(tc + 1) * 128],
                    start=(tc == 0), stop=(tc == TC2 - 1)))
                for tc in range(TC2)
            ], waits=[s2i_, rx2])
            sbe2 = s.stage("vector", [
                lambda e: e.tensor_copy(TPf[:, 256:512], CTP[:, 0:256]),
            ], waits=[sbm2])
            hq_prev = None
            for k0 in range(0, HC, 8):
                wq_ = [sbe2]
                if hq_prev is not None:
                    wq_.append(hq_prev)
                mul = s.stage("vector", [
                    (lambda e, kc=k0 + i: e.tensor_mul(
                        HP[:, kc * 256:(kc + 1) * 256],
                        HP[:, kc * 256:(kc + 1) * 256], TPf[:, 256:512]))
                    for i in range(8)
                ], waits=wq_)
                def _adds(kcs):
                    return [
                        (lambda e, kc=kc: e.tensor_scalar_add(
                            HPf[:, kc * 256:(kc + 1) * 256],
                            HP[:, kc * 256:(kc + 1) * 256], 1536.0))
                        for kc in kcs
                    ]
                if k0 == 0:
                    s.stage("vector", [
                        lambda e: e.tensor_scalar_add(TP[:, 0:256], HP[:, 0:256], 1536.0),
                    ], waits=[mul])
                    s.stage("vector", _adds([1]) + [
                        lambda e: e.tensor_scalar_add(TP[:, 0:256], TP[:, 0:256], -1536.0),
                    ])
                    s.stage("vector", _adds([2, 3]))
                    hq_prev = s.stage("vector", _adds([4, 5, 6, 7]))
                else:
                    hq_prev = s.stage("vector", _adds(range(k0, k0 + 8)), waits=[mul])
            hq2 = s.stage("vector", [
                lambda e: e.tensor_scalar_add(HPf[:, 256:4096], HPf[:, 256:4096], -1536.0),
                lambda e: e.tensor_scalar_add(HPf[:, 4096:8192], HPf[:, 4096:8192], -1536.0),
            ], waits=[hq_prev])
            last_ev1 = None
            for tc in range(TC2):
                g = t2 * TC2 + tc
                x2l2 = s.dmas("sync", [
                    (lambda e, g=g: e.dma_start(
                        out=TPf[:, 2048:3072],
                        in_=x2s[g * 128:(g + 1) * 128, 0:1024])),
                ], waits=[hq2] if tc == 0 else [prev_store], stream="dxa")
                oev = []
                for half in range(2):
                    pab = PAB_A if half == 0 else PAB_B
                    wv = [hq2, w2ab, w2cd, f2i_, b2_cast] if tc == 0 else [hq2]
                    if last_ev1 is not None:
                        wv.append(last_ev1)
                    mmo = s.stage("tensor", [
                        (lambda e, kc=kc, tc=tc, half=half, pab=pab: e.matmul(
                            pab[:, 0:512],
                            (TP[:, tc * 128: tc * 128 + 128] if kc == 0 else
                             HPf[:, kc * 256 + tc * 128: kc * 256 + tc * 128 + 128]),
                            WP[:, w2t_region(kc) + half * 512: w2t_region(kc) + (half + 1) * 512],
                            start=(kc == 0), stop=False))
                        for kc in range(HC)
                    ] + [
                        (lambda e, tc=tc, half=half, pab=pab: e.matmul(
                            pab[:, 0:512],
                            r_f2[:, tc * 128:(tc + 1) * 128],
                            r_b2[:, half * 512:(half + 1) * 512],
                            start=False, stop=True, tile_position=(64, 0))),
                    ], waits=wv)
                    ev1w = [mmo]
                    if oev:
                        ev1w.append(oev[-1])
                    elif prev_oev is not None:
                        ev1w.append(prev_oev)
                    ev1 = s.stage("scalar", [
                        (lambda e, tc=tc, half=half, pab=pab: e.activation(
                            TPf[:, 1024 + half * 512: 1024 + (half + 1) * 512],
                            pab[:, 0:512], AF.Copy, scale=m_s2i[:, tc:tc + 1])),
                    ], waits=ev1w)
                    last_ev1 = ev1
                    oev.append(s.stage("vector", [
                        (lambda e, half=half: e.tensor_add(
                            TPf[:, 3072 + half * 512: 3072 + (half + 1) * 512],
                            TPf[:, 1024 + half * 512: 1024 + (half + 1) * 512],
                            TPf[:, 2048 + half * 512: 2048 + (half + 1) * 512])),
                    ], waits=[ev1, x2l2]))
                prev_oev = oev[-1]
                prev_store = s.dmas("gpsimd", [
                    (lambda e, g=g: e.dma_start(
                        out=outd[g * 128:(g + 1) * 128, :],
                        in_=TPf[:, 3072:4096]))
                ], waits=[oev[0], oev[1]])

        s.mark("all")
        import os
        trunc = os.environ.get("MK_TRUNCATE", "")
        if trunc:
            s.truncate(trunc)
        fin = s.barrier()
        s.prog["gpsimd"].append(("fin", fin, [], None))

        with nc.Block() as block:
            def mk(engname):
                def body(eng):
                    hw = {}
                    for kind, waits, fns, inc in s.prog[engname]:
                        for (nm, val) in waits:
                            if hw.get(nm, -1) >= val:
                                continue
                            hw[nm] = val
                            eng.wait_ge(sems[nm], val)
                        if kind == "fin":
                            continue
                        if kind == "ccstage":
                            last = None
                            for fn in fns:
                                last = fn(eng)
                            nm, v = inc
                            last.then_inc(sems[nm], 1)
                        elif kind == "pestage":
                            last = None
                            for fn in fns:
                                last = fn(eng)
                            nm, v = inc
                            last.then_inc(sems[nm], 1)
                        elif kind == "stage":
                            for fn in fns:
                                fn(eng)
                            nm, v = inc
                            eng.drain().then_inc(sems[nm], 1)
                        else:
                            nm, v = inc
                            for fn in fns:
                                fn(eng).then_inc(sems[nm], 16)
                return body
            block.sync(mk("sync"))
            block.vector(mk("vector"))
            block.scalar(mk("scalar"))
            block.tensor(mk("tensor"))
            block.gpsimd(mk("gpsimd"))
    return nc


_NC_CACHE = {}


def kernel(**inputs):
    if "nc" not in _NC_CACHE:
        _NC_CACHE["nc"] = build_program()
    nc = _NC_CACHE["nc"]
    x = np.ascontiguousarray(np.asarray(inputs["L0_attn_x"], dtype=np.float32))
    f32 = lambda k: np.ascontiguousarray(np.asarray(inputs[k], dtype=np.float32))
    shared = {
        "Wq": f32("L0_attn_Wq"), "Wk": f32("L0_attn_Wk"),
        "Wv": f32("L0_attn_Wv"), "Wo": f32("L0_attn_Wo"),
        "bq": f32("L0_attn_bq"), "bk": f32("L0_attn_bk"),
        "bv": f32("L0_attn_bv"), "bo": f32("L0_attn_bo"),
        "proj": f32("L0_attn_proj"),
        "w1": f32("L0_ffn_w1"), "b1": f32("L0_ffn_b1"),
        "w2": f32("L0_ffn_w2"), "b2": f32("L0_ffn_b2"),
        "gamma": f32("L0_ffn_gamma"), "beta": f32("L0_ffn_beta"),
    }
    in_maps = [dict(shared, x=x[c]) for c in range(NCORES)]
    res = run_bass_kernel_spmd(nc, in_maps, core_ids=list(range(NCORES)))
    return np.stack([np.asarray(res.results[c]["out"]) for c in range(NCORES)],
                    axis=0)


# version 09
# speedup vs baseline: 1.0144x; 1.0144x over previous
import contextlib
import numpy as np
import concourse.bass as bass
import concourse.mybir as mybir
from concourse.bass_utils import run_bass_kernel_spmd

F32 = mybir.dt.float32
F16 = mybir.dt.float16
AF = mybir.ActivationFunctionType
ALU = mybir.AluOpType
AX = mybir.AxisListType

NCORES = 8
N = 4096
E = 1024
H = 16
M = 256
HID = 4096
T = 512
NT = N // T
TC = T // 128
EC = E // 128
HC = HID // 128
CSTAB = 20.0
KEPS = 1e-4
LNEPS = 1e-5
CQ = float(64.0 ** -0.25)

ENGINES = ("sync", "vector", "scalar", "tensor", "gpsimd")


class Sched:
    DSTREAMS = ("dxa", "dwt", "dwq", "dsb", "dst", "dgo", "dpr", "dscalar")

    def __init__(self):
        self.prog = {e: [] for e in ENGINES}
        self.cnt = {e: 0 for e in ENGINES}
        self.dma = {k: 0 for k in self.DSTREAMS}
        self.cc = 0

    def _rw(self, waits):
        out = []
        for w in waits:
            if not w:
                continue
            nm, v = w
            if nm in self.dma:
                v = self.dma[nm]
                if v == 0:
                    continue
            out.append((nm, v))
        return out

    def stage(self, eng, fns, waits=()):
        w = self._rw(waits)
        if self.cnt[eng] > 0:
            w.append((eng, self.cnt[eng]))
        self.cnt[eng] += 1
        kind = "pestage"
        self.prog[eng].append((kind, w, fns, (eng, self.cnt[eng])))
        return (eng, self.cnt[eng])

    def dmas(self, eng, fns, waits=(), stream=None):
        nm = stream or ("dscalar" if eng == "scalar" else
                        ("dst" if eng == "gpsimd" else "dwt"))
        w = self._rw(waits)
        self.dma[nm] += 16 * len(fns)
        self.prog[eng].append(("dma", w, fns, (nm, self.dma[nm])))
        return (nm, self.dma[nm])

    def collective(self, fn, waits=()):
        self.cc += 1
        self.prog["gpsimd"].append(("ccstage", self._rw(waits), [fn],
                                    ("cc", self.cc)))
        return ("cc", self.cc)

    def mark(self, name):
        self.marks = getattr(self, "marks", {})
        self.marks[name] = {e: len(self.prog[e]) for e in ENGINES}

    def truncate(self, name):
        for e in ENGINES:
            self.prog[e] = self.prog[e][: self.marks[name][e]]
        self.cnt = {e: 0 for e in ENGINES}
        self.dma = {k: 0 for k in self.DSTREAMS}
        self.cc = 0
        for e in ENGINES:
            for kind, waits, fns, inc in self.prog[e]:
                if inc is None:
                    continue
                nm, v = inc
                if nm in self.dma:
                    self.dma[nm] = max(self.dma[nm], v)
                elif nm == "cc":
                    self.cc = max(self.cc, v)
                else:
                    self.cnt[nm] = max(self.cnt[nm], v)

    def barrier(self):
        w = [(e, self.cnt[e]) for e in ENGINES if self.cnt[e] > 0]
        w += [(k, v) for k, v in self.dma.items() if v > 0]
        if self.cc:
            w.append(("cc", self.cc))
        return w


def build_program():
    nc = bass.Bass("TRN2", target_bir_lowering=False, debug=False)

    xin = nc.dram_tensor("x", [N, E], F32, kind="ExternalInput").ap()
    Wq = nc.dram_tensor("Wq", [E, E], F32, kind="ExternalInput").ap()
    Wk = nc.dram_tensor("Wk", [E, E], F32, kind="ExternalInput").ap()
    Wv = nc.dram_tensor("Wv", [E, E], F32, kind="ExternalInput").ap()
    Wo = nc.dram_tensor("Wo", [E, E], F32, kind="ExternalInput").ap()
    bqd = nc.dram_tensor("bq", [E], F32, kind="ExternalInput").ap()
    bkd = nc.dram_tensor("bk", [E], F32, kind="ExternalInput").ap()
    bvd = nc.dram_tensor("bv", [E], F32, kind="ExternalInput").ap()
    bod = nc.dram_tensor("bo", [E], F32, kind="ExternalInput").ap()
    projd = nc.dram_tensor("proj", [M, 64], F32, kind="ExternalInput").ap()
    w1d = nc.dram_tensor("w1", [HID, E], F32, kind="ExternalInput").ap()
    b1d = nc.dram_tensor("b1", [HID], F32, kind="ExternalInput").ap()
    w2d = nc.dram_tensor("w2", [E, HID], F32, kind="ExternalInput").ap()
    b2d = nc.dram_tensor("b2", [E], F32, kind="ExternalInput").ap()
    gammad = nc.dram_tensor("gamma", [E], F32, kind="ExternalInput").ap()
    betad = nc.dram_tensor("beta", [E], F32, kind="ExternalInput").ap()
    outd = nc.dram_tensor("out", [N, E], F32, kind="ExternalOutput").ap()

    x2s = nc.dram_tensor("x2s", [N, E], F32)
    wsw = nc.dram_tensor("wsw", [4 * E, E], F16)
    wsp = nc.dram_tensor("wsp", [M, 128], F16)
    ws1 = nc.dram_tensor("ws1", [HID, E], F16)
    ws2 = nc.dram_tensor("ws2", [E, HID], F16)
    rmio = nc.dram_tensor("rmio", [128, 16], F32)
    rmo = nc.dram_tensor("rmo", [128, 16], F32)
    srow = nc.dram_tensor("srow", [128, 1], F32)

    identd = nc.inline_tensor(np.eye(128, dtype=np.float16), "identc")
    ident32d = nc.inline_tensor(np.eye(128, dtype=np.float32), "ident32c")
    ind_np = np.zeros((128, 2), dtype=np.float16)
    ind_np[0:64, 0] = 1.0
    ind_np[64:128, 1] = 1.0
    indd = nc.inline_tensor(ind_np, "indc")

    s = Sched()
    es = contextlib.ExitStack()
    with es:
        sbuf = lambda n_, sh, dt_: es.enter_context(nc.sbuf_tensor(n_, sh, dt_))
        WP = sbuf("WP", [128, 65536], F16)
        HP = sbuf("HP", [128, 12320], F32)
        TP = sbuf("TP", [128, 9472], F16)
        MS = sbuf("MS", [128, 872], F32)
        CXP = sbuf("CXP", [128, 2080], F16)
        BR = sbuf("BR", [128, 2193], F16)
        IDN = sbuf("IDN", [128, 128], F16)
        IDN32 = sbuf("IDN32", [128, 128], F32)
        IND = sbuf("IND", [128, 2], F16)
        PJ = sbuf("PJ", [128, 256], F16)
        KC = sbuf("KC", [128, 8], F32)

        PAB_A = es.enter_context(nc.psum_tensor("PAB_A", [128, 512], F32))
        PAB_B = es.enter_context(nc.psum_tensor("PAB_B", [128, 512], F32))
        DDP = es.enter_context(nc.psum_tensor("DDP", [128, 2048], F32))
        CTP = es.enter_context(nc.psum_tensor("CTP", [128, 1024], F32))
        PA16 = PAB_A.bitcast(F16)
        PB16 = PAB_B.bitcast(F16)

        sems = {}
        for nm in ("sync", "vector", "scalar", "tensor", "gpsimd",
                   "dxa", "dwt", "dwq", "dsb", "dst", "dgo", "dpr", "dscalar", "cc"):
            sems[nm] = es.enter_context(nc.semaphore(nm))

        TPf = TP.bitcast(F32)
        HPf = HP.bitcast(F16)

        RA, RB, RXL, RC, RD = 0, 8192, 16384, 49152, 57344
        KN = 4352

        def w2t_region(kc):
            return [RA, RB, RC, RD][kc // 8] + (kc % 8) * 1024

        m_bn = MS[:, 0:12]
        m_mv = MS[:, 16:24]
        m_sd = MS[:, 24:28]
        m_rstd = MS[:, 28:32]
        m_negm = MS[:, 32:36]
        m_cbk = MS[:, 36:44]
        m_cbq = MS[:, 44:52]
        m_b1h = MS[:, 52:84]
        m_RM = MS[:, 84:100]
        m_diag = MS[:, 100:164]
        m_asum = MS[:, 800:864]
        m_nst = MS[:, 164:228]
        m_edb = MS[:, 228:292]
        m_stab = MS[:, 292:293]
        m_cstab = MS[:, 293:294]
        m_red2 = MS[:, 296:298]
        m_dinv = MS[:, 300:304]
        m_wred = MS[:, 304:306]
        m_sca = MS[0:1, 306:316]
        m_rx1 = MS[:, 320:321]
        m_mx1 = MS[:, 321:322]
        m_s1c = MS[:, 322:323]
        m_s1i = MS[:, 324:328]
        m_rx2 = MS[:, 328:332]
        m_mx2 = MS[:, 332:336]
        m_s2c = MS[:, 336:340]
        m_s2i = MS[:, 340:344]
        m_f2 = MS[:, 344:348]
        m_sw1b = MS[:, 348:349]
        m_sw2b = MS[:, 349:350]
        m_mw1db = MS[:, 350:351]
        m_mw2db = MS[:, 351:352]
        o32row = MS[0:1, 416:544]
        m_bn2 = MS[:, 352:376]
        m_mv2 = MS[:, 376:380]
        m_rstd2 = MS[:, 380:382]
        m_sd2 = MS[:, 390:392]
        m_rx1b = MS[:, 382:384]
        m_mx1b = MS[:, 384:386]
        m_s1cb = MS[:, 386:388]
        m_mw1d = MS[:, 392:393]
        m_mw2d = MS[:, 393:394]
        m_L1col = MS[:, 394:395]
        m_L1 = MS[0:1, 395:396]
        m_cLt = MS[:, 396:397]
        m_mLt = MS[:, 397:398]
        m_eD = MS[:, 398:399]
        m_Lprev = MS[:, 399:400]
        m_mS = MS[:, 400:401]
        m_f = MS[:, 401:402]
        m_sx = MS[:, 402:410]
        m_dinv8 = MS[:, 864:872]
        m_srow = MS[0:1, 544:672]
        m_s1row = MS[0:1, 544:800]

        r_bv = BR[0:1, 0:1024]
        ones16row = BR[0:1, 1024:1152]
        r_bo = BR[32:33, 0:1024]
        ones16row32 = BR[32:33, 1024:1152]
        r_f2 = BR[64:65, 0:512]
        r_b2 = BR[64:65, 512:1536]
        r_eps = BR[32:33, 1152:2192]
        ones16col = BR[:, 2192:2193]
        k4096h = BR[0:1, 1536:1537]

        k1536 = KC[:, 0:1]
        kmC = KC[:, 1:2]
        kpC = KC[:, 2:3]
        kCQ = KC[:, 3:4]
        kmhalf = KC[:, 4:5]
        kones32 = KC[:, 5:6]
        kLNEPS = KC[:, 6:7]

        def dma1(eng, outap, inap, waits=(), stream=None, **kw):
            return s.dmas(eng, [lambda e, o=outap, i=inap, kw=kw:
                                e.dma_start(out=o, in_=i, **kw)], waits=waits,
                          stream=stream)

        s.dmas("sync", [
            lambda e: e.dma_start(out=IDN[:], in_=identd.ap()),
            lambda e: e.dma_start(out=IDN32[:], in_=ident32d.ap()),
            lambda e: e.dma_start(out=IND[:], in_=indd.ap()),
        ])
        st_ones = s.stage("vector", [
            lambda e: e.memset(KC[:, 0:1], 1536.0),
            lambda e: e.memset(KC[:, 1:2], -CSTAB),
            lambda e: e.memset(KC[:, 2:3], CSTAB),
            lambda e: e.memset(KC[:, 3:4], CQ),
            lambda e: e.memset(KC[:, 4:5], -0.5),
            lambda e: e.memset(KC[:, 5:6], 1.0),
            lambda e: e.memset(KC[:, 6:7], LNEPS),
            lambda e: e.memset(o32row, 1.0),
            lambda e: e.memset(ones16row, 1.0),
            lambda e: e.memset(ones16row32, 1.0),
            lambda e: e.memset(ones16col, 1.0),
            lambda e: e.memset(m_RM, -3.0e38),
            lambda e: e.memset(KC[:, 7:8], 4096.0),
            lambda e: e.memset(k4096h, 4096.0),
            lambda e: e.memset(m_Lprev, -1.0e30),
        ])

        row_ld = s.dmas("gpsimd", [
            lambda e: e.dma_start(out=r_bv, in_=bvd[None, :]),
            lambda e: e.dma_start(out=r_bo, in_=bod[None, :]),
        ], stream="dpr")
        wlist = [(Wk, RA), (Wv, RB), (Wq, RC), (Wo, RD)]
        cast_done = []
        for wi, (wap, dst) in enumerate(wlist):
            cast_done.append(dma1("gpsimd", wsw[wi * E:(wi + 1) * E, :], wap, stream="dpr"))
        proj_cast = s.dmas("gpsimd", [
            lambda e: e.dma_start(out=wsp[:, 0:64], in_=projd),
            lambda e: e.dma_start(out=wsp[:, 64:128], in_=projd),
        ], stream="dpr")
        b2_cast = dma1("gpsimd", r_b2, b2d[None, :], stream="dpr")
        wt_loads = []
        for wi, (wap, dst) in enumerate(wlist):
            fns = []
            for kc in range(EC):
                fns.append(lambda e, wi=wi, kc=kc, dst=dst: e.dma_start(
                    out=WP[:, dst + kc * 1024: dst + (kc + 1) * 1024],
                    in_=wsw[wi * E:(wi + 1) * E, kc * 128:(kc + 1) * 128],
                    transpose=True))
            wt_loads.append(s.dmas("sync", fns, waits=[cast_done[wi]]))
        pj_load = s.dmas("sync", [
            lambda e: e.dma_start(out=PJ[:, :], in_=wsp[:, :], transpose=True),
        ], waits=[proj_cast])

        bias_ld = s.dmas("sync", [
            lambda e: e.dma_start(out=m_cbk, in_=bkd.rearrange("(c p) -> p c", p=128),
                                  allow_slow_non_contiguous=True),
            lambda e: e.dma_start(out=m_cbq, in_=bqd.rearrange("(c p) -> p c", p=128),
                                  allow_slow_non_contiguous=True),
            lambda e: e.dma_start(out=m_b1h, in_=b1d.rearrange("(c p) -> p c", p=128),
                                  allow_slow_non_contiguous=True),
        ])
        sc_bias = s.stage("vector", [
            lambda e: e.tensor_scalar_mul(m_cbk, m_cbk, CQ),
            lambda e: e.tensor_scalar_mul(m_cbq, m_cbq, CQ),
        ], waits=[bias_ld])

        def emit_ln(src_of_tc, dst_of_tc, wait_load, dst_w=()):
            for tc in range(TC):
                s.stage("vector", [
                    (lambda e, tc=tc: e.bn_stats(m_bn[:, 0:6], src_of_tc(tc)[:, 0:512])),
                    (lambda e, tc=tc: e.bn_stats(m_bn[:, 6:12], src_of_tc(tc)[:, 512:1024])),
                ], waits=[wait_load] if tc == 0 else [])
                s.stage("vector", [
                    (lambda e, tc=tc: e.bn_aggr(m_mv[:, 2 * tc:2 * tc + 2], m_bn[:, 0:12])),
                ])
            veps = s.stage("vector", [
                lambda e: e.tensor_scalar_add(m_sd, m_mv[:, 1:8:2], LNEPS)])
            ssq = s.stage("scalar", [
                lambda e: e.activation(m_sd, m_sd, AF.Sqrt)], waits=[veps])
            s.stage("vector", [
                lambda e: e.reciprocal(m_rstd, m_sd)], waits=[ssq])
            vnm = s.stage("vector", [
                lambda e: e.scalar_tensor_tensor(m_negm, m_mv[:, 0:8:2], -1.0,
                                                 m_rstd, ALU.mult, ALU.mult)])
            last = None
            for tc in range(TC):
                last = s.stage("scalar", [
                    (lambda e, tc=tc: e.activation(
                        dst_of_tc(tc), src_of_tc(tc), AF.Identity,
                        bias=m_negm[:, tc:tc + 1], scale=m_rstd[:, tc:tc + 1])),
                ], waits=[vnm] + list(dst_w) if tc == 0 else [])
            return last

        def emit_tblock(src_of_k, dst_of_k, nk, wait_src, ps16, extra_w=()):
            last_ev = None
            for g0 in range(0, nk, 8):
                gn = min(8, nk - g0)
                w = [wait_src] + list(extra_w)
                if last_ev is not None:
                    w.append(last_ev)
                tp_ = s.stage("tensor", [
                    (lambda e, k=g0 + i, sl=i: e.transpose(
                        ps16[:, sl * 128:(sl + 1) * 128], src_of_k(k), IDN[:]))
                    for i in range(gn)
                ], waits=w)
                last_ev = s.stage("vector", [
                    (lambda e, k=g0 + i, sl=i: e.tensor_copy(
                        dst_of_k(k), ps16[:, sl * 128:(sl + 1) * 128]))
                    for i in range(gn)
                ], waits=[tp_])
            return last_ev

        def emit_proj(wreg, cb_col, t, dst_of_ec, waits):
            evs = []
            for ec in range(EC):
                pab = PAB_A if ec % 2 == 0 else PAB_B
                w = list(waits) if ec <= 1 else []
                if len(evs) >= 2:
                    w.append(evs[-2])
                mm = s.stage("tensor", [
                    (lambda e, kc=kc, ec=ec, pab=pab: e.matmul(
                        pab[:, 0:512],
                        WP[:, wreg + kc * 1024 + ec * 128: wreg + kc * 1024 + ec * 128 + 128],
                        WP[:, RXL + kc * 4096 + t * 512: RXL + kc * 4096 + (t + 1) * 512],
                        start=(kc == 0), stop=(kc == EC - 1)))
                    for kc in range(EC)
                ], waits=w)
                evs.append(s.stage("scalar", [
                    (lambda e, ec=ec, pab=pab: e.activation(
                        dst_of_ec(ec), pab[:, 0:512], AF.Identity,
                        bias=cb_col[:, ec:ec + 1], scale=kCQ)),
                ], waits=[mm]))
            return evs[-1]

        def emit_diag(src_off, wait_kn, extra_first=()):
            dmm = []
            for ec in range(EC):
                p = ec % 2
                wsq = [wait_kn] if ec == 0 else []
                if len(dmm) >= 2:
                    wsq.append(dmm[ec - 2])
                sq_ = s.stage("scalar", [
                    (lambda e, ec=ec, p=p: e.activation(
                        TP[:, 8448 + p * 512: 8448 + (p + 1) * 512],
                        TP[:, src_off + ec * 512: src_off + (ec + 1) * 512],
                        AF.Square)),
                ], waits=wsq)
                wm = [sq_]
                if ec == 0:
                    wm += list(extra_first)
                dmm.append(s.stage("tensor", [
                    (lambda e, ec=ec, tc=tc, p=p: e.matmul(
                        DDP[:, tc * 16 + 2 * ec: tc * 16 + 2 * ec + 2],
                        TP[:, 8448 + p * 512 + tc * 128: 8448 + p * 512 + tc * 128 + 128],
                        IND[:], start=(ec == 0 and tc == 0),
                        stop=(ec == EC - 1 and tc == TC - 1)))
                    for tc in range(TC)
                ], waits=wm))
            return s.stage("vector", [
                lambda e: e.tensor_copy(m_diag, DDP[:, 0:64])], waits=[dmm[-1]])

        wq_items = ([("w1", ch, None) for ch in range(HC)] +
                    [("w2", ch, ss) for ch in range(EC) for ss in range(4)])
        wq1 = {"i": 0, "last": None}
        wq2 = {"i": 0, "last": None, "mean": None}
        wq_stg = HP[:, 4096:5120]
        wq_ta = HP[:, 5120:6144]
        wq_tb = HP[:, 6144:7168]
        wq_o16 = HPf[:, 14336:15360]

        def wq_src(it):
            wname, ch, ss = it
            if wname == "w1":
                return w1d[ch * 128:(ch + 1) * 128, :], m_asum[:, ch:ch + 1], m_sw1b
            return (w2d[ch * 128:(ch + 1) * 128, ss * 1024:(ss + 1) * 1024],
                    m_asum[:, 32 + ch * 4 + ss: 33 + ch * 4 + ss], m_sw2b)

        def wq_dst(it):
            wname, ch, ss = it
            if wname == "w1":
                return ws1[ch * 128:(ch + 1) * 128, :]
            return ws2[ch * 128:(ch + 1) * 128, ss * 1024:(ss + 1) * 1024]

        def emit_wq_pass1(n):
            for _ in range(n):
                i = wq1["i"]
                if i >= len(wq_items):
                    return
                wq1["i"] += 1
                src_, acol, _ = wq_src(wq_items[i])
                w = [wq1["last"]] if wq1["last"] else []
                ld = dma1("sync", wq_stg, src_, waits=w, stream="dwq")
                wq1["last"] = s.stage("scalar", [
                    (lambda e, acol=acol: e.activation(
                        wq_stg, wq_stg, AF.Abs, accum_out=acol)),
                ], waits=[ld])

        def emit_wq_mean():
            red = s.stage("vector", [
                lambda e: e.reduce_sum(m_wred[:, 0:1], m_asum[:, 0:32], AX.X),
                lambda e: e.reduce_sum(m_wred[:, 1:2], m_asum[:, 32:64], AX.X),
            ], waits=[wq1["last"]])
            pm = s.stage("tensor", [
                lambda e: e.matmul(CTP[0:1, 0:1], kones32, m_wred[:, 0:1],
                                   start=True, stop=False),
                lambda e: e.matmul(CTP[0:1, 1:2], kones32, m_wred[:, 1:2],
                                   start=False, stop=True),
            ], waits=[red])
            s.stage("vector", [
                lambda e: e.tensor_scalar(m_sca[:, 0:2], CTP[0:1, 0:2],
                                          1.0 / (HID * E), 1e-5, ALU.mult, ALU.max),
            ], waits=[pm])
            sc = s.stage("vector", [
                lambda e: e.reciprocal(m_sca[:, 2:4], m_sca[:, 0:2]),
                lambda e: e.tensor_scalar_mul(m_sca[:, 4:6], m_sca[:, 0:2], 1.0 / 127.0),
            ])
            bc = s.stage("tensor", [
                lambda e: e.matmul(CTP[:, 0:1], o32row, m_sca[:, 2:3],
                                   start=True, stop=False),
                lambda e: e.matmul(CTP[:, 1:2], o32row, m_sca[:, 3:4],
                                   start=False, stop=False),
                lambda e: e.matmul(CTP[:, 2:3], o32row, m_sca[:, 4:5],
                                   start=False, stop=False),
                lambda e: e.matmul(CTP[:, 3:4], o32row, m_sca[:, 5:6],
                                   start=False, stop=False),
                lambda e: e.matmul(CTP[:, 4:5], o32row, m_sca[:, 0:1],
                                   start=False, stop=False),
                lambda e: e.matmul(CTP[:, 5:6], o32row, m_sca[:, 1:2],
                                   start=False, stop=True),
            ], waits=[sc])
            wq2["mean"] = s.stage("vector", [
                lambda e: e.tensor_copy(m_sw1b, CTP[:, 0:1]),
                lambda e: e.tensor_copy(m_sw2b, CTP[:, 1:2]),
                lambda e: e.tensor_copy(m_mw1db, CTP[:, 2:3]),
                lambda e: e.tensor_copy(m_mw2db, CTP[:, 3:4]),
                lambda e: e.tensor_copy(m_mw1d, CTP[:, 4:5]),
                lambda e: e.tensor_copy(m_mw2d, CTP[:, 5:6]),
            ], waits=[bc])

        def emit_wq_pass2(n):
            for _ in range(n):
                i = wq2["i"]
                if i >= len(wq_items):
                    return
                wq2["i"] += 1
                it = wq_items[i]
                src_, _, swb = wq_src(it)
                w = [wq2["mean"]]
                if wq2["last"]:
                    w.append(wq2["last"])
                ld = dma1("sync", wq_stg, src_, waits=w, stream="dwq")
                g1 = s.stage("vector", [
                    (lambda e, swb=swb: e.tensor_scalar(
                        wq_ta, wq_stg, swb[:, 0:1], 0.5, ALU.mult, ALU.is_gt)),
                    (lambda e, swb=swb: e.tensor_scalar(
                        wq_tb, wq_stg, swb[:, 0:1], -0.5, ALU.mult, ALU.is_lt)),
                ], waits=[ld] + ([wq2["last"]] if wq2["last"] else []))
                g2 = s.stage("vector", [
                    lambda e: e.tensor_sub(wq_o16, wq_ta, wq_tb),
                ])
                wq2["last"] = dma1("gpsimd", wq_dst(it), wq_o16, waits=[g2],
                                   stream="dgo")

        s.mark("prologue")
        prol_w = s.barrier()
        ctx_init = s.stage("vector", [
            lambda e: e.memset(HP[:, 8192:10272], 0.0),
        ], waits=prol_w)
        ctx_last = None
        last_xuse = None
        for t in range(NT):
            w = list(prol_w) if t == 0 else []
            if last_xuse is not None:
                w.append(last_xuse)
            xld = s.dmas("sync", [
                (lambda e, t=t: e.dma_start(
                    out=HP[:, 0:4096].rearrange("p (a m) -> p a m", a=TC),
                    in_=xin[t * 512:(t + 1) * 512, :].rearrange("(a p) m -> p a m", p=128)))
            ], waits=w, stream="dxa")
            lnd = emit_ln(lambda tc: HP[:, tc * 1024:(tc + 1) * 1024],
                          lambda tc: TP[:, tc * 1024:(tc + 1) * 1024], xld,
                          dst_w=[ctx_last] if ctx_last else [])
            last_xuse = lnd
            trd = emit_tblock(
                lambda k: TP[:, (k // EC) * 1024 + (k % EC) * 128:
                             (k // EC) * 1024 + (k % EC) * 128 + 128],
                lambda k, t=t: WP[:, RXL + (k % EC) * 4096 + t * 512 + (k // EC) * 128:
                                  RXL + (k % EC) * 4096 + t * 512 + (k // EC) * 128 + 128],
                TC * EC, lnd, PA16)
            if t == 0:
                s.mark("p1tr")
            kev = emit_proj(RA, m_cbk, t,
                            lambda ec: TP[:, KN + ec * 512: KN + (ec + 1) * 512],
                            [trd, wt_loads[0], sc_bias, st_ones] if t == 0
                            else [trd])
            if t == 0:
                s.mark("p1kp")
            vev = []
            for tc in range(TC):
                for half in range(2):
                    pab = PAB_A if half == 0 else PAB_B
                    wv = [kev] if tc == 0 else []
                    if t == 0 and tc == 0 and half == 0:
                        wv += [wt_loads[1], row_ld]
                    if len(vev) >= 2:
                        wv.append(vev[-2])
                    mmv = s.stage("tensor", [
                        (lambda e, kc=kc, tc=tc, half=half, pab=pab, t=t: e.matmul(
                            pab[:, 0:512],
                            WP[:, RXL + kc * 4096 + t * 512 + tc * 128:
                               RXL + kc * 4096 + t * 512 + tc * 128 + 128],
                            WP[:, RB + kc * 1024 + half * 512: RB + kc * 1024 + (half + 1) * 512],
                            start=(kc == 0), stop=False))
                        for kc in range(EC)
                    ] + [
                        (lambda e, half=half, pab=pab: e.matmul(
                            pab[:, 0:512], ones16row,
                            BR[0:1, half * 512:(half + 1) * 512],
                            start=False, stop=True)),
                    ], waits=wv)
                    vev.append(s.stage("scalar", [
                        (lambda e, tc=tc, half=half, pab=pab: e.activation(
                            TP[:, tc * 1040 + half * 520: tc * 1040 + half * 520 + 520]
                            .rearrange("p (h d) -> p h d", d=65)[:, :, 0:64],
                            pab[:, 0:512].rearrange("p (h d) -> p h d", d=64),
                            AF.Copy)),
                    ], waits=[mmv]))
            ones_v = s.stage("vector", [
                lambda e: e.memset(
                    TP[:, 0:4160].rearrange("p (a h d) -> p a h d", a=TC, d=65)
                    [:, :, :, 64:65], 1.0),
            ], waits=[vev[-1]])
            dgd = emit_diag(KN, kev,
                            extra_first=[ctx_last] if ctx_last else list(prol_w))
            prev_red = None
            for hp in range(8):
                wm = [kev, dgd] if hp == 0 else [prev_red]
                if t == 0 and hp == 0:
                    wm.append(pj_load)
                mm = s.stage("tensor", [
                    mmf
                    for tc in range(TC)
                    for mmf in [
                        (lambda e, hp=hp, tc=tc: e.matmul(
                            DDP[:, tc * 256: tc * 256 + 256],
                            TP[0:64, KN + hp * 512 + tc * 128: KN + hp * 512 + tc * 128 + 128],
                            PJ[0:64, :], start=(tc % 2 == 0), stop=(tc % 2 == 1),
                            tile_position=(0, 0))),
                        (lambda e, hp=hp, tc=tc: e.matmul(
                            DDP[:, 1024 + tc * 256: 1024 + tc * 256 + 256],
                            TP[64:128, KN + hp * 512 + tc * 128: KN + hp * 512 + tc * 128 + 128],
                            PJ[64:128, :], start=(tc % 2 == 0), stop=(tc % 2 == 1),
                            tile_position=(64, 0))),
                    ]
                ], waits=wm)
                if t == 0 and hp == 0:
                    s.mark("p1d1")
                prev_red = s.stage("vector", [
                    lambda e: e.reduce_max(
                        m_red2,
                        DDP[:, 0:2048].rearrange("p (h m) -> p h m", h=2),
                        AX.X),
                ], waits=[mm])
                rm_done = s.stage("vector", [
                    (lambda e, hp=hp: e.tensor_max(
                        m_RM[:, 2 * hp:2 * hp + 2], m_RM[:, 2 * hp:2 * hp + 2],
                        m_red2)),
                ])
            if t == 0:
                s.mark("p1dd")
            ra = s.stage("vector", [
                lambda e: e.reduce_max(m_L1col, m_RM, AX.X)])
            ra_last = ra
            rowm = s.stage("tensor", [
                lambda e: e.matmul(CTP[0:1, 0:128], m_L1col, IDN32[:],
                                   start=True, stop=True)], waits=[ra])
            rL = s.stage("vector", [
                lambda e: e.reduce_max(m_L1, CTP[0:1, 0:128], AX.X)], waits=[rowm])
            bcm = s.stage("tensor", [
                lambda e: e.matmul(CTP[:, 0:1], o32row, m_L1,
                                   start=True, stop=True)], waits=[rL])
            vL = s.stage("vector", [
                lambda e: e.tensor_scalar(m_cLt, CTP[:, 0:1], -1.0, CSTAB,
                                          ALU.mult, ALU.add),
                lambda e: e.tensor_scalar_mul(m_mLt, CTP[:, 0:1], -1.0),
            ], waits=[bcm])
            eD = s.stage("scalar", [
                lambda e: e.activation(m_eD, m_Lprev, AF.Exp, bias=m_mLt[:, 0:1]),
            ], waits=[vL])
            s.stage("vector", [
                lambda e: e.tensor_scalar_mul(m_Lprev, m_mLt, -1.0),
            ], waits=[eD])
            edb = s.stage("scalar", [
                lambda e: e.activation(m_edb, m_diag, AF.Exp,
                                       bias=m_cLt[:, 0:1], scale=kmhalf),
            ], waits=[dgd])
            prev_ctx_ev = None
            prev_readers = [prev_red]
            last_kf = {0: None, 1: None}
            last_cmmz = {0: None, 1: None}
            cmm = None
            for hp in range(8):
                wm = [kev, edb] + prev_readers
                hmm = s.stage("tensor", [
                    mmf
                    for tc in range(TC)
                    for mmf in [
                        (lambda e, hp=hp, tc=tc: e.matmul(
                            DDP[:, tc * 256: tc * 256 + 256],
                            TP[0:64, KN + hp * 512 + tc * 128: KN + hp * 512 + tc * 128 + 128],
                            PJ[0:64, :], start=(tc % 2 == 0), stop=(tc % 2 == 1),
                            tile_position=(0, 0))),
                        (lambda e, hp=hp, tc=tc: e.matmul(
                            DDP[:, 1024 + tc * 256: 1024 + tc * 256 + 256],
                            TP[64:128, KN + hp * 512 + tc * 128: KN + hp * 512 + tc * 128 + 128],
                            PJ[64:128, :], start=(tc % 2 == 0), stop=(tc % 2 == 1),
                            tile_position=(64, 0))),
                    ]
                ], waits=wm)
                prev_readers = []
                exw = [hmm]
                for sub_ in range(2):
                    if last_kf.get(sub_) is not None:
                        exw.append(last_kf[sub_])
                exb = s.stage("scalar", [
                    (lambda e, sub=sub: e.activation(
                        HP[:, 10272 + sub * 1024: 10272 + (sub + 1) * 1024],
                        DDP[:, sub * 1024:(sub + 1) * 1024], AF.Exp, bias=kmC))
                    for sub in range(2)
                ], waits=exw)
                prev_readers.append(exb)
                for sub in range(2):
                    h = 2 * hp + sub
                    kz = TP[:, 8448:9472] if sub == 0 else HPf[:, 15360:16384]
                    kfw = [exb, ones_v]
                    if last_cmmz.get(sub) is not None:
                        kfw.append(last_cmmz[sub])
                    kf = s.stage("vector", [
                        (lambda e, sub=sub, h=h, tc=tc, kz=kz: e.tensor_single_scalar(
                            kz[:, tc * 256:(tc + 1) * 256],
                            HP[:, 10272 + sub * 1024 + tc * 256:
                               10272 + sub * 1024 + (tc + 1) * 256],
                            m_edb[:, tc * 16 + h: tc * 16 + h + 1],
                            ALU.mult))
                        for tc in range(TC)
                    ], waits=kfw)
                    last_kf[sub] = kf
                    prev_readers.append(kf)
                    zb = h % 2
                    wcm = [kf]
                    if prev_ctx_ev is not None and zb == 0:
                        wcm.append(prev_ctx_ev)
                    cmm = s.stage("tensor", [
                        (lambda e, ms=ms, tc=tc, h=h, zb=zb, kz=kz: e.matmul(
                            CTP[:, zb * 512 + ms * 65: zb * 512 + ms * 65 + 65],
                            kz[:, tc * 256 + ms * 128: tc * 256 + ms * 128 + 128],
                            TP[:, tc * 1040 + h * 65: tc * 1040 + h * 65 + 65],
                            start=(ms == 0 and tc == 0), stop=(ms == 1 and tc == TC - 1)))
                        for ms in range(2) for tc in range(TC)
                    ], waits=wcm)
                    last_cmmz[sub] = cmm
                prev_ctx_ev = s.stage("vector", [
                    (lambda e, hp=hp: e.scalar_tensor_tensor(
                        HP[:, 8192 + hp * 260: 8192 + (hp + 1) * 260]
                        .rearrange("p (z c) -> p z c", c=130),
                        HP[:, 8192 + hp * 260: 8192 + (hp + 1) * 260]
                        .rearrange("p (z c) -> p z c", c=130),
                        m_eD[:, 0:1],
                        CTP[:, 0:1024].rearrange("p (z c) -> p z c", z=2)[:, :, 0:130],
                        ALU.mult, ALU.add)),
                ], waits=[cmm, ctx_init])
                prev_readers.append(prev_ctx_ev)
            ctx_last = prev_ctx_ev
            if t < 3:
                emit_wq_pass1(22)
            elif t == 3:
                emit_wq_pass1(len(wq_items))
                emit_wq_mean()
                emit_wq_pass2(13)
            else:
                emit_wq_pass2(13)
        emit_wq_pass2(len(wq_items))
        wq_final = wq2["last"]
        s.mark("p1")

        ar_o = dma1("gpsimd", rmio.ap(), m_RM, waits=[rm_done], stream="dgo")
        import os as _os
        if _os.environ.get("MK_NOCC"):
            cc1 = dma1("gpsimd", rmo.ap(), rmio.ap(), waits=[ar_o], stream="dgo")
        else:
            cc1 = s.collective(
                lambda e: e.collective_compute(
                    "AllReduce", ALU.max, replica_groups=[list(range(NCORES))],
                    ins=[rmio.ap().opt()], outs=[rmo.ap().opt()]),
                waits=[ar_o])
        ar_i = dma1("gpsimd", m_RM, rmo.ap(), waits=[cc1, ra_last], stream="dgo")
        rr = s.stage("vector", [
            lambda e: e.reduce_max(m_red2[:, 0:1], m_RM, AX.X)], waits=[ar_i])
        sb_o = dma1("gpsimd", srow.ap(), m_red2[:, 0:1], waits=[rr], stream="dgo")
        sb_i = dma1("sync", m_srow, srow.ap().rearrange("a b -> b a"),
                    waits=[sb_o], stream="dsb", allow_slow_non_contiguous=True)
        sr = s.stage("vector", [
            lambda e: e.reduce_max(m_sca[:, 6:7], m_srow, AX.X)], waits=[sb_i])
        sbm = s.stage("tensor", [
            lambda e: e.matmul(CTP[:, 0:1], o32row, m_sca[:, 6:7],
                               start=True, stop=True)], waits=[sr, ctx_last])
        sev = s.stage("vector", [
            lambda e: e.tensor_copy(m_stab, CTP[:, 0:1]),
            lambda e: e.tensor_scalar_mul(m_mS, CTP[:, 0:1], -1.0),
        ], waits=[sbm])
        fst = s.stage("scalar", [
            lambda e: e.activation(m_f, m_Lprev, AF.Exp, bias=m_mS[:, 0:1]),
        ], waits=[sev])
        s.mark("ar")

        sxr = s.stage("vector", [
            (lambda e, kc=kc: e.reduce_sum(
                m_sx[:, kc:kc + 1],
                WP[:, RXL + kc * 4096: RXL + (kc + 1) * 4096], AX.X))
            for kc in range(EC)
        ])
        sxc = s.stage("vector", [
            lambda e: e.tensor_copy(TP[:, 8448:8456], m_sx),
        ])
        vsm = s.stage("tensor", [
            (lambda e, kc=kc, half=half: e.matmul(
                CTP[0:1, half * 512:(half + 1) * 512],
                TP[:, 8448 + kc: 8449 + kc],
                WP[:, RB + kc * 1024 + half * 512: RB + kc * 1024 + (half + 1) * 512],
                start=(kc == 0), stop=False))
            for half in range(2) for kc in range(EC)
        ] + [
            (lambda e, half=half: e.matmul(
                CTP[0:1, half * 512:(half + 1) * 512],
                k4096h, r_bv[:, half * 512:(half + 1) * 512],
                start=False, stop=True))
            for half in range(2)
        ], waits=[sxc, ctx_last])
        vse = s.stage("vector", [
            lambda e: e.tensor_copy(HP[0:1, 0:1024], CTP[0:1, 0:1024]),
        ], waits=[vsm])
        v2a = s.stage("vector", [
            (lambda e, ms=ms: e.tensor_scalar_mul(
                HP[0:1, 1024:3104].rearrange("a (h m c) -> a h m c", m=2, c=65)
                [:, :, ms:ms + 1, 0:64],
                HP[0:1, 0:1024].rearrange("a (h one d) -> a h one d", one=1, d=64),
                KEPS))
            for ms in range(2)
        ] + [
            lambda e: e.memset(
                HP[0:1, 1024:3104].rearrange("a (h m c) -> a h m c", m=2, c=65)
                [:, :, :, 64:65], KEPS * 4096.0),
        ], waits=[vse])
        vsb = s.stage("tensor", [
            (lambda e, c=c: e.matmul(
                DDP[:, c * 512:(c + 1) * 512], o32row,
                HP[0:1, 1024 + c * 512: 1024 + (c + 1) * 512],
                start=True, stop=True))
            for c in range(4)
        ] + [
            lambda e: e.matmul(CTP[:, 0:32], o32row, HP[0:1, 3072:3104],
                               start=True, stop=True),
        ], waits=[v2a])
        vsc = s.stage("vector", [
            (lambda e, c=c: e.tensor_copy(HP[:, c * 512:(c + 1) * 512],
                                          DDP[:, c * 512:(c + 1) * 512]))
            for c in range(4)
        ] + [
            lambda e: e.tensor_copy(HP[:, 2048:2080], CTP[:, 0:32]),
        ], waits=[vsb])
        cxc = s.stage("vector", [
            lambda e: e.scalar_tensor_tensor(
                CXP[:], HP[:, 8192:10272], m_f[:, 0:1], HP[:, 0:2080],
                ALU.mult, ALU.add),
        ], waits=[vsc, fst])
        ebm = s.stage("tensor", [
            (lambda e, h=h, ms=ms: e.matmul(
                DDP[0:1, h * 128: h * 128 + 65], ones16col,
                CXP[:, h * 130 + ms * 65: h * 130 + ms * 65 + 65],
                start=(ms == 0 and h % 4 == 0), stop=(ms == 1 and h % 4 == 3)))
            for h in range(H) for ms in range(2)
        ], waits=[cxc])
        ebv = s.stage("vector", [
            lambda e: e.tensor_scalar_mul(
                r_eps.rearrange("a (h c) -> a h c", c=65),
                DDP[0:1, 0:2048].rearrange("p (h c) -> p h c", h=16)[:, :, 0:65], KEPS),
        ], waits=[ebm])

        w2ab = s.dmas("sync", [
            (lambda e, kc=kc: e.dma_start(
                out=WP[:, w2t_region(kc): w2t_region(kc) + 1024],
                in_=ws2[:, kc * 128:(kc + 1) * 128], transpose=True))
            for kc in range(16)
        ], waits=[wq_final, vsm])

        s.mark("p2")
        x2st = [None] * NT
        p3a_last = None
        for t in range(NT):
            w0 = [ctx_last] if t == 0 else [p3a_last, x2st[t - 1]]
            xl = s.dmas("sync", [
                (lambda e, t=t: e.dma_start(
                    out=HP[:, 0:4096].rearrange("p (a m) -> p a m", a=TC),
                    in_=xin[t * 512:(t + 1) * 512, :].rearrange("(a p) m -> p a m", p=128)))
            ], waits=w0 + ([cxc] if t == 0 else []), stream="dxa")
            qev = emit_proj(RC, m_cbq, t,
                            lambda ec: TP[:, KN + ec * 512: KN + (ec + 1) * 512], w0)
            dgd = emit_diag(KN, qev,
                            extra_first=w0 + ([ebv] if t == 0 else []))
            prev_out_ev = None
            prev_readers = [dgd]
            last_qf = {0: None, 1: None}
            last_eqf = {0: None, 1: None}
            last_ommz = {0: None, 1: None}
            for hp in range(8):
                wm = [qev] + prev_readers
                hmm = s.stage("tensor", [
                    mmf
                    for tc in range(TC)
                    for mmf in [
                        (lambda e, hp=hp, tc=tc: e.matmul(
                            DDP[:, tc * 256: tc * 256 + 256],
                            TP[0:64, KN + hp * 512 + tc * 128: KN + hp * 512 + tc * 128 + 128],
                            PJ[0:64, :], start=(tc % 2 == 0), stop=(tc % 2 == 1),
                            tile_position=(0, 0))),
                        (lambda e, hp=hp, tc=tc: e.matmul(
                            DDP[:, 1024 + tc * 256: 1024 + tc * 256 + 256],
                            TP[64:128, KN + hp * 512 + tc * 128: KN + hp * 512 + tc * 128 + 128],
                            PJ[64:128, :], start=(tc % 2 == 0), stop=(tc % 2 == 1),
                            tile_position=(64, 0))),
                    ]
                ], waits=wm)
                nst = s.stage("vector", [
                    (lambda e, sub=sub, tc=tc, hp=hp: e.reduce_max(
                        m_nst[:, tc * 16 + 2 * hp + sub: tc * 16 + 2 * hp + sub + 1],
                        DDP[:, sub * 1024 + tc * 256: sub * 1024 + (tc + 1) * 256],
                        AX.X, negate=True))
                    for sub in range(2) for tc in range(TC)
                ], waits=[hmm])
                edq = s.stage("vector", [
                    (lambda e, hp=hp: e.scalar_tensor_tensor(
                        m_edb.rearrange("p (a h) -> p a h", a=TC)[:, :, 2 * hp:2 * hp + 2],
                        m_diag.rearrange("p (a h) -> p a h", a=TC)[:, :, 2 * hp:2 * hp + 2],
                        -0.5,
                        m_nst.rearrange("p (a h) -> p a h", a=TC)[:, :, 2 * hp:2 * hp + 2],
                        ALU.mult, ALU.add)),
                ], waits=[nst, dgd])
                edq2 = s.stage("scalar", [
                    (lambda e, hp=hp: e.activation(
                        m_edb.rearrange("p (a h) -> p a h", a=TC)[:, :, 2 * hp:2 * hp + 2],
                        m_edb.rearrange("p (a h) -> p a h", a=TC)[:, :, 2 * hp:2 * hp + 2],
                        AF.Exp, bias=kpC)),
                ], waits=[edq])
                prev_readers = [nst]
                exw = [hmm, edq2]
                for sub in range(2):
                    if last_qf.get(sub) is not None:
                        exw.append(last_qf[sub])
                exb = s.stage("scalar", [
                    (lambda e, sub=sub: e.activation(
                        HP[:, 10272 + sub * 1024: 10272 + (sub + 1) * 1024],
                        DDP[:, sub * 1024:(sub + 1) * 1024], AF.Exp, bias=kmC))
                    for sub in range(2)
                ], waits=exw)
                prev_readers.append(exb)
                ozoneP = [[DDP[:, z * 512: z * 512 + 65] for z in range(4)],
                          [DDP[:, z * 512 + 128: z * 512 + 193] for z in range(4)]]
                ozcolP = [[DDP[:, z * 512 + 64: z * 512 + 65] for z in range(4)],
                          [DDP[:, z * 512 + 192: z * 512 + 193] for z in range(4)]]
                oznumP = [[DDP[:, z * 512: z * 512 + 64] for z in range(4)],
                          [DDP[:, z * 512 + 128: z * 512 + 192] for z in range(4)]]
                omms = []
                for sub in range(2):
                    h = 2 * hp + sub
                    par = sub
                    qz = TP[:, 8448:9472] if par == 0 else HPf[:, 16384:18432]
                    pz = PA16 if par == 0 else PB16
                    ozone = ozoneP[par]
                    qfw = [exb]
                    if last_ommz.get(par) is not None:
                        qfw.append(last_ommz[par])
                    qf = s.stage("vector", [
                        (lambda e, sub=sub, tc=tc, h=h, qz=qz: e.tensor_scalar(
                            qz[:, tc * 256:(tc + 1) * 256],
                            HP[:, 10272 + sub * 1024 + tc * 256:
                               10272 + sub * 1024 + (tc + 1) * 256],
                            m_edb[:, tc * 16 + h: tc * 16 + h + 1], KEPS,
                            ALU.mult, ALU.add))
                        for tc in range(TC)
                    ], waits=qfw)
                    last_qf[sub] = qf
                    tqw = [qf]
                    if last_eqf.get(par) is not None:
                        tqw.append(last_eqf[par])
                    tqf = s.stage("tensor", [
                        (lambda e, k=k, qz=qz, pz=pz: e.transpose(
                            pz[:, k * 128:(k + 1) * 128],
                            qz[:, k * 128:(k + 1) * 128],
                            IDN[:]))
                        for k in range(8)
                    ], waits=tqw)
                    eqf = s.stage("scalar" if par == 0 else "vector", [
                        (lambda e, k=k, qz=qz, pz=pz, par=par: (
                            e.activation(qz[:, k * 128:(k + 1) * 128],
                                         pz[:, k * 128:(k + 1) * 128], AF.Copy)
                            if par == 0 else
                            e.tensor_copy(qz[:, k * 128:(k + 1) * 128],
                                          pz[:, k * 128:(k + 1) * 128])))
                        for k in range(8)
                    ], waits=[tqf])
                    last_eqf[par] = eqf
                    ommw = [eqf, exb, cxc, ebv]
                    if sub == 0 and prev_out_ev is not None:
                        ommw.append(prev_out_ev)
                    omm = s.stage("tensor", [
                        mmf
                        for tc in range(TC)
                        for mmf in [
                            (lambda e, tc=tc, h=h, qz=qz, ozone=ozone, sub=sub: e.matmul(
                                ozone[tc],
                                qz[:, tc * 256: tc * 256 + 128],
                                CXP[:, h * 130: h * 130 + 65],
                                start=(sub == 0), stop=False)),
                            (lambda e, tc=tc, h=h, qz=qz, ozone=ozone: e.matmul(
                                ozone[tc],
                                qz[:, tc * 256 + 128: tc * 256 + 256],
                                CXP[:, h * 130 + 65: h * 130 + 130],
                                start=False, stop=False)),
                            (lambda e, tc=tc, h=h, ozone=ozone, sub=sub: e.matmul(
                                ozone[tc],
                                ones16row32, r_eps[:, h * 65: h * 65 + 65],
                                start=False, stop=(sub == 1))),
                        ]
                    ], waits=ommw)
                    last_ommz[par] = omm
                    omms.append(omm)
                dvs = s.stage("vector", [
                    (lambda e, sub=sub, tc=tc: e.reciprocal(
                        m_dinv8[:, sub * 4 + tc: sub * 4 + tc + 1],
                        ozcolP[sub][tc]))
                    for sub in range(2) for tc in range(TC)
                ], waits=[omms[1]])
                prev_out_ev = s.stage("scalar", [
                    (lambda e, sub=sub, tc=tc, h=2 * hp + sub: e.activation(
                        TP[:, tc * 1024 + h * 64: tc * 1024 + h * 64 + 64],
                        oznumP[sub][tc], AF.Copy,
                        scale=m_dinv8[:, sub * 4 + tc: sub * 4 + tc + 1]))
                    for sub in range(2) for tc in range(TC)
                ], waits=[dvs])
                prev_readers.append(prev_out_ev)
            trd = emit_tblock(
                lambda k: TP[:, (k // EC) * 1024 + (k % EC) * 128:
                             (k // EC) * 1024 + (k % EC) * 128 + 128],
                lambda k: TP[:, KN + (k % EC) * 512 + (k // EC) * 128:
                             KN + (k % EC) * 512 + (k // EC) * 128 + 128],
                TC * EC, prev_out_ev, PB16)
            x2ev = []
            for tc in range(TC):
                for half in range(2):
                    pab = PAB_A if half == 0 else PAB_B
                    wv = [trd, xl] if tc == 0 else []
                    if len(x2ev) >= 2:
                        wv.append(x2ev[-2])
                    mmo = s.stage("tensor", [
                        (lambda e, kc=kc, tc=tc, half=half, pab=pab: e.matmul(
                            pab[:, 0:512],
                            TP[:, KN + kc * 512 + tc * 128: KN + kc * 512 + tc * 128 + 128],
                            WP[:, RD + kc * 1024 + half * 512: RD + kc * 1024 + (half + 1) * 512],
                            start=(kc == 0), stop=False))
                        for kc in range(EC)
                    ] + [
                        (lambda e, half=half, pab=pab: e.matmul(
                            pab[:, 0:512], ones16row32,
                            BR[32:33, half * 512:(half + 1) * 512],
                            start=False, stop=True)),
                    ], waits=wv)
                    x2ev.append(s.stage("vector", [
                        (lambda e, tc=tc, half=half, pab=pab: e.tensor_add(
                            HP[:, 4096 + tc * 1024 + half * 512:
                               4096 + tc * 1024 + (half + 1) * 512],
                            pab[:, 0:512],
                            HP[:, tc * 1024 + half * 512: tc * 1024 + (half + 1) * 512])),
                    ], waits=[mmo]))
            p3a_last = x2ev[-1]
            x2st[t] = s.dmas("gpsimd", [
                (lambda e, t=t, tc=tc: e.dma_start(
                    out=x2s[(t * TC + tc) * 128:(t * TC + tc + 1) * 128, :],
                    in_=HP[:, 4096 + tc * 1024: 4096 + (tc + 1) * 1024]))
                for tc in range(TC)
            ], waits=[x2ev[-1]])

        s.mark("p3a")
        p3ab = s.barrier()
        w1ld = s.dmas("sync", [
            (lambda e, fc=fc: e.dma_start(
                out=WP[:, RXL + fc * 4096: RXL + (fc + 1) * 4096],
                in_=ws1[:, fc * 128:(fc + 1) * 128], transpose=True))
            for fc in range(EC)
        ], waits=p3ab)
        w2cd = s.dmas("scalar", [
            (lambda e, kc=kc: e.dma_start(
                out=WP[:, w2t_region(kc): w2t_region(kc) + 1024],
                in_=ws2[:, kc * 128:(kc + 1) * 128], transpose=True))
            for kc in range(16, 32)
        ], waits=p3ab)
        gb_last = None
        for idx, vecd in enumerate([gammad, betad]):
            for hlf in range(4):
                gl = dma1("sync", m_s1row,
                          vecd[None, hlf * 256:(hlf + 1) * 256],
                          waits=list(p3ab) + ([gb_last] if gb_last else []))
                gm = s.stage("tensor", [
                    lambda e: e.matmul(CTP[:, 0:256], o32row, m_s1row,
                                       start=True, stop=True),
                ], waits=[gl] + ([gb_last] if gb_last else []))
                gb_last = s.stage("vector", [
                    (lambda e, idx=idx, hlf=hlf: e.tensor_copy(
                        CXP[:, idx * 1024 + hlf * 256: idx * 1024 + (hlf + 1) * 256],
                        CTP[:, 0:256])),
                ], waits=[gm])

        s.mark("bnd")
        T2 = 256
        NT2 = N // T2
        TC2 = 2
        p3bw = s.barrier()
        prev_store = None
        prev_oev = None
        for t2 in range(NT2):
            w0 = list(p3bw) if t2 == 0 else [prev_store]
            xq_done = []
            x2la = s.dmas("sync", [
                (lambda e, g=t2 * TC2: e.dma_start(
                    out=TPf[:, 2048:3072],
                    in_=x2s[g * 128:(g + 1) * 128, 0:1024])),
            ], waits=list(w0) + [x2st[t2 // 2]], stream="dxa")
            x2lb = s.dmas("sync", [
                (lambda e, g=t2 * TC2 + 1: e.dma_start(
                    out=TPf[:, 3072:4096],
                    in_=x2s[g * 128:(g + 1) * 128, 0:1024])),
            ], waits=[], stream="dxa")
            s.stage("vector", [
                lambda e: e.bn_stats(m_bn2[:, 0:6], TPf[:, 2048:2560]),
                lambda e: e.bn_stats(m_bn2[:, 6:12], TPf[:, 2560:3072]),
                lambda e: e.bn_stats(m_bn2[:, 12:18], TPf[:, 3072:3584]),
                lambda e: e.bn_stats(m_bn2[:, 18:24], TPf[:, 3584:4096]),
            ], waits=[x2la, x2lb])
            ag = s.stage("vector", [
                lambda e: e.bn_aggr(m_mv2[:, 0:2], m_bn2[:, 0:12]),
                lambda e: e.bn_aggr(m_mv2[:, 2:4], m_bn2[:, 12:24]),
            ])
            rs = s.stage("scalar", [
                lambda e: e.activation(m_sd2[:, 0:1], m_mv2[:, 1:2],
                                       AF.Sqrt, bias=kLNEPS),
                lambda e: e.activation(m_sd2[:, 1:2], m_mv2[:, 3:4],
                                       AF.Sqrt, bias=kLNEPS),
            ], waits=[ag])
            s.stage("vector", [
                (lambda e, tc=tc: e.scalar_tensor_tensor(
                    HP[:, tc * 1024:(tc + 1) * 1024],
                    TPf[:, 2048 + tc * 1024: 2048 + (tc + 1) * 1024],
                    m_mv2[:, 2 * tc: 2 * tc + 1],
                    CXP[:, 0:1024], ALU.subtract, ALU.mult))
                for tc in range(TC2)
            ])
            s.stage("vector", [
                lambda e: e.reciprocal(m_rstd2[:, 0:2], m_sd2[:, 0:2]),
            ], waits=[rs])
            s.stage("vector", [
                (lambda e, tc=tc: e.scalar_tensor_tensor(
                    HP[:, 2048 + tc * 1024: 2048 + (tc + 1) * 1024],
                    HP[:, tc * 1024:(tc + 1) * 1024],
                    m_rstd2[:, tc:tc + 1],
                    CXP[:, 1024:2048], ALU.mult, ALU.add))
                for tc in range(TC2)
            ])
            s.stage("vector", [
                (lambda e, tc=tc: e.reduce_max(
                    m_rx1b[:, tc:tc + 1],
                    HP[:, 2048 + tc * 1024: 2048 + (tc + 1) * 1024],
                    AX.X, apply_absolute_value=True))
                for tc in range(TC2)
            ])
            s.stage("vector", [
                lambda e: e.tensor_scalar(m_mx1b[:, 0:2], m_rx1b[:, 0:2],
                                          1e-5, 1.0 / 127.0, ALU.max, ALU.mult),
            ])
            rcp = s.stage("vector", [
                lambda e: e.reciprocal(m_s1cb[:, 0:2], m_mx1b[:, 0:2]),
                lambda e: e.tensor_single_scalar(m_s1i[:, 0:2], m_mx1b[:, 0:2],
                                                 m_mw1d[:, 0:1], ALU.mult),
            ])
            u1 = s.stage("scalar", [
                (lambda e, tc=tc: e.activation(
                    TP[:, tc * 1024:(tc + 1) * 1024],
                    HP[:, 2048 + tc * 1024: 2048 + (tc + 1) * 1024],
                    AF.Copy, bias=1536.0, scale=m_s1cb[:, tc:tc + 1]))
                for tc in range(TC2)
            ], waits=[rcp])
            for tc in range(TC2):
                tp_ = s.stage("tensor", [
                    (lambda e, fc=i, tc=tc: e.transpose(
                        PA16[:, fc * 128:(fc + 1) * 128],
                        TP[:, tc * 1024 + fc * 128: tc * 1024 + fc * 128 + 128],
                        IDN[:]))
                    for i in range(8)
                ], waits=[u1] + ([xq_done[-1]] if xq_done else []))
                xq_done.append(s.stage("vector", [
                    (lambda e, fc=i, tc=tc: e.tensor_scalar_add(
                        TP[:, 2048 + fc * 256 + tc * 128: 2048 + fc * 256 + tc * 128 + 128],
                        PA16[:, fc * 128:(fc + 1) * 128], -1536.0))
                    for i in range(8)
                ], waits=[tp_]))
            s1o = s.dmas("gpsimd", [
                lambda e: e.dma_start(out=rmio[:, 0:2], in_=m_s1i[:, 0:2]),
            ], waits=[xq_done[-1]], stream="dgo")
            s1i_ = s.dmas("sync", [
                lambda e: e.dma_start(out=m_s1row[:, 0:256].rearrange("a (c p) -> a c p", p=128),
                                      in_=rmio[:, 0:2].rearrange("p c -> c p")[None],
                                      allow_slow_non_contiguous=True),
            ], waits=[s1o], stream="dsb")
            sbm1 = s.stage("tensor", [
                (lambda e, tc=tc: e.matmul(
                    CTP[:, tc * 128:(tc + 1) * 128], o32row,
                    m_s1row[:, tc * 128:(tc + 1) * 128],
                    start=(tc == 0), stop=(tc == TC2 - 1)))
                for tc in range(TC2)
            ], waits=[s1i_])
            sbe1 = s.stage("vector", [
                lambda e: e.tensor_copy(TPf[:, 0:256], CTP[:, 0:256]),
            ], waits=[sbm1])
            sc1_hist = []
            m1_done = None
            for r in range(4):
                if r % 2 == 0:
                    zones = [DDP[:, 0:512], DDP[:, 512:1024],
                             DDP[:, 1024:1536], DDP[:, 1536:2048]]
                else:
                    zones = [PAB_A[:, 0:512], PAB_B[:, 0:512],
                             CTP[:, 0:512], CTP[:, 512:1024]]
                if r == 0:
                    wv = [xq_done[-1], w1ld]
                elif r == 1:
                    wv = [sbe1]
                else:
                    wv = [sc1_hist[r - 2]]
                mmh = s.stage("tensor", [
                    (lambda e, fc=fc, hc=r * 8 + j, z=zones[j // 2],
                     o=(j % 2) * 256,
                     st=(j % 2 == 0 and fc == 0),
                     sp=(j % 2 == 1 and fc == EC - 1): e.matmul(
                        z[:, o:o + 256],
                        WP[:, RXL + fc * 4096 + hc * 128: RXL + fc * 4096 + hc * 128 + 128],
                        TP[:, 2048 + fc * 256: 2048 + (fc + 1) * 256],
                        start=st, stop=sp))
                    for j in range(8)
                    for fc in range(EC)
                ], waits=wv)
                sc1 = s.stage("vector", [
                    (lambda e, hc=r * 8 + j, z=zones[j // 2], o=(j % 2) * 256:
                     e.tensor_mul(
                        HP[:, hc * 256:(hc + 1) * 256], z[:, o:o + 256],
                        TPf[:, 0:256]))
                    for j in range(8)
                ], waits=[mmh])
                sc1_hist.append(sc1)
                gl = s.stage("scalar", [
                    (lambda e, hc=r * 8 + j: e.activation(
                        HP[:, hc * 256:(hc + 1) * 256],
                        HP[:, hc * 256:(hc + 1) * 256],
                        AF.Gelu, bias=m_b1h[:, hc:hc + 1]))
                    for j in range(8)
                ], waits=[sc1])
                if r == 0:
                    m1_done = s.stage("vector", [
                        lambda e: e.tensor_copy(HP[:, 8192:10240], HP[:, 0:2048]),
                    ], waits=[gl])
                else:
                    m1_done = s.stage("vector", [
                        (lambda e, r=r: e.tensor_max(
                            HP[:, 8192:10240], HP[:, 8192:10240],
                            HP[:, r * 2048:(r + 1) * 2048])),
                    ], waits=[gl, m1_done])
            m1_done = s.stage("vector", [
                lambda e: e.tensor_max(HP[:, 8192:9216], HP[:, 8192:9216],
                                       HP[:, 9216:10240]),
            ], waits=[m1_done])
            m1_done = s.stage("vector", [
                lambda e: e.tensor_max(HP[:, 8192:8704], HP[:, 8192:8704],
                                       HP[:, 8704:9216]),
            ])
            m1_done = s.stage("vector", [
                lambda e: e.tensor_max(HP[:, 8192:8448], HP[:, 8192:8448],
                                       HP[:, 8448:8704]),
            ])
            m132 = s.stage("tensor", [
                (lambda e, tc=tc: e.transpose(
                    CTP[:, tc * 128:(tc + 1) * 128],
                    HP[:, 8192 + tc * 128: 8192 + (tc + 1) * 128], IDN32[:]))
                for tc in range(TC2)
            ], waits=[m1_done])
            s.stage("vector", [
                (lambda e, tc=tc: e.reduce_max(
                    m_rx2[:, tc:tc + 1], CTP[:, tc * 128:(tc + 1) * 128], AX.X))
                for tc in range(TC2)
            ], waits=[m132])
            s.stage("vector", [
                lambda e: e.tensor_scalar(m_mx2[:, 0:2], m_rx2[:, 0:2],
                                          1e-5, 1.0 / 127.0, ALU.max, ALU.mult),
            ])
            s.stage("vector", [
                lambda e: e.reciprocal(m_s2c[:, 0:2], m_mx2[:, 0:2]),
                lambda e: e.tensor_single_scalar(m_s2i[:, 0:2], m_mx2[:, 0:2],
                                                 m_mw2d[:, 0:1], ALU.mult),
            ])
            rx2 = s.stage("vector", [
                lambda e: e.tensor_single_scalar(m_f2[:, 0:2], m_s2c[:, 0:2],
                                                 m_sw2b[:, 0:1], ALU.mult),
            ])
            s2o = s.dmas("gpsimd", [
                lambda e: e.dma_start(out=rmio[:, 0:2], in_=m_s2c[:, 0:2]),
                lambda e: e.dma_start(out=rmio[:, 2:4], in_=m_f2[:, 0:2]),
            ], waits=[rx2], stream="dgo")
            s2i_ = s.dmas("sync", [
                lambda e: e.dma_start(out=m_s1row[:, 0:256].rearrange("a (c p) -> a c p", p=128),
                                      in_=rmio[:, 0:2].rearrange("p c -> c p")[None],
                                      allow_slow_non_contiguous=True),
            ], waits=[s2o], stream="dsb")
            f2i_ = s.dmas("gpsimd", [
                lambda e: e.dma_start(out=r_f2[:, 0:256].rearrange("a (c p) -> a c p", p=128),
                                      in_=rmio[:, 2:4].rearrange("p c -> c p")[None],
                                      allow_slow_non_contiguous=True),
            ], waits=[s2o], stream="dgo")
            sbm2 = s.stage("tensor", [
                (lambda e, tc=tc: e.matmul(
                    CTP[:, tc * 128:(tc + 1) * 128], o32row,
                    m_s1row[:, tc * 128:(tc + 1) * 128],
                    start=(tc == 0), stop=(tc == TC2 - 1)))
                for tc in range(TC2)
            ], waits=[s2i_, rx2])
            sbe2 = s.stage("vector", [
                lambda e: e.tensor_copy(TPf[:, 256:512], CTP[:, 0:256]),
            ], waits=[sbm2])
            hq_prev = None
            for k0 in range(0, HC, 8):
                wq_ = [sbe2]
                if hq_prev is not None:
                    wq_.append(hq_prev)
                mul = s.stage("vector", [
                    (lambda e, kc=k0 + i: e.tensor_mul(
                        HP[:, kc * 256:(kc + 1) * 256],
                        HP[:, kc * 256:(kc + 1) * 256], TPf[:, 256:512]))
                    for i in range(8)
                ], waits=wq_)
                def _adds(kcs):
                    return [
                        (lambda e, kc=kc: e.tensor_scalar_add(
                            HPf[:, kc * 256:(kc + 1) * 256],
                            HP[:, kc * 256:(kc + 1) * 256], 1536.0))
                        for kc in kcs
                    ]
                if k0 == 0:
                    s.stage("vector", [
                        lambda e: e.tensor_scalar_add(TP[:, 0:256], HP[:, 0:256], 1536.0),
                    ], waits=[mul])
                    s.stage("vector", _adds([1]) + [
                        lambda e: e.tensor_scalar_add(TP[:, 0:256], TP[:, 0:256], -1536.0),
                    ])
                    s.stage("vector", _adds([2, 3]))
                    hq_prev = s.stage("vector", _adds([4, 5, 6, 7]))
                else:
                    hq_prev = s.stage("vector", _adds(range(k0, k0 + 8)), waits=[mul])
            hq2 = s.stage("vector", [
                lambda e: e.tensor_scalar_add(HPf[:, 256:4096], HPf[:, 256:4096], -1536.0),
                lambda e: e.tensor_scalar_add(HPf[:, 4096:8192], HPf[:, 4096:8192], -1536.0),
            ], waits=[hq_prev])
            last_ev1 = None
            for tc in range(TC2):
                g = t2 * TC2 + tc
                x2l2 = s.dmas("sync", [
                    (lambda e, g=g: e.dma_start(
                        out=TPf[:, 2048:3072],
                        in_=x2s[g * 128:(g + 1) * 128, 0:1024])),
                ], waits=[hq2] if tc == 0 else [prev_store], stream="dxa")
                oev = []
                for half in range(2):
                    pab = PAB_A if half == 0 else PAB_B
                    wv = [hq2, w2ab, w2cd, f2i_, b2_cast] if tc == 0 else [hq2]
                    if last_ev1 is not None:
                        wv.append(last_ev1)
                    mmo = s.stage("tensor", [
                        (lambda e, kc=kc, tc=tc, half=half, pab=pab: e.matmul(
                            pab[:, 0:512],
                            (TP[:, tc * 128: tc * 128 + 128] if kc == 0 else
                             HPf[:, kc * 256 + tc * 128: kc * 256 + tc * 128 + 128]),
                            WP[:, w2t_region(kc) + half * 512: w2t_region(kc) + (half + 1) * 512],
                            start=(kc == 0), stop=False))
                        for kc in range(HC)
                    ] + [
                        (lambda e, tc=tc, half=half, pab=pab: e.matmul(
                            pab[:, 0:512],
                            r_f2[:, tc * 128:(tc + 1) * 128],
                            r_b2[:, half * 512:(half + 1) * 512],
                            start=False, stop=True, tile_position=(64, 0))),
                    ], waits=wv)
                    ev1w = [mmo]
                    if oev:
                        ev1w.append(oev[-1])
                    elif prev_oev is not None:
                        ev1w.append(prev_oev)
                    ev1 = s.stage("scalar", [
                        (lambda e, tc=tc, half=half, pab=pab: e.activation(
                            TPf[:, 1024 + half * 512: 1024 + (half + 1) * 512],
                            pab[:, 0:512], AF.Copy, scale=m_s2i[:, tc:tc + 1])),
                    ], waits=ev1w)
                    last_ev1 = ev1
                    oev.append(s.stage("vector", [
                        (lambda e, half=half: e.tensor_add(
                            TPf[:, 3072 + half * 512: 3072 + (half + 1) * 512],
                            TPf[:, 1024 + half * 512: 1024 + (half + 1) * 512],
                            TPf[:, 2048 + half * 512: 2048 + (half + 1) * 512])),
                    ], waits=[ev1, x2l2]))
                prev_oev = oev[-1]
                prev_store = s.dmas("gpsimd", [
                    (lambda e, g=g: e.dma_start(
                        out=outd[g * 128:(g + 1) * 128, :],
                        in_=TPf[:, 3072:4096]))
                ], waits=[oev[0], oev[1]])

        s.mark("all")
        import os
        trunc = os.environ.get("MK_TRUNCATE", "")
        if trunc:
            s.truncate(trunc)
        fin = s.barrier()
        s.prog["gpsimd"].append(("fin", fin, [], None))

        with nc.Block() as block:
            def mk(engname):
                def body(eng):
                    hw = {}
                    for kind, waits, fns, inc in s.prog[engname]:
                        for (nm, val) in waits:
                            if hw.get(nm, -1) >= val:
                                continue
                            hw[nm] = val
                            eng.wait_ge(sems[nm], val)
                        if kind == "fin":
                            continue
                        if kind == "ccstage":
                            last = None
                            for fn in fns:
                                last = fn(eng)
                            nm, v = inc
                            last.then_inc(sems[nm], 1)
                        elif kind == "pestage":
                            last = None
                            for fn in fns:
                                last = fn(eng)
                            nm, v = inc
                            last.then_inc(sems[nm], 1)
                        elif kind == "stage":
                            for fn in fns:
                                fn(eng)
                            nm, v = inc
                            eng.drain().then_inc(sems[nm], 1)
                        else:
                            nm, v = inc
                            for fn in fns:
                                fn(eng).then_inc(sems[nm], 16)
                return body
            block.sync(mk("sync"))
            block.vector(mk("vector"))
            block.scalar(mk("scalar"))
            block.tensor(mk("tensor"))
            block.gpsimd(mk("gpsimd"))
    return nc


_NC_CACHE = {}


def kernel(**inputs):
    if "nc" not in _NC_CACHE:
        _NC_CACHE["nc"] = build_program()
    nc = _NC_CACHE["nc"]
    x = np.ascontiguousarray(np.asarray(inputs["L0_attn_x"], dtype=np.float32))
    f32 = lambda k: np.ascontiguousarray(np.asarray(inputs[k], dtype=np.float32))
    shared = {
        "Wq": f32("L0_attn_Wq"), "Wk": f32("L0_attn_Wk"),
        "Wv": f32("L0_attn_Wv"), "Wo": f32("L0_attn_Wo"),
        "bq": f32("L0_attn_bq"), "bk": f32("L0_attn_bk"),
        "bv": f32("L0_attn_bv"), "bo": f32("L0_attn_bo"),
        "proj": f32("L0_attn_proj"),
        "w1": f32("L0_ffn_w1"), "b1": f32("L0_ffn_b1"),
        "w2": f32("L0_ffn_w2"), "b2": f32("L0_ffn_b2"),
        "gamma": f32("L0_ffn_gamma"), "beta": f32("L0_ffn_beta"),
    }
    in_maps = [dict(shared, x=x[c]) for c in range(NCORES)]
    res = run_bass_kernel_spmd(nc, in_maps, core_ids=list(range(NCORES)))
    return np.stack([np.asarray(res.results[c]["out"]) for c in range(NCORES)],
                    axis=0)


# version 11
# speedup vs baseline: 1.0808x; 1.0655x over previous
import contextlib
import numpy as np
import concourse.bass as bass
import concourse.mybir as mybir
from concourse.bass_utils import run_bass_kernel_spmd

F32 = mybir.dt.float32
F16 = mybir.dt.float16
AF = mybir.ActivationFunctionType
ALU = mybir.AluOpType
AX = mybir.AxisListType

NCORES = 8
N = 4096
E = 1024
H = 16
M = 256
HID = 4096
T = 512
NT = N // T
TC = T // 128
EC = E // 128
HC = HID // 128
CSTAB = 20.0
KEPS = 1e-4
LNEPS = 1e-5
CQ = float(64.0 ** -0.25)

ENGINES = ("sync", "vector", "scalar", "tensor", "gpsimd")


class Sched:
    DSTREAMS = ("dxa", "dwt", "dwq", "dsb", "dst", "dgo", "dpr", "dscalar")

    def __init__(self):
        self.prog = {e: [] for e in ENGINES}
        self.cnt = {e: 0 for e in ENGINES}
        self.dma = {k: 0 for k in self.DSTREAMS}
        self.cc = 0

    def _rw(self, waits):
        out = []
        for w in waits:
            if not w:
                continue
            nm, v = w
            if nm in self.dma:
                v = self.dma[nm]
                if v == 0:
                    continue
            out.append((nm, v))
        return out

    def stage(self, eng, fns, waits=()):
        w = self._rw(waits)
        if self.cnt[eng] > 0:
            w.append((eng, self.cnt[eng]))
        self.cnt[eng] += 1
        kind = "pestage"
        self.prog[eng].append((kind, w, fns, (eng, self.cnt[eng])))
        return (eng, self.cnt[eng])

    def dmas(self, eng, fns, waits=(), stream=None):
        nm = stream or ("dscalar" if eng == "scalar" else
                        ("dst" if eng == "gpsimd" else "dwt"))
        w = self._rw(waits)
        self.dma[nm] += 16 * len(fns)
        self.prog[eng].append(("dma", w, fns, (nm, self.dma[nm])))
        return (nm, self.dma[nm])

    def collective(self, fn, waits=()):
        self.cc += 1
        self.prog["gpsimd"].append(("ccstage", self._rw(waits), [fn],
                                    ("cc", self.cc)))
        return ("cc", self.cc)

    def mark(self, name):
        self.marks = getattr(self, "marks", {})
        self.marks[name] = {e: len(self.prog[e]) for e in ENGINES}

    def truncate(self, name):
        for e in ENGINES:
            self.prog[e] = self.prog[e][: self.marks[name][e]]
        self.cnt = {e: 0 for e in ENGINES}
        self.dma = {k: 0 for k in self.DSTREAMS}
        self.cc = 0
        for e in ENGINES:
            for kind, waits, fns, inc in self.prog[e]:
                if inc is None:
                    continue
                nm, v = inc
                if nm in self.dma:
                    self.dma[nm] = max(self.dma[nm], v)
                elif nm == "cc":
                    self.cc = max(self.cc, v)
                else:
                    self.cnt[nm] = max(self.cnt[nm], v)

    def barrier(self):
        w = [(e, self.cnt[e]) for e in ENGINES if self.cnt[e] > 0]
        w += [(k, v) for k, v in self.dma.items() if v > 0]
        if self.cc:
            w.append(("cc", self.cc))
        return w


def build_program():
    nc = bass.Bass("TRN2", target_bir_lowering=False, debug=False)

    xin = nc.dram_tensor("x", [N, E], F32, kind="ExternalInput").ap()
    Wq = nc.dram_tensor("Wq", [E, E], F32, kind="ExternalInput").ap()
    Wk = nc.dram_tensor("Wk", [E, E], F32, kind="ExternalInput").ap()
    Wv = nc.dram_tensor("Wv", [E, E], F32, kind="ExternalInput").ap()
    Wo = nc.dram_tensor("Wo", [E, E], F32, kind="ExternalInput").ap()
    bqd = nc.dram_tensor("bq", [E], F32, kind="ExternalInput").ap()
    bkd = nc.dram_tensor("bk", [E], F32, kind="ExternalInput").ap()
    bvd = nc.dram_tensor("bv", [E], F32, kind="ExternalInput").ap()
    bod = nc.dram_tensor("bo", [E], F32, kind="ExternalInput").ap()
    projd = nc.dram_tensor("proj", [M, 64], F32, kind="ExternalInput").ap()
    w1d = nc.dram_tensor("w1", [HID, E], F32, kind="ExternalInput").ap()
    b1d = nc.dram_tensor("b1", [HID], F32, kind="ExternalInput").ap()
    w2d = nc.dram_tensor("w2", [E, HID], F32, kind="ExternalInput").ap()
    b2d = nc.dram_tensor("b2", [E], F32, kind="ExternalInput").ap()
    gammad = nc.dram_tensor("gamma", [E], F32, kind="ExternalInput").ap()
    betad = nc.dram_tensor("beta", [E], F32, kind="ExternalInput").ap()
    outd = nc.dram_tensor("out", [N, E], F32, kind="ExternalOutput").ap()

    x2s = nc.dram_tensor("x2s", [N, E], F32)
    wsw = nc.dram_tensor("wsw", [4 * E, E], F16)
    wsp = nc.dram_tensor("wsp", [M, 128], F16)
    ws1 = nc.dram_tensor("ws1", [HID, E], F16)
    ws2 = nc.dram_tensor("ws2", [E, HID], F16)
    rmio = nc.dram_tensor("rmio", [128, 16], F32)
    rmo = nc.dram_tensor("rmo", [128, 16], F32)
    srow = nc.dram_tensor("srow", [128, 1], F32)

    identd = nc.inline_tensor(np.eye(128, dtype=np.float16), "identc")
    ident32d = nc.inline_tensor(np.eye(128, dtype=np.float32), "ident32c")
    ind_np = np.zeros((128, 2), dtype=np.float16)
    ind_np[0:64, 0] = 1.0
    ind_np[64:128, 1] = 1.0
    indd = nc.inline_tensor(ind_np, "indc")

    s = Sched()
    es = contextlib.ExitStack()
    with es:
        sbuf = lambda n_, sh, dt_: es.enter_context(nc.sbuf_tensor(n_, sh, dt_))
        WP = sbuf("WP", [128, 65536], F16)
        HP = sbuf("HP", [128, 12320], F32)
        TP = sbuf("TP", [128, 9472], F16)
        MS = sbuf("MS", [128, 872], F32)
        CXP = sbuf("CXP", [128, 2080], F16)
        BR = sbuf("BR", [128, 2193], F16)
        IDN = sbuf("IDN", [128, 128], F16)
        IDN32 = sbuf("IDN32", [128, 128], F32)
        IND = sbuf("IND", [128, 2], F16)
        PJ = sbuf("PJ", [128, 256], F16)
        KC = sbuf("KC", [128, 8], F32)

        PAB_A = es.enter_context(nc.psum_tensor("PAB_A", [128, 512], F32))
        PAB_B = es.enter_context(nc.psum_tensor("PAB_B", [128, 512], F32))
        DDP = es.enter_context(nc.psum_tensor("DDP", [128, 2048], F32))
        CTP = es.enter_context(nc.psum_tensor("CTP", [128, 1024], F32))
        PA16 = PAB_A.bitcast(F16)
        PB16 = PAB_B.bitcast(F16)

        sems = {}
        for nm in ("sync", "vector", "scalar", "tensor", "gpsimd",
                   "dxa", "dwt", "dwq", "dsb", "dst", "dgo", "dpr", "dscalar", "cc"):
            sems[nm] = es.enter_context(nc.semaphore(nm))

        TPf = TP.bitcast(F32)
        HPf = HP.bitcast(F16)

        RA, RB, RXL, RC, RD = 0, 8192, 16384, 49152, 57344
        KN = 4352

        def w2t_region(kc):
            return [RA, RB, RC, RD][kc // 8] + (kc % 8) * 1024

        m_bn = MS[:, 0:12]
        m_mv = MS[:, 16:24]
        m_sd = MS[:, 24:28]
        m_rstd = MS[:, 28:32]
        m_negm = MS[:, 32:36]
        m_cbk = MS[:, 36:44]
        m_cbq = MS[:, 44:52]
        m_b1h = MS[:, 52:84]
        m_RM = MS[:, 84:100]
        m_diag = MS[:, 100:164]
        m_asum = MS[:, 800:864]
        m_nst = MS[:, 164:228]
        m_edb = MS[:, 228:292]
        m_stab = MS[:, 292:293]
        m_cstab = MS[:, 293:294]
        m_red2 = MS[:, 296:298]
        m_red4 = MS[:, 294:296]
        m_dinv = MS[:, 300:304]
        m_wred = MS[:, 304:306]
        m_sca = MS[0:1, 306:316]
        m_rx1 = MS[:, 320:321]
        m_mx1 = MS[:, 321:322]
        m_s1c = MS[:, 322:323]
        m_s1i = MS[:, 324:328]
        m_rx2 = MS[:, 328:332]
        m_mx2 = MS[:, 332:336]
        m_s2c = MS[:, 336:340]
        m_s2i = MS[:, 340:344]
        m_f2 = MS[:, 344:348]
        m_sw1b = MS[:, 348:349]
        m_sw2b = MS[:, 349:350]
        m_mw1db = MS[:, 350:351]
        m_mw2db = MS[:, 351:352]
        o32row = MS[0:1, 416:544]
        m_bn2 = MS[:, 352:376]
        m_mv2 = MS[:, 376:380]
        m_rstd2 = MS[:, 380:382]
        m_sd2 = MS[:, 390:392]
        m_rx1b = MS[:, 382:384]
        m_mx1b = MS[:, 384:386]
        m_s1cb = MS[:, 386:388]
        m_mw1d = MS[:, 392:393]
        m_mw2d = MS[:, 393:394]
        m_L1col = MS[:, 394:395]
        m_L1 = MS[0:1, 395:396]
        m_cLt = MS[:, 396:397]
        m_mLt = MS[:, 397:398]
        m_eD = MS[:, 398:399]
        m_Lprev = MS[:, 399:400]
        m_mS = MS[:, 400:401]
        m_f = MS[:, 401:402]
        m_sx = MS[:, 402:410]
        m_dinv8 = MS[:, 864:872]
        m_srow = MS[0:1, 544:672]
        m_s1row = MS[0:1, 544:800]

        r_bv = BR[0:1, 0:1024]
        ones16row = BR[0:1, 1024:1152]
        r_bo = BR[32:33, 0:1024]
        ones16row32 = BR[32:33, 1024:1152]
        r_f2 = BR[64:65, 0:512]
        r_b2 = BR[64:65, 512:1536]
        r_eps = BR[32:33, 1152:2192]
        ones16col = BR[:, 2192:2193]
        k4096h = BR[0:1, 1536:1537]

        k1536 = KC[:, 0:1]
        kmC = KC[:, 1:2]
        kpC = KC[:, 2:3]
        kCQ = KC[:, 3:4]
        kmhalf = KC[:, 4:5]
        kones32 = KC[:, 5:6]
        kLNEPS = KC[:, 6:7]

        def dma1(eng, outap, inap, waits=(), stream=None, **kw):
            return s.dmas(eng, [lambda e, o=outap, i=inap, kw=kw:
                                e.dma_start(out=o, in_=i, **kw)], waits=waits,
                          stream=stream)

        s.dmas("sync", [
            lambda e: e.dma_start(out=IDN[:], in_=identd.ap()),
            lambda e: e.dma_start(out=IDN32[:], in_=ident32d.ap()),
            lambda e: e.dma_start(out=IND[:], in_=indd.ap()),
        ])
        st_ones = s.stage("vector", [
            lambda e: e.memset(KC[:, 0:1], 1536.0),
            lambda e: e.memset(KC[:, 1:2], -CSTAB),
            lambda e: e.memset(KC[:, 2:3], CSTAB),
            lambda e: e.memset(KC[:, 3:4], CQ),
            lambda e: e.memset(KC[:, 4:5], -0.5),
            lambda e: e.memset(KC[:, 5:6], 1.0),
            lambda e: e.memset(KC[:, 6:7], LNEPS),
            lambda e: e.memset(o32row, 1.0),
            lambda e: e.memset(ones16row, 1.0),
            lambda e: e.memset(ones16row32, 1.0),
            lambda e: e.memset(ones16col, 1.0),
            lambda e: e.memset(m_RM, -3.0e38),
            lambda e: e.memset(KC[:, 7:8], 4096.0),
            lambda e: e.memset(k4096h, 4096.0),
            lambda e: e.memset(m_Lprev, -1.0e30),
        ])

        row_ld = s.dmas("gpsimd", [
            lambda e: e.dma_start(out=r_bv, in_=bvd[None, :]),
            lambda e: e.dma_start(out=r_bo, in_=bod[None, :]),
        ], stream="dpr")
        wlist = [(Wk, RA), (Wv, RB), (Wq, RC), (Wo, RD)]
        cast_done = []
        for wi, (wap, dst) in enumerate(wlist):
            cast_done.append(dma1("gpsimd", wsw[wi * E:(wi + 1) * E, :], wap, stream="dpr"))
        proj_cast = s.dmas("gpsimd", [
            lambda e: e.dma_start(out=wsp[:, 0:64], in_=projd),
            lambda e: e.dma_start(out=wsp[:, 64:128], in_=projd),
        ], stream="dpr")
        b2_cast = dma1("gpsimd", r_b2, b2d[None, :], stream="dpr")
        wt_loads = []
        for wi, (wap, dst) in enumerate(wlist):
            fns = []
            for kc in range(EC):
                fns.append(lambda e, wi=wi, kc=kc, dst=dst: e.dma_start(
                    out=WP[:, dst + kc * 1024: dst + (kc + 1) * 1024],
                    in_=wsw[wi * E:(wi + 1) * E, kc * 128:(kc + 1) * 128],
                    transpose=True))
            wt_loads.append(s.dmas("sync", fns, waits=[cast_done[wi]]))
        pj_load = s.dmas("sync", [
            lambda e: e.dma_start(out=PJ[:, :], in_=wsp[:, :], transpose=True),
        ], waits=[proj_cast])

        bias_ld = s.dmas("sync", [
            lambda e: e.dma_start(out=m_cbk, in_=bkd.rearrange("(c p) -> p c", p=128),
                                  allow_slow_non_contiguous=True),
            lambda e: e.dma_start(out=m_cbq, in_=bqd.rearrange("(c p) -> p c", p=128),
                                  allow_slow_non_contiguous=True),
            lambda e: e.dma_start(out=m_b1h, in_=b1d.rearrange("(c p) -> p c", p=128),
                                  allow_slow_non_contiguous=True),
        ])
        sc_bias = s.stage("vector", [
            lambda e: e.tensor_scalar_mul(m_cbk, m_cbk, CQ),
            lambda e: e.tensor_scalar_mul(m_cbq, m_cbq, CQ),
        ], waits=[bias_ld])

        def emit_ln(src_of_tc, dst_of_tc, wait_load, dst_w=()):
            for tc in range(TC):
                s.stage("vector", [
                    (lambda e, tc=tc: e.bn_stats(m_bn[:, 0:6], src_of_tc(tc)[:, 0:512])),
                    (lambda e, tc=tc: e.bn_stats(m_bn[:, 6:12], src_of_tc(tc)[:, 512:1024])),
                ], waits=[wait_load] if tc == 0 else [])
                s.stage("vector", [
                    (lambda e, tc=tc: e.bn_aggr(m_mv[:, 2 * tc:2 * tc + 2], m_bn[:, 0:12])),
                ])
            veps = s.stage("vector", [
                lambda e: e.tensor_scalar_add(m_sd, m_mv[:, 1:8:2], LNEPS)])
            ssq = s.stage("scalar", [
                lambda e: e.activation(m_sd, m_sd, AF.Sqrt)], waits=[veps])
            s.stage("vector", [
                lambda e: e.reciprocal(m_rstd, m_sd)], waits=[ssq])
            vnm = s.stage("vector", [
                lambda e: e.scalar_tensor_tensor(m_negm, m_mv[:, 0:8:2], -1.0,
                                                 m_rstd, ALU.mult, ALU.mult)])
            last = None
            for tc in range(TC):
                last = s.stage("scalar", [
                    (lambda e, tc=tc: e.activation(
                        dst_of_tc(tc), src_of_tc(tc), AF.Identity,
                        bias=m_negm[:, tc:tc + 1], scale=m_rstd[:, tc:tc + 1])),
                ], waits=[vnm] + list(dst_w) if tc == 0 else [])
            return last

        def emit_tblock(src_of_k, dst_of_k, nk, wait_src, ps16, extra_w=()):
            last_ev = None
            for g0 in range(0, nk, 8):
                gn = min(8, nk - g0)
                w = [wait_src] + list(extra_w)
                if last_ev is not None:
                    w.append(last_ev)
                tp_ = s.stage("tensor", [
                    (lambda e, k=g0 + i, sl=i: e.transpose(
                        ps16[:, sl * 128:(sl + 1) * 128], src_of_k(k), IDN[:]))
                    for i in range(gn)
                ], waits=w)
                last_ev = s.stage("vector", [
                    (lambda e, k=g0 + i, sl=i: e.tensor_copy(
                        dst_of_k(k), ps16[:, sl * 128:(sl + 1) * 128]))
                    for i in range(gn)
                ], waits=[tp_])
            return last_ev

        def emit_proj(wreg, cb_col, t, dst_of_ec, waits):
            evs = []
            for ec in range(EC):
                pab = PAB_A if ec % 2 == 0 else PAB_B
                w = list(waits) if ec <= 1 else []
                if len(evs) >= 2:
                    w.append(evs[-2])
                mm = s.stage("tensor", [
                    (lambda e, kc=kc, ec=ec, pab=pab: e.matmul(
                        pab[:, 0:512],
                        WP[:, wreg + kc * 1024 + ec * 128: wreg + kc * 1024 + ec * 128 + 128],
                        WP[:, RXL + kc * 4096 + t * 512: RXL + kc * 4096 + (t + 1) * 512],
                        start=(kc == 0), stop=(kc == EC - 1)))
                    for kc in range(EC)
                ], waits=w)
                evs.append(s.stage("scalar", [
                    (lambda e, ec=ec, pab=pab: e.activation(
                        dst_of_ec(ec), pab[:, 0:512], AF.Identity,
                        bias=cb_col[:, ec:ec + 1], scale=kCQ)),
                ], waits=[mm]))
            return evs[-1]

        def emit_diag(src_off, wait_kn, extra_first=()):
            dmm = []
            for ec in range(EC):
                p = ec % 2
                wsq = [wait_kn] if ec == 0 else []
                if len(dmm) >= 2:
                    wsq.append(dmm[ec - 2])
                sq_ = s.stage("scalar", [
                    (lambda e, ec=ec, p=p: e.activation(
                        TP[:, 8448 + p * 512: 8448 + (p + 1) * 512],
                        TP[:, src_off + ec * 512: src_off + (ec + 1) * 512],
                        AF.Square)),
                ], waits=wsq)
                wm = [sq_]
                if ec == 0:
                    wm += list(extra_first)
                dmm.append(s.stage("tensor", [
                    (lambda e, ec=ec, tc=tc, p=p: e.matmul(
                        DDP[:, tc * 16 + 2 * ec: tc * 16 + 2 * ec + 2],
                        TP[:, 8448 + p * 512 + tc * 128: 8448 + p * 512 + tc * 128 + 128],
                        IND[:], start=(ec == 0 and tc == 0),
                        stop=(ec == EC - 1 and tc == TC - 1)))
                    for tc in range(TC)
                ], waits=wm))
            return s.stage("vector", [
                lambda e: e.tensor_copy(m_diag, DDP[:, 0:64])], waits=[dmm[-1]])

        wq_items = ([("w1", ch, None) for ch in range(HC)] +
                    [("w2", ch, ss) for ch in range(EC) for ss in range(4)])
        wq1 = {"i": 0, "last": None}
        wq2 = {"i": 0, "last": None, "mean": None}
        wq_stg = HP[:, 4096:5120]
        wq_ta = HP[:, 5120:6144]
        wq_tb = HP[:, 6144:7168]
        wq_o16 = HPf[:, 14336:15360]

        def wq_src(it):
            wname, ch, ss = it
            if wname == "w1":
                return w1d[ch * 128:(ch + 1) * 128, :], m_asum[:, ch:ch + 1], m_sw1b
            return (w2d[ch * 128:(ch + 1) * 128, ss * 1024:(ss + 1) * 1024],
                    m_asum[:, 32 + ch * 4 + ss: 33 + ch * 4 + ss], m_sw2b)

        def wq_dst(it):
            wname, ch, ss = it
            if wname == "w1":
                return ws1[ch * 128:(ch + 1) * 128, :]
            return ws2[ch * 128:(ch + 1) * 128, ss * 1024:(ss + 1) * 1024]

        def emit_wq_pass1(n):
            for _ in range(n):
                i = wq1["i"]
                if i >= len(wq_items):
                    return
                wq1["i"] += 1
                src_, acol, _ = wq_src(wq_items[i])
                w = [wq1["last"]] if wq1["last"] else []
                ld = dma1("sync", wq_stg, src_, waits=w, stream="dwq")
                wq1["last"] = s.stage("scalar", [
                    (lambda e, acol=acol: e.activation(
                        wq_stg, wq_stg, AF.Abs, accum_out=acol)),
                ], waits=[ld])

        def emit_wq_mean():
            red = s.stage("vector", [
                lambda e: e.reduce_sum(m_wred[:, 0:1], m_asum[:, 0:32], AX.X),
                lambda e: e.reduce_sum(m_wred[:, 1:2], m_asum[:, 32:64], AX.X),
            ], waits=[wq1["last"]])
            pm = s.stage("tensor", [
                lambda e: e.matmul(CTP[0:1, 0:1], kones32, m_wred[:, 0:1],
                                   start=True, stop=False),
                lambda e: e.matmul(CTP[0:1, 1:2], kones32, m_wred[:, 1:2],
                                   start=False, stop=True),
            ], waits=[red])
            s.stage("vector", [
                lambda e: e.tensor_scalar(m_sca[:, 0:2], CTP[0:1, 0:2],
                                          1.0 / (HID * E), 1e-5, ALU.mult, ALU.max),
            ], waits=[pm])
            sc = s.stage("vector", [
                lambda e: e.reciprocal(m_sca[:, 2:4], m_sca[:, 0:2]),
                lambda e: e.tensor_scalar_mul(m_sca[:, 4:6], m_sca[:, 0:2], 1.0 / 127.0),
            ])
            bc = s.stage("tensor", [
                lambda e: e.matmul(CTP[:, 0:1], o32row, m_sca[:, 2:3],
                                   start=True, stop=False),
                lambda e: e.matmul(CTP[:, 1:2], o32row, m_sca[:, 3:4],
                                   start=False, stop=False),
                lambda e: e.matmul(CTP[:, 2:3], o32row, m_sca[:, 4:5],
                                   start=False, stop=False),
                lambda e: e.matmul(CTP[:, 3:4], o32row, m_sca[:, 5:6],
                                   start=False, stop=False),
                lambda e: e.matmul(CTP[:, 4:5], o32row, m_sca[:, 0:1],
                                   start=False, stop=False),
                lambda e: e.matmul(CTP[:, 5:6], o32row, m_sca[:, 1:2],
                                   start=False, stop=True),
            ], waits=[sc])
            wq2["mean"] = s.stage("vector", [
                lambda e: e.tensor_copy(m_sw1b, CTP[:, 0:1]),
                lambda e: e.tensor_copy(m_sw2b, CTP[:, 1:2]),
                lambda e: e.tensor_copy(m_mw1db, CTP[:, 2:3]),
                lambda e: e.tensor_copy(m_mw2db, CTP[:, 3:4]),
                lambda e: e.tensor_copy(m_mw1d, CTP[:, 4:5]),
                lambda e: e.tensor_copy(m_mw2d, CTP[:, 5:6]),
            ], waits=[bc])

        def emit_wq_pass2(n):
            for _ in range(n):
                i = wq2["i"]
                if i >= len(wq_items):
                    return
                wq2["i"] += 1
                it = wq_items[i]
                src_, _, swb = wq_src(it)
                w = [wq2["mean"]]
                if wq2["last"]:
                    w.append(wq2["last"])
                ld = dma1("sync", wq_stg, src_, waits=w, stream="dwq")
                g1 = s.stage("vector", [
                    (lambda e, swb=swb: e.tensor_scalar(
                        wq_ta, wq_stg, swb[:, 0:1], 0.5, ALU.mult, ALU.is_gt)),
                    (lambda e, swb=swb: e.tensor_scalar(
                        wq_tb, wq_stg, swb[:, 0:1], -0.5, ALU.mult, ALU.is_lt)),
                ], waits=[ld] + ([wq2["last"]] if wq2["last"] else []))
                g2 = s.stage("vector", [
                    lambda e: e.tensor_sub(wq_o16, wq_ta, wq_tb),
                ])
                wq2["last"] = dma1("gpsimd", wq_dst(it), wq_o16, waits=[g2],
                                   stream="dgo")

        s.mark("prologue")
        prol_w = s.barrier()
        ctx_init = s.stage("vector", [
            lambda e: e.memset(HP[:, 8192:10272], 0.0),
        ], waits=prol_w)
        ctx_last = None
        last_xuse = None
        for t in range(NT):
            w = list(prol_w) if t == 0 else []
            if last_xuse is not None:
                w.append(last_xuse)
            xld = s.dmas("sync", [
                (lambda e, t=t: e.dma_start(
                    out=HP[:, 0:4096].rearrange("p (a m) -> p a m", a=TC),
                    in_=xin[t * 512:(t + 1) * 512, :].rearrange("(a p) m -> p a m", p=128)))
            ], waits=w, stream="dxa")
            lnd = emit_ln(lambda tc: HP[:, tc * 1024:(tc + 1) * 1024],
                          lambda tc: TP[:, tc * 1024:(tc + 1) * 1024], xld,
                          dst_w=[ctx_last] if ctx_last else [])
            last_xuse = lnd
            trd = emit_tblock(
                lambda k: TP[:, (k // EC) * 1024 + (k % EC) * 128:
                             (k // EC) * 1024 + (k % EC) * 128 + 128],
                lambda k, t=t: WP[:, RXL + (k % EC) * 4096 + t * 512 + (k // EC) * 128:
                                  RXL + (k % EC) * 4096 + t * 512 + (k // EC) * 128 + 128],
                TC * EC, lnd, PA16)
            if t == 0:
                s.mark("p1tr")
            kev = emit_proj(RA, m_cbk, t,
                            lambda ec: TP[:, KN + ec * 512: KN + (ec + 1) * 512],
                            [trd, wt_loads[0], sc_bias, st_ones] if t == 0
                            else [trd])
            if t == 0:
                s.mark("p1kp")
            vev = []
            for tc in range(TC):
                for half in range(2):
                    pab = PAB_A if half == 0 else PAB_B
                    wv = [kev] if tc == 0 else []
                    if t == 0 and tc == 0 and half == 0:
                        wv += [wt_loads[1], row_ld]
                    if len(vev) >= 2:
                        wv.append(vev[-2])
                    mmv = s.stage("tensor", [
                        (lambda e, kc=kc, tc=tc, half=half, pab=pab, t=t: e.matmul(
                            pab[:, 0:512],
                            WP[:, RXL + kc * 4096 + t * 512 + tc * 128:
                               RXL + kc * 4096 + t * 512 + tc * 128 + 128],
                            WP[:, RB + kc * 1024 + half * 512: RB + kc * 1024 + (half + 1) * 512],
                            start=(kc == 0), stop=False))
                        for kc in range(EC)
                    ] + [
                        (lambda e, half=half, pab=pab: e.matmul(
                            pab[:, 0:512], ones16row,
                            BR[0:1, half * 512:(half + 1) * 512],
                            start=False, stop=True)),
                    ], waits=wv)
                    vev.append(s.stage("scalar", [
                        (lambda e, tc=tc, half=half, pab=pab: e.activation(
                            TP[:, tc * 1040 + half * 520: tc * 1040 + half * 520 + 520]
                            .rearrange("p (h d) -> p h d", d=65)[:, :, 0:64],
                            pab[:, 0:512].rearrange("p (h d) -> p h d", d=64),
                            AF.Copy)),
                    ], waits=[mmv]))
            ones_v = s.stage("vector", [
                lambda e: e.memset(
                    TP[:, 0:4160].rearrange("p (a h d) -> p a h d", a=TC, d=65)
                    [:, :, :, 64:65], 1.0),
            ], waits=[vev[-1]])
            dgd = emit_diag(KN, kev,
                            extra_first=[ctx_last] if ctx_last else list(prol_w))
            red_tok = {0: None, 1: None}
            for hp in range(8):
                par1 = hp % 2
                if hp == 0:
                    wm = [kev, dgd]
                elif hp == 1:
                    wm = [kev, dgd, vev[-1]]
                    if wq2["mean"] is not None:
                        wm.append(wq2["mean"])
                else:
                    wm = [red_tok[par1]]
                if t == 0 and hp == 0:
                    wm.append(pj_load)
                if par1 == 0:
                    zlo = lambda tc: DDP[:, tc * 256: tc * 256 + 256]
                    zhi = lambda tc: DDP[:, 1024 + tc * 256: 1024 + tc * 256 + 256]
                else:
                    zlo = lambda tc: (PAB_A[:, (tc % 2) * 256: (tc % 2) * 256 + 256]
                                      if tc < 2 else
                                      CTP[:, (tc % 2) * 256: (tc % 2) * 256 + 256])
                    zhi = lambda tc: (PAB_B[:, (tc % 2) * 256: (tc % 2) * 256 + 256]
                                      if tc < 2 else
                                      CTP[:, 512 + (tc % 2) * 256: 512 + (tc % 2) * 256 + 256])
                mm = s.stage("tensor", [
                    mmf
                    for tc in range(TC)
                    for mmf in [
                        (lambda e, hp=hp, tc=tc, z=zlo(tc): e.matmul(
                            z,
                            TP[0:64, KN + hp * 512 + tc * 128: KN + hp * 512 + tc * 128 + 128],
                            PJ[0:64, :], start=(tc % 2 == 0), stop=(tc % 2 == 1),
                            tile_position=(0, 0))),
                        (lambda e, hp=hp, tc=tc, z=zhi(tc): e.matmul(
                            z,
                            TP[64:128, KN + hp * 512 + tc * 128: KN + hp * 512 + tc * 128 + 128],
                            PJ[64:128, :], start=(tc % 2 == 0), stop=(tc % 2 == 1),
                            tile_position=(64, 0))),
                    ]
                ], waits=wm)
                if t == 0 and hp == 0:
                    s.mark("p1d1")
                if par1 == 0:
                    red_tok[0] = s.stage("vector", [
                        lambda e: e.reduce_max(
                            m_red2,
                            DDP[:, 0:2048].rearrange("p (h m) -> p h m", h=2),
                            AX.X),
                    ], waits=[mm])
                else:
                    red_tok[1] = s.stage("vector", [
                        lambda e: e.reduce_max(m_red4[:, 0:1], PAB_A[:, 0:512], AX.X),
                        lambda e: e.reduce_max(m_red4[:, 1:2], PAB_B[:, 0:512], AX.X),
                        lambda e: e.reduce_max(
                            m_red2,
                            CTP[:, 0:1024].rearrange("p (h m) -> p h m", h=2),
                            AX.X),
                    ], waits=[mm])
                    s.stage("vector", [
                        lambda e: e.tensor_max(m_red2, m_red2, m_red4),
                    ])
                rm_done = s.stage("vector", [
                    (lambda e, hp=hp: e.tensor_max(
                        m_RM[:, 2 * hp:2 * hp + 2], m_RM[:, 2 * hp:2 * hp + 2],
                        m_red2)),
                ])
            if t == 0:
                s.mark("p1dd")
            ra = s.stage("vector", [
                lambda e: e.reduce_max(m_L1col, m_RM, AX.X)])
            ra_last = ra
            rowm = s.stage("tensor", [
                lambda e: e.matmul(CTP[0:1, 0:128], m_L1col, IDN32[:],
                                   start=True, stop=True)], waits=[ra])
            rL = s.stage("vector", [
                lambda e: e.reduce_max(m_L1, CTP[0:1, 0:128], AX.X)], waits=[rowm])
            bcm = s.stage("tensor", [
                lambda e: e.matmul(CTP[:, 0:1], o32row, m_L1,
                                   start=True, stop=True)], waits=[rL])
            vL = s.stage("vector", [
                lambda e: e.tensor_scalar(m_cLt, CTP[:, 0:1], -1.0, CSTAB,
                                          ALU.mult, ALU.add),
                lambda e: e.tensor_scalar_mul(m_mLt, CTP[:, 0:1], -1.0),
            ], waits=[bcm])
            eD = s.stage("scalar", [
                lambda e: e.activation(m_eD, m_Lprev, AF.Exp, bias=m_mLt[:, 0:1]),
            ], waits=[vL])
            s.stage("vector", [
                lambda e: e.tensor_scalar_mul(m_Lprev, m_mLt, -1.0),
            ], waits=[eD])
            edl = s.stage("vector", [
                lambda e: e.tensor_scalar(m_edb, m_diag, -0.5, m_mLt[:, 0:1],
                                          ALU.mult, ALU.add),
            ], waits=[dgd, vL])
            prev_ctx_ev = None
            prev_readers = [red_tok[0], red_tok[1]]
            last_cmmz = {0: None, 1: None}
            cmm = None
            for hp in range(8):
                wm = [kev] + prev_readers
                hmm = s.stage("tensor", [
                    mmf
                    for tc in range(TC)
                    for mmf in [
                        (lambda e, hp=hp, tc=tc: e.matmul(
                            DDP[:, tc * 256: tc * 256 + 256],
                            TP[0:64, KN + hp * 512 + tc * 128: KN + hp * 512 + tc * 128 + 128],
                            PJ[0:64, :], start=(tc % 2 == 0), stop=(tc % 2 == 1),
                            tile_position=(0, 0))),
                        (lambda e, hp=hp, tc=tc: e.matmul(
                            DDP[:, 1024 + tc * 256: 1024 + tc * 256 + 256],
                            TP[64:128, KN + hp * 512 + tc * 128: KN + hp * 512 + tc * 128 + 128],
                            PJ[64:128, :], start=(tc % 2 == 0), stop=(tc % 2 == 1),
                            tile_position=(64, 0))),
                    ]
                ], waits=wm)
                prev_readers = []
                for sub in range(2):
                    h = 2 * hp + sub
                    kz = TP[:, 8448:9472] if sub == 0 else HPf[:, 15360:16384]
                    kfw = [hmm, edl]
                    if last_cmmz.get(sub) is not None:
                        kfw.append(last_cmmz[sub])
                    kf = s.stage("scalar", [
                        (lambda e, sub=sub, h=h, tc=tc, kz=kz: e.activation(
                            kz[:, tc * 256:(tc + 1) * 256],
                            DDP[:, sub * 1024 + tc * 256:
                                sub * 1024 + (tc + 1) * 256],
                            AF.Exp, bias=m_edb[:, tc * 16 + h: tc * 16 + h + 1]))
                        for tc in range(TC)
                    ], waits=kfw)
                    prev_readers.append(kf)
                    zb = h % 2
                    wcm = [kf, ones_v]
                    if prev_ctx_ev is not None and zb == 0:
                        wcm.append(prev_ctx_ev)
                    cmm = s.stage("tensor", [
                        (lambda e, ms=ms, tc=tc, h=h, zb=zb, kz=kz: e.matmul(
                            CTP[:, zb * 512 + ms * 65: zb * 512 + ms * 65 + 65],
                            kz[:, tc * 256 + ms * 128: tc * 256 + ms * 128 + 128],
                            TP[:, tc * 1040 + h * 65: tc * 1040 + h * 65 + 65],
                            start=(ms == 0 and tc == 0), stop=(ms == 1 and tc == TC - 1)))
                        for ms in range(2) for tc in range(TC)
                    ], waits=wcm)
                    last_cmmz[sub] = cmm
                prev_ctx_ev = s.stage("vector", [
                    (lambda e, hp=hp: e.scalar_tensor_tensor(
                        HP[:, 8192 + hp * 260: 8192 + (hp + 1) * 260]
                        .rearrange("p (z c) -> p z c", c=130),
                        HP[:, 8192 + hp * 260: 8192 + (hp + 1) * 260]
                        .rearrange("p (z c) -> p z c", c=130),
                        m_eD[:, 0:1],
                        CTP[:, 0:1024].rearrange("p (z c) -> p z c", z=2)[:, :, 0:130],
                        ALU.mult, ALU.add)),
                ], waits=[cmm, ctx_init])
            ctx_last = prev_ctx_ev
            if t < 3:
                emit_wq_pass1(22)
            elif t == 3:
                emit_wq_pass1(len(wq_items))
                emit_wq_mean()
                emit_wq_pass2(13)
            else:
                emit_wq_pass2(13)
        emit_wq_pass2(len(wq_items))
        wq_final = wq2["last"]
        s.mark("p1")

        ar_o = dma1("gpsimd", rmio.ap(), m_RM, waits=[rm_done], stream="dgo")
        import os as _os
        if _os.environ.get("MK_NOCC"):
            cc1 = dma1("gpsimd", rmo.ap(), rmio.ap(), waits=[ar_o], stream="dgo")
        else:
            cc1 = s.collective(
                lambda e: e.collective_compute(
                    "AllReduce", ALU.max, replica_groups=[list(range(NCORES))],
                    ins=[rmio.ap().opt()], outs=[rmo.ap().opt()]),
                waits=[ar_o])
        ar_i = dma1("gpsimd", m_RM, rmo.ap(), waits=[cc1, ra_last], stream="dgo")
        rr = s.stage("vector", [
            lambda e: e.reduce_max(m_red2[:, 0:1], m_RM, AX.X)], waits=[ar_i])
        sb_o = dma1("gpsimd", srow.ap(), m_red2[:, 0:1], waits=[rr], stream="dgo")
        sb_i = dma1("sync", m_srow, srow.ap().rearrange("a b -> b a"),
                    waits=[sb_o], stream="dsb", allow_slow_non_contiguous=True)
        sr = s.stage("vector", [
            lambda e: e.reduce_max(m_sca[:, 6:7], m_srow, AX.X)], waits=[sb_i])
        sbm = s.stage("tensor", [
            lambda e: e.matmul(CTP[:, 0:1], o32row, m_sca[:, 6:7],
                               start=True, stop=True)], waits=[sr, ctx_last])
        sev = s.stage("vector", [
            lambda e: e.tensor_copy(m_stab, CTP[:, 0:1]),
            lambda e: e.tensor_scalar_mul(m_mS, CTP[:, 0:1], -1.0),
        ], waits=[sbm])
        fst = s.stage("scalar", [
            lambda e: e.activation(m_f, m_Lprev, AF.Exp, bias=m_mS[:, 0:1]),
        ], waits=[sev])
        s.mark("ar")

        sxr = s.stage("vector", [
            (lambda e, kc=kc: e.reduce_sum(
                m_sx[:, kc:kc + 1],
                WP[:, RXL + kc * 4096: RXL + (kc + 1) * 4096], AX.X))
            for kc in range(EC)
        ])
        sxc = s.stage("vector", [
            lambda e: e.tensor_copy(TP[:, 8448:8456], m_sx),
        ])
        vsm = s.stage("tensor", [
            (lambda e, kc=kc, half=half: e.matmul(
                CTP[0:1, half * 512:(half + 1) * 512],
                TP[:, 8448 + kc: 8449 + kc],
                WP[:, RB + kc * 1024 + half * 512: RB + kc * 1024 + (half + 1) * 512],
                start=(kc == 0), stop=False))
            for half in range(2) for kc in range(EC)
        ] + [
            (lambda e, half=half: e.matmul(
                CTP[0:1, half * 512:(half + 1) * 512],
                k4096h, r_bv[:, half * 512:(half + 1) * 512],
                start=False, stop=True))
            for half in range(2)
        ], waits=[sxc, ctx_last])
        vse = s.stage("vector", [
            lambda e: e.tensor_copy(HP[0:1, 0:1024], CTP[0:1, 0:1024]),
        ], waits=[vsm])
        v2a = s.stage("vector", [
            (lambda e, ms=ms: e.tensor_scalar_mul(
                HP[0:1, 1024:3104].rearrange("a (h m c) -> a h m c", m=2, c=65)
                [:, :, ms:ms + 1, 0:64],
                HP[0:1, 0:1024].rearrange("a (h one d) -> a h one d", one=1, d=64),
                KEPS))
            for ms in range(2)
        ] + [
            lambda e: e.memset(
                HP[0:1, 1024:3104].rearrange("a (h m c) -> a h m c", m=2, c=65)
                [:, :, :, 64:65], KEPS * 4096.0),
        ], waits=[vse])
        vsb = s.stage("tensor", [
            (lambda e, c=c: e.matmul(
                DDP[:, c * 512:(c + 1) * 512], o32row,
                HP[0:1, 1024 + c * 512: 1024 + (c + 1) * 512],
                start=True, stop=True))
            for c in range(4)
        ] + [
            lambda e: e.matmul(CTP[:, 0:32], o32row, HP[0:1, 3072:3104],
                               start=True, stop=True),
        ], waits=[v2a])
        vsc = s.stage("vector", [
            (lambda e, c=c: e.tensor_copy(HP[:, c * 512:(c + 1) * 512],
                                          DDP[:, c * 512:(c + 1) * 512]))
            for c in range(4)
        ] + [
            lambda e: e.tensor_copy(HP[:, 2048:2080], CTP[:, 0:32]),
        ], waits=[vsb])
        cxc = s.stage("vector", [
            lambda e: e.scalar_tensor_tensor(
                CXP[:], HP[:, 8192:10272], m_f[:, 0:1], HP[:, 0:2080],
                ALU.mult, ALU.add),
        ], waits=[vsc, fst])
        ebm = s.stage("tensor", [
            (lambda e, h=h, ms=ms: e.matmul(
                DDP[0:1, h * 128: h * 128 + 65], ones16col,
                CXP[:, h * 130 + ms * 65: h * 130 + ms * 65 + 65],
                start=(ms == 0 and h % 4 == 0), stop=(ms == 1 and h % 4 == 3)))
            for h in range(H) for ms in range(2)
        ], waits=[cxc])
        ebv = s.stage("vector", [
            lambda e: e.tensor_scalar_mul(
                r_eps.rearrange("a (h c) -> a h c", c=65),
                DDP[0:1, 0:2048].rearrange("p (h c) -> p h c", h=16)[:, :, 0:65], KEPS),
        ], waits=[ebm])

        w2ab = s.dmas("sync", [
            (lambda e, kc=kc: e.dma_start(
                out=WP[:, w2t_region(kc): w2t_region(kc) + 1024],
                in_=ws2[:, kc * 128:(kc + 1) * 128], transpose=True))
            for kc in range(16)
        ], waits=[wq_final, vsm])

        s.mark("p2")
        x2st = [None] * NT
        p3a_last = None
        for t in range(NT):
            w0 = [ctx_last] if t == 0 else [p3a_last, x2st[t - 1]]
            xl = s.dmas("sync", [
                (lambda e, t=t: e.dma_start(
                    out=HP[:, 0:4096].rearrange("p (a m) -> p a m", a=TC),
                    in_=xin[t * 512:(t + 1) * 512, :].rearrange("(a p) m -> p a m", p=128)))
            ], waits=w0 + ([cxc] if t == 0 else []), stream="dxa")
            qev = emit_proj(RC, m_cbq, t,
                            lambda ec: TP[:, KN + ec * 512: KN + (ec + 1) * 512], w0)
            dgd = emit_diag(KN, qev,
                            extra_first=w0 + ([ebv] if t == 0 else []))
            prev_out_ev = None
            prev_readers = [dgd]
            last_qf = {0: None, 1: None}
            last_eqf = {0: None, 1: None}
            last_ommz = {0: None, 1: None}
            for hp in range(8):
                wm = [qev] + prev_readers
                hmm = s.stage("tensor", [
                    mmf
                    for tc in range(TC)
                    for mmf in [
                        (lambda e, hp=hp, tc=tc: e.matmul(
                            DDP[:, tc * 256: tc * 256 + 256],
                            TP[0:64, KN + hp * 512 + tc * 128: KN + hp * 512 + tc * 128 + 128],
                            PJ[0:64, :], start=(tc % 2 == 0), stop=(tc % 2 == 1),
                            tile_position=(0, 0))),
                        (lambda e, hp=hp, tc=tc: e.matmul(
                            DDP[:, 1024 + tc * 256: 1024 + tc * 256 + 256],
                            TP[64:128, KN + hp * 512 + tc * 128: KN + hp * 512 + tc * 128 + 128],
                            PJ[64:128, :], start=(tc % 2 == 0), stop=(tc % 2 == 1),
                            tile_position=(64, 0))),
                    ]
                ], waits=wm)
                nst = s.stage("vector", [
                    (lambda e, sub=sub, tc=tc, hp=hp: e.reduce_max(
                        m_nst[:, tc * 16 + 2 * hp + sub: tc * 16 + 2 * hp + sub + 1],
                        DDP[:, sub * 1024 + tc * 256: sub * 1024 + (tc + 1) * 256],
                        AX.X, negate=True))
                    for sub in range(2) for tc in range(TC)
                ], waits=[hmm])
                edq = s.stage("vector", [
                    (lambda e, hp=hp: e.scalar_tensor_tensor(
                        m_edb.rearrange("p (a h) -> p a h", a=TC)[:, :, 2 * hp:2 * hp + 2],
                        m_diag.rearrange("p (a h) -> p a h", a=TC)[:, :, 2 * hp:2 * hp + 2],
                        -0.5,
                        m_nst.rearrange("p (a h) -> p a h", a=TC)[:, :, 2 * hp:2 * hp + 2],
                        ALU.mult, ALU.add)),
                ], waits=[nst, dgd])

                prev_readers = [nst]
                ozb = lambda sub, tc: (tc // 2) * 512 + sub * 256 + (tc % 2) * 128
                ozoneP = [[CTP[:, ozb(s_, tc): ozb(s_, tc) + 65] for tc in range(4)]
                          for s_ in range(2)]
                ozcolP = [[CTP[:, ozb(s_, tc) + 64: ozb(s_, tc) + 65] for tc in range(4)]
                          for s_ in range(2)]
                oznumP = [[CTP[:, ozb(s_, tc): ozb(s_, tc) + 64] for tc in range(4)]
                          for s_ in range(2)]
                omms = []
                qzs = [TP[:, 8448:9472], HPf[:, 16384:18432]]
                qfs = []
                for sub in range(2):
                    h = 2 * hp + sub
                    qz = qzs[sub]
                    qfw = [hmm, edq]
                    if last_ommz.get(sub) is not None:
                        qfw.append(last_ommz[sub])
                    qf = s.stage("scalar", [
                        (lambda e, sub=sub, tc=tc, h=h, qz=qz: e.activation(
                            qz[:, tc * 256:(tc + 1) * 256],
                            DDP[:, sub * 1024 + tc * 256:
                                sub * 1024 + (tc + 1) * 256],
                            AF.Exp, bias=m_edb[:, tc * 16 + h: tc * 16 + h + 1]))
                        for tc in range(TC)
                    ], waits=qfw)
                    last_qf[sub] = qf
                    prev_readers.append(qf)
                    qfs.append(qf)
                for sub in range(2):
                    h = 2 * hp + sub
                    par = sub
                    qz = qzs[sub]
                    pz = PA16 if par == 0 else PB16
                    ozone = ozoneP[par]
                    tqw = [qfs[sub]]
                    if last_eqf.get(par) is not None:
                        tqw.append(last_eqf[par])
                    tqf = s.stage("tensor", [
                        (lambda e, k=k, qz=qz, pz=pz: e.transpose(
                            pz[:, k * 128:(k + 1) * 128],
                            qz[:, k * 128:(k + 1) * 128],
                            IDN[:]))
                        for k in range(8)
                    ], waits=tqw)
                    eqf = s.stage("scalar" if par == 0 else "vector", [
                        (lambda e, k=k, qz=qz, pz=pz, par=par: (
                            e.activation(qz[:, k * 128:(k + 1) * 128],
                                         pz[:, k * 128:(k + 1) * 128], AF.Copy)
                            if par == 0 else
                            e.tensor_copy(qz[:, k * 128:(k + 1) * 128],
                                          pz[:, k * 128:(k + 1) * 128])))
                        for k in range(8)
                    ], waits=[tqf])
                    last_eqf[par] = eqf
                    ommw = [eqf, qfs[0], qfs[1], cxc, ebv]
                    if sub == 0 and prev_out_ev is not None:
                        ommw.append(prev_out_ev)
                    omm = s.stage("tensor", [
                        mmf
                        for tc in range(TC)
                        for mmf in [
                            (lambda e, tc=tc, h=h, qz=qz, ozone=ozone, sub=sub: e.matmul(
                                ozone[tc],
                                qz[:, tc * 256: tc * 256 + 128],
                                CXP[:, h * 130: h * 130 + 65],
                                start=(sub == 0 and tc % 2 == 0), stop=False)),
                            (lambda e, tc=tc, h=h, qz=qz, ozone=ozone: e.matmul(
                                ozone[tc],
                                qz[:, tc * 256 + 128: tc * 256 + 256],
                                CXP[:, h * 130 + 65: h * 130 + 130],
                                start=False, stop=False)),
                            (lambda e, tc=tc, h=h, ozone=ozone, sub=sub: e.matmul(
                                ozone[tc],
                                ones16row32, r_eps[:, h * 65: h * 65 + 65],
                                start=False, stop=(sub == 1 and tc % 2 == 1))),
                        ]
                    ], waits=ommw)
                    last_ommz[par] = omm
                    omms.append(omm)
                dvs = s.stage("vector", [
                    (lambda e, sub=sub, tc=tc: e.reciprocal(
                        m_dinv8[:, sub * 4 + tc: sub * 4 + tc + 1],
                        ozcolP[sub][tc]))
                    for sub in range(2) for tc in range(TC)
                ], waits=[omms[1]])
                prev_out_ev = s.stage("scalar", [
                    (lambda e, sub=sub, tc=tc, h=2 * hp + sub: e.activation(
                        TP[:, tc * 1024 + h * 64: tc * 1024 + h * 64 + 64],
                        oznumP[sub][tc], AF.Copy,
                        scale=m_dinv8[:, sub * 4 + tc: sub * 4 + tc + 1]))
                    for sub in range(2) for tc in range(TC)
                ], waits=[dvs])
            trd = emit_tblock(
                lambda k: TP[:, (k // EC) * 1024 + (k % EC) * 128:
                             (k // EC) * 1024 + (k % EC) * 128 + 128],
                lambda k: TP[:, KN + (k % EC) * 512 + (k // EC) * 128:
                             KN + (k % EC) * 512 + (k // EC) * 128 + 128],
                TC * EC, prev_out_ev, PB16)
            x2ev = []
            for tc in range(TC):
                for half in range(2):
                    pab = PAB_A if half == 0 else PAB_B
                    wv = [trd, xl] if tc == 0 else []
                    if len(x2ev) >= 2:
                        wv.append(x2ev[-2])
                    mmo = s.stage("tensor", [
                        (lambda e, kc=kc, tc=tc, half=half, pab=pab: e.matmul(
                            pab[:, 0:512],
                            TP[:, KN + kc * 512 + tc * 128: KN + kc * 512 + tc * 128 + 128],
                            WP[:, RD + kc * 1024 + half * 512: RD + kc * 1024 + (half + 1) * 512],
                            start=(kc == 0), stop=False))
                        for kc in range(EC)
                    ] + [
                        (lambda e, half=half, pab=pab: e.matmul(
                            pab[:, 0:512], ones16row32,
                            BR[32:33, half * 512:(half + 1) * 512],
                            start=False, stop=True)),
                    ], waits=wv)
                    x2ev.append(s.stage("vector", [
                        (lambda e, tc=tc, half=half, pab=pab: e.tensor_add(
                            HP[:, 4096 + tc * 1024 + half * 512:
                               4096 + tc * 1024 + (half + 1) * 512],
                            pab[:, 0:512],
                            HP[:, tc * 1024 + half * 512: tc * 1024 + (half + 1) * 512])),
                    ], waits=[mmo]))
            p3a_last = x2ev[-1]
            x2st[t] = s.dmas("gpsimd", [
                (lambda e, t=t, tc=tc: e.dma_start(
                    out=x2s[(t * TC + tc) * 128:(t * TC + tc + 1) * 128, :],
                    in_=HP[:, 4096 + tc * 1024: 4096 + (tc + 1) * 1024]))
                for tc in range(TC)
            ], waits=[x2ev[-1]])

        s.mark("p3a")
        p3ab = s.barrier()
        w1ld = s.dmas("sync", [
            (lambda e, fc=fc: e.dma_start(
                out=WP[:, RXL + fc * 4096: RXL + (fc + 1) * 4096],
                in_=ws1[:, fc * 128:(fc + 1) * 128], transpose=True))
            for fc in range(EC)
        ], waits=p3ab)
        w2cd = s.dmas("scalar", [
            (lambda e, kc=kc: e.dma_start(
                out=WP[:, w2t_region(kc): w2t_region(kc) + 1024],
                in_=ws2[:, kc * 128:(kc + 1) * 128], transpose=True))
            for kc in range(16, 32)
        ], waits=p3ab)
        gb_last = None
        for idx, vecd in enumerate([gammad, betad]):
            for hlf in range(4):
                gl = dma1("sync", m_s1row,
                          vecd[None, hlf * 256:(hlf + 1) * 256],
                          waits=list(p3ab) + ([gb_last] if gb_last else []))
                gm = s.stage("tensor", [
                    lambda e: e.matmul(CTP[:, 0:256], o32row, m_s1row,
                                       start=True, stop=True),
                ], waits=[gl] + ([gb_last] if gb_last else []))
                gb_last = s.stage("vector", [
                    (lambda e, idx=idx, hlf=hlf: e.tensor_copy(
                        CXP[:, idx * 1024 + hlf * 256: idx * 1024 + (hlf + 1) * 256],
                        CTP[:, 0:256])),
                ], waits=[gm])

        s.mark("bnd")
        T2 = 256
        NT2 = N // T2
        TC2 = 2
        p3bw = s.barrier()
        prev_store = None
        prev_oev = None
        for t2 in range(NT2):
            w0 = list(p3bw) if t2 == 0 else [prev_store]
            xq_done = []
            x2la = s.dmas("sync", [
                (lambda e, g=t2 * TC2: e.dma_start(
                    out=TPf[:, 2048:3072],
                    in_=x2s[g * 128:(g + 1) * 128, 0:1024])),
            ], waits=list(w0) + [x2st[t2 // 2]], stream="dxa")
            x2lb = s.dmas("sync", [
                (lambda e, g=t2 * TC2 + 1: e.dma_start(
                    out=TPf[:, 3072:4096],
                    in_=x2s[g * 128:(g + 1) * 128, 0:1024])),
            ], waits=[], stream="dxa")
            s.stage("vector", [
                lambda e: e.bn_stats(m_bn2[:, 0:6], TPf[:, 2048:2560]),
                lambda e: e.bn_stats(m_bn2[:, 6:12], TPf[:, 2560:3072]),
                lambda e: e.bn_stats(m_bn2[:, 12:18], TPf[:, 3072:3584]),
                lambda e: e.bn_stats(m_bn2[:, 18:24], TPf[:, 3584:4096]),
            ], waits=[x2la, x2lb])
            ag = s.stage("vector", [
                lambda e: e.bn_aggr(m_mv2[:, 0:2], m_bn2[:, 0:12]),
                lambda e: e.bn_aggr(m_mv2[:, 2:4], m_bn2[:, 12:24]),
            ])
            rs = s.stage("scalar", [
                lambda e: e.activation(m_sd2[:, 0:1], m_mv2[:, 1:2],
                                       AF.Sqrt, bias=kLNEPS),
                lambda e: e.activation(m_sd2[:, 1:2], m_mv2[:, 3:4],
                                       AF.Sqrt, bias=kLNEPS),
            ], waits=[ag])
            s.stage("vector", [
                (lambda e, tc=tc: e.scalar_tensor_tensor(
                    HP[:, tc * 1024:(tc + 1) * 1024],
                    TPf[:, 2048 + tc * 1024: 2048 + (tc + 1) * 1024],
                    m_mv2[:, 2 * tc: 2 * tc + 1],
                    CXP[:, 0:1024], ALU.subtract, ALU.mult))
                for tc in range(TC2)
            ])
            s.stage("vector", [
                lambda e: e.reciprocal(m_rstd2[:, 0:2], m_sd2[:, 0:2]),
            ], waits=[rs])
            s.stage("vector", [
                (lambda e, tc=tc: e.scalar_tensor_tensor(
                    HP[:, 2048 + tc * 1024: 2048 + (tc + 1) * 1024],
                    HP[:, tc * 1024:(tc + 1) * 1024],
                    m_rstd2[:, tc:tc + 1],
                    CXP[:, 1024:2048], ALU.mult, ALU.add))
                for tc in range(TC2)
            ])
            s.stage("vector", [
                (lambda e, tc=tc: e.reduce_max(
                    m_rx1b[:, tc:tc + 1],
                    HP[:, 2048 + tc * 1024: 2048 + (tc + 1) * 1024],
                    AX.X, apply_absolute_value=True))
                for tc in range(TC2)
            ])
            s.stage("vector", [
                lambda e: e.tensor_scalar(m_mx1b[:, 0:2], m_rx1b[:, 0:2],
                                          1e-5, 1.0 / 127.0, ALU.max, ALU.mult),
            ])
            rcp = s.stage("vector", [
                lambda e: e.reciprocal(m_s1cb[:, 0:2], m_mx1b[:, 0:2]),
                lambda e: e.tensor_single_scalar(m_s1i[:, 0:2], m_mx1b[:, 0:2],
                                                 m_mw1d[:, 0:1], ALU.mult),
            ])
            u1 = s.stage("scalar", [
                (lambda e, tc=tc: e.activation(
                    TP[:, tc * 1024:(tc + 1) * 1024],
                    HP[:, 2048 + tc * 1024: 2048 + (tc + 1) * 1024],
                    AF.Copy, bias=1536.0, scale=m_s1cb[:, tc:tc + 1]))
                for tc in range(TC2)
            ], waits=[rcp])
            for tc in range(TC2):
                tp_ = s.stage("tensor", [
                    (lambda e, fc=i, tc=tc: e.transpose(
                        PA16[:, fc * 128:(fc + 1) * 128],
                        TP[:, tc * 1024 + fc * 128: tc * 1024 + fc * 128 + 128],
                        IDN[:]))
                    for i in range(8)
                ], waits=[u1] + ([xq_done[-1]] if xq_done else []))
                xq_done.append(s.stage("vector", [
                    (lambda e, fc=i, tc=tc: e.tensor_scalar_add(
                        TP[:, 2048 + fc * 256 + tc * 128: 2048 + fc * 256 + tc * 128 + 128],
                        PA16[:, fc * 128:(fc + 1) * 128], -1536.0))
                    for i in range(8)
                ], waits=[tp_]))
            s1o = s.dmas("gpsimd", [
                lambda e: e.dma_start(out=rmio[:, 0:2], in_=m_s1i[:, 0:2]),
            ], waits=[xq_done[-1]], stream="dgo")
            s1i_ = s.dmas("sync", [
                lambda e: e.dma_start(out=m_s1row[:, 0:256].rearrange("a (c p) -> a c p", p=128),
                                      in_=rmio[:, 0:2].rearrange("p c -> c p")[None],
                                      allow_slow_non_contiguous=True),
            ], waits=[s1o], stream="dsb")
            sbm1 = s.stage("tensor", [
                (lambda e, tc=tc: e.matmul(
                    CTP[:, tc * 128:(tc + 1) * 128], o32row,
                    m_s1row[:, tc * 128:(tc + 1) * 128],
                    start=(tc == 0), stop=(tc == TC2 - 1)))
                for tc in range(TC2)
            ], waits=[s1i_])
            sbe1 = s.stage("vector", [
                lambda e: e.tensor_copy(TPf[:, 0:256], CTP[:, 0:256]),
            ], waits=[sbm1])
            sc1_hist = []
            m1_done = None
            for r in range(4):
                if r % 2 == 0:
                    zones = [DDP[:, 0:512], DDP[:, 512:1024],
                             DDP[:, 1024:1536], DDP[:, 1536:2048]]
                else:
                    zones = [PAB_A[:, 0:512], PAB_B[:, 0:512],
                             CTP[:, 0:512], CTP[:, 512:1024]]
                if r == 0:
                    wv = [xq_done[-1], w1ld]
                elif r == 1:
                    wv = [sbe1]
                else:
                    wv = [sc1_hist[r - 2]]
                mmh = s.stage("tensor", [
                    (lambda e, fc=fc, hc=r * 8 + j, z=zones[j // 2],
                     o=(j % 2) * 256,
                     st=(j % 2 == 0 and fc == 0),
                     sp=(j % 2 == 1 and fc == EC - 1): e.matmul(
                        z[:, o:o + 256],
                        WP[:, RXL + fc * 4096 + hc * 128: RXL + fc * 4096 + hc * 128 + 128],
                        TP[:, 2048 + fc * 256: 2048 + (fc + 1) * 256],
                        start=st, stop=sp))
                    for j in range(8)
                    for fc in range(EC)
                ], waits=wv)
                sc1 = s.stage("vector", [
                    (lambda e, hc=r * 8 + j, z=zones[j // 2], o=(j % 2) * 256:
                     e.tensor_mul(
                        HP[:, hc * 256:(hc + 1) * 256], z[:, o:o + 256],
                        TPf[:, 0:256]))
                    for j in range(8)
                ], waits=[mmh])
                sc1_hist.append(sc1)
                gl = s.stage("scalar", [
                    (lambda e, hc=r * 8 + j: e.activation(
                        HP[:, hc * 256:(hc + 1) * 256],
                        HP[:, hc * 256:(hc + 1) * 256],
                        AF.Gelu, bias=m_b1h[:, hc:hc + 1]))
                    for j in range(8)
                ], waits=[sc1])
                if r == 0:
                    m1_done = s.stage("vector", [
                        lambda e: e.tensor_copy(HP[:, 8192:10240], HP[:, 0:2048]),
                    ], waits=[gl])
                else:
                    m1_done = s.stage("vector", [
                        (lambda e, r=r: e.tensor_max(
                            HP[:, 8192:10240], HP[:, 8192:10240],
                            HP[:, r * 2048:(r + 1) * 2048])),
                    ], waits=[gl, m1_done])
            m1_done = s.stage("vector", [
                lambda e: e.tensor_max(HP[:, 8192:9216], HP[:, 8192:9216],
                                       HP[:, 9216:10240]),
            ], waits=[m1_done])
            m1_done = s.stage("vector", [
                lambda e: e.tensor_max(HP[:, 8192:8704], HP[:, 8192:8704],
                                       HP[:, 8704:9216]),
            ])
            m1_done = s.stage("vector", [
                lambda e: e.tensor_max(HP[:, 8192:8448], HP[:, 8192:8448],
                                       HP[:, 8448:8704]),
            ])
            m132 = s.stage("tensor", [
                (lambda e, tc=tc: e.transpose(
                    CTP[:, tc * 128:(tc + 1) * 128],
                    HP[:, 8192 + tc * 128: 8192 + (tc + 1) * 128], IDN32[:]))
                for tc in range(TC2)
            ], waits=[m1_done])
            s.stage("vector", [
                (lambda e, tc=tc: e.reduce_max(
                    m_rx2[:, tc:tc + 1], CTP[:, tc * 128:(tc + 1) * 128], AX.X))
                for tc in range(TC2)
            ], waits=[m132])
            s.stage("vector", [
                lambda e: e.tensor_scalar(m_mx2[:, 0:2], m_rx2[:, 0:2],
                                          1e-5, 1.0 / 127.0, ALU.max, ALU.mult),
            ])
            s.stage("vector", [
                lambda e: e.reciprocal(m_s2c[:, 0:2], m_mx2[:, 0:2]),
                lambda e: e.tensor_single_scalar(m_s2i[:, 0:2], m_mx2[:, 0:2],
                                                 m_mw2d[:, 0:1], ALU.mult),
            ])
            rx2 = s.stage("vector", [
                lambda e: e.tensor_single_scalar(m_f2[:, 0:2], m_s2c[:, 0:2],
                                                 m_sw2b[:, 0:1], ALU.mult),
            ])
            s2o = s.dmas("gpsimd", [
                lambda e: e.dma_start(out=rmio[:, 0:2], in_=m_s2c[:, 0:2]),
                lambda e: e.dma_start(out=rmio[:, 2:4], in_=m_f2[:, 0:2]),
            ], waits=[rx2], stream="dgo")
            s2i_ = s.dmas("sync", [
                lambda e: e.dma_start(out=m_s1row[:, 0:256].rearrange("a (c p) -> a c p", p=128),
                                      in_=rmio[:, 0:2].rearrange("p c -> c p")[None],
                                      allow_slow_non_contiguous=True),
            ], waits=[s2o], stream="dsb")
            f2i_ = s.dmas("gpsimd", [
                lambda e: e.dma_start(out=r_f2[:, 0:256].rearrange("a (c p) -> a c p", p=128),
                                      in_=rmio[:, 2:4].rearrange("p c -> c p")[None],
                                      allow_slow_non_contiguous=True),
            ], waits=[s2o], stream="dgo")
            sbm2 = s.stage("tensor", [
                (lambda e, tc=tc: e.matmul(
                    CTP[:, tc * 128:(tc + 1) * 128], o32row,
                    m_s1row[:, tc * 128:(tc + 1) * 128],
                    start=(tc == 0), stop=(tc == TC2 - 1)))
                for tc in range(TC2)
            ], waits=[s2i_, rx2])
            sbe2 = s.stage("vector", [
                lambda e: e.tensor_copy(TPf[:, 256:512], CTP[:, 0:256]),
            ], waits=[sbm2])
            hq_prev = None
            for k0 in range(0, HC, 8):
                wq_ = [sbe2]
                if hq_prev is not None:
                    wq_.append(hq_prev)
                mul = s.stage("vector", [
                    (lambda e, kc=k0 + i: e.tensor_mul(
                        HP[:, kc * 256:(kc + 1) * 256],
                        HP[:, kc * 256:(kc + 1) * 256], TPf[:, 256:512]))
                    for i in range(8)
                ], waits=wq_)
                def _adds(kcs):
                    return [
                        (lambda e, kc=kc: e.tensor_scalar_add(
                            HPf[:, kc * 256:(kc + 1) * 256],
                            HP[:, kc * 256:(kc + 1) * 256], 1536.0))
                        for kc in kcs
                    ]
                if k0 == 0:
                    s.stage("vector", [
                        lambda e: e.tensor_scalar_add(TP[:, 0:256], HP[:, 0:256], 1536.0),
                    ], waits=[mul])
                    s.stage("vector", _adds([1]) + [
                        lambda e: e.tensor_scalar_add(TP[:, 0:256], TP[:, 0:256], -1536.0),
                    ])
                    s.stage("vector", _adds([2, 3]))
                    hq_prev = s.stage("vector", _adds([4, 5, 6, 7]))
                else:
                    hq_prev = s.stage("vector", _adds(range(k0, k0 + 8)), waits=[mul])
            hq2 = s.stage("vector", [
                lambda e: e.tensor_scalar_add(HPf[:, 256:4096], HPf[:, 256:4096], -1536.0),
                lambda e: e.tensor_scalar_add(HPf[:, 4096:8192], HPf[:, 4096:8192], -1536.0),
            ], waits=[hq_prev])
            last_ev1 = None
            for tc in range(TC2):
                g = t2 * TC2 + tc
                x2l2 = s.dmas("sync", [
                    (lambda e, g=g: e.dma_start(
                        out=TPf[:, 2048:3072],
                        in_=x2s[g * 128:(g + 1) * 128, 0:1024])),
                ], waits=[hq2] if tc == 0 else [prev_store], stream="dxa")
                oev = []
                for half in range(2):
                    pab = PAB_A if half == 0 else PAB_B
                    wv = [hq2, w2ab, w2cd, f2i_, b2_cast] if tc == 0 else [hq2]
                    if last_ev1 is not None:
                        wv.append(last_ev1)
                    mmo = s.stage("tensor", [
                        (lambda e, kc=kc, tc=tc, half=half, pab=pab: e.matmul(
                            pab[:, 0:512],
                            (TP[:, tc * 128: tc * 128 + 128] if kc == 0 else
                             HPf[:, kc * 256 + tc * 128: kc * 256 + tc * 128 + 128]),
                            WP[:, w2t_region(kc) + half * 512: w2t_region(kc) + (half + 1) * 512],
                            start=(kc == 0), stop=False))
                        for kc in range(HC)
                    ] + [
                        (lambda e, tc=tc, half=half, pab=pab: e.matmul(
                            pab[:, 0:512],
                            r_f2[:, tc * 128:(tc + 1) * 128],
                            r_b2[:, half * 512:(half + 1) * 512],
                            start=False, stop=True, tile_position=(64, 0))),
                    ], waits=wv)
                    ev1w = [mmo]
                    if oev:
                        ev1w.append(oev[-1])
                    elif prev_oev is not None:
                        ev1w.append(prev_oev)
                    ev1 = s.stage("scalar", [
                        (lambda e, tc=tc, half=half, pab=pab: e.activation(
                            TPf[:, 1024 + half * 512: 1024 + (half + 1) * 512],
                            pab[:, 0:512], AF.Copy, scale=m_s2i[:, tc:tc + 1])),
                    ], waits=ev1w)
                    last_ev1 = ev1
                    oev.append(s.stage("vector", [
                        (lambda e, half=half: e.tensor_add(
                            TPf[:, 3072 + half * 512: 3072 + (half + 1) * 512],
                            TPf[:, 1024 + half * 512: 1024 + (half + 1) * 512],
                            TPf[:, 2048 + half * 512: 2048 + (half + 1) * 512])),
                    ], waits=[ev1, x2l2]))
                prev_oev = oev[-1]
                prev_store = s.dmas("gpsimd", [
                    (lambda e, g=g: e.dma_start(
                        out=outd[g * 128:(g + 1) * 128, :],
                        in_=TPf[:, 3072:4096]))
                ], waits=[oev[0], oev[1]])

        s.mark("all")
        import os
        trunc = os.environ.get("MK_TRUNCATE", "")
        if trunc:
            s.truncate(trunc)
        fin = s.barrier()
        s.prog["gpsimd"].append(("fin", fin, [], None))

        with nc.Block() as block:
            def mk(engname):
                def body(eng):
                    hw = {}
                    for kind, waits, fns, inc in s.prog[engname]:
                        for (nm, val) in waits:
                            if hw.get(nm, -1) >= val:
                                continue
                            hw[nm] = val
                            eng.wait_ge(sems[nm], val)
                        if kind == "fin":
                            continue
                        if kind == "ccstage":
                            last = None
                            for fn in fns:
                                last = fn(eng)
                            nm, v = inc
                            last.then_inc(sems[nm], 1)
                        elif kind == "pestage":
                            last = None
                            for fn in fns:
                                last = fn(eng)
                            nm, v = inc
                            last.then_inc(sems[nm], 1)
                        elif kind == "stage":
                            for fn in fns:
                                fn(eng)
                            nm, v = inc
                            eng.drain().then_inc(sems[nm], 1)
                        else:
                            nm, v = inc
                            for fn in fns:
                                fn(eng).then_inc(sems[nm], 16)
                return body
            block.sync(mk("sync"))
            block.vector(mk("vector"))
            block.scalar(mk("scalar"))
            block.tensor(mk("tensor"))
            block.gpsimd(mk("gpsimd"))
    return nc


_NC_CACHE = {}


def kernel(**inputs):
    if "nc" not in _NC_CACHE:
        _NC_CACHE["nc"] = build_program()
    nc = _NC_CACHE["nc"]
    x = np.ascontiguousarray(np.asarray(inputs["L0_attn_x"], dtype=np.float32))
    f32 = lambda k: np.ascontiguousarray(np.asarray(inputs[k], dtype=np.float32))
    shared = {
        "Wq": f32("L0_attn_Wq"), "Wk": f32("L0_attn_Wk"),
        "Wv": f32("L0_attn_Wv"), "Wo": f32("L0_attn_Wo"),
        "bq": f32("L0_attn_bq"), "bk": f32("L0_attn_bk"),
        "bv": f32("L0_attn_bv"), "bo": f32("L0_attn_bo"),
        "proj": f32("L0_attn_proj"),
        "w1": f32("L0_ffn_w1"), "b1": f32("L0_ffn_b1"),
        "w2": f32("L0_ffn_w2"), "b2": f32("L0_ffn_b2"),
        "gamma": f32("L0_ffn_gamma"), "beta": f32("L0_ffn_beta"),
    }
    in_maps = [dict(shared, x=x[c]) for c in range(NCORES)]
    res = run_bass_kernel_spmd(nc, in_maps, core_ids=list(range(NCORES)))
    return np.stack([np.asarray(res.results[c]["out"]) for c in range(NCORES)],
                    axis=0)


# version 12
# speedup vs baseline: 1.0872x; 1.0059x over previous
import contextlib
import numpy as np
import concourse.bass as bass
import concourse.mybir as mybir
from concourse.bass_utils import run_bass_kernel_spmd

F32 = mybir.dt.float32
F16 = mybir.dt.float16
AF = mybir.ActivationFunctionType
ALU = mybir.AluOpType
AX = mybir.AxisListType

NCORES = 8
N = 4096
E = 1024
H = 16
M = 256
HID = 4096
T = 512
NT = N // T
TC = T // 128
EC = E // 128
HC = HID // 128
CSTAB = 20.0
KEPS = 1e-4
LNEPS = 1e-5
CQ = float(64.0 ** -0.25)

ENGINES = ("sync", "vector", "scalar", "tensor", "gpsimd")


class Sched:
    DSTREAMS = ("dxa", "dwt", "dwq", "dsb", "dst", "dgo", "dpr", "dscalar")

    def __init__(self):
        self.prog = {e: [] for e in ENGINES}
        self.cnt = {e: 0 for e in ENGINES}
        self.dma = {k: 0 for k in self.DSTREAMS}
        self.cc = 0

    def _rw(self, waits):
        out = []
        for w in waits:
            if not w:
                continue
            nm, v = w
            if nm in self.dma:
                v = self.dma[nm]
                if v == 0:
                    continue
            out.append((nm, v))
        return out

    def stage(self, eng, fns, waits=()):
        w = self._rw(waits)
        if self.cnt[eng] > 0:
            w.append((eng, self.cnt[eng]))
        self.cnt[eng] += 1
        kind = "pestage"
        self.prog[eng].append((kind, w, fns, (eng, self.cnt[eng])))
        return (eng, self.cnt[eng])

    def dmas(self, eng, fns, waits=(), stream=None):
        nm = stream or ("dscalar" if eng == "scalar" else
                        ("dst" if eng == "gpsimd" else "dwt"))
        w = self._rw(waits)
        self.dma[nm] += 16 * len(fns)
        self.prog[eng].append(("dma", w, fns, (nm, self.dma[nm])))
        return (nm, self.dma[nm])

    def collective(self, fn, waits=()):
        self.cc += 1
        self.prog["gpsimd"].append(("ccstage", self._rw(waits), [fn],
                                    ("cc", self.cc)))
        return ("cc", self.cc)

    def mark(self, name):
        self.marks = getattr(self, "marks", {})
        self.marks[name] = {e: len(self.prog[e]) for e in ENGINES}

    def truncate(self, name):
        for e in ENGINES:
            self.prog[e] = self.prog[e][: self.marks[name][e]]
        self.cnt = {e: 0 for e in ENGINES}
        self.dma = {k: 0 for k in self.DSTREAMS}
        self.cc = 0
        for e in ENGINES:
            for kind, waits, fns, inc in self.prog[e]:
                if inc is None:
                    continue
                nm, v = inc
                if nm in self.dma:
                    self.dma[nm] = max(self.dma[nm], v)
                elif nm == "cc":
                    self.cc = max(self.cc, v)
                else:
                    self.cnt[nm] = max(self.cnt[nm], v)

    def barrier(self):
        w = [(e, self.cnt[e]) for e in ENGINES if self.cnt[e] > 0]
        w += [(k, v) for k, v in self.dma.items() if v > 0]
        if self.cc:
            w.append(("cc", self.cc))
        return w


def build_program():
    nc = bass.Bass("TRN2", target_bir_lowering=False, debug=False)

    xin = nc.dram_tensor("x", [N, E], F32, kind="ExternalInput").ap()
    Wq = nc.dram_tensor("Wq", [E, E], F32, kind="ExternalInput").ap()
    Wk = nc.dram_tensor("Wk", [E, E], F32, kind="ExternalInput").ap()
    Wv = nc.dram_tensor("Wv", [E, E], F32, kind="ExternalInput").ap()
    Wo = nc.dram_tensor("Wo", [E, E], F32, kind="ExternalInput").ap()
    bqd = nc.dram_tensor("bq", [E], F32, kind="ExternalInput").ap()
    bkd = nc.dram_tensor("bk", [E], F32, kind="ExternalInput").ap()
    bvd = nc.dram_tensor("bv", [E], F32, kind="ExternalInput").ap()
    bod = nc.dram_tensor("bo", [E], F32, kind="ExternalInput").ap()
    projd = nc.dram_tensor("proj", [M, 64], F32, kind="ExternalInput").ap()
    w1d = nc.dram_tensor("w1", [HID, E], F32, kind="ExternalInput").ap()
    b1d = nc.dram_tensor("b1", [HID], F32, kind="ExternalInput").ap()
    w2d = nc.dram_tensor("w2", [E, HID], F32, kind="ExternalInput").ap()
    b2d = nc.dram_tensor("b2", [E], F32, kind="ExternalInput").ap()
    gammad = nc.dram_tensor("gamma", [E], F32, kind="ExternalInput").ap()
    betad = nc.dram_tensor("beta", [E], F32, kind="ExternalInput").ap()
    outd = nc.dram_tensor("out", [N, E], F32, kind="ExternalOutput").ap()

    x2s = nc.dram_tensor("x2s", [N, E], F32)
    wsw = nc.dram_tensor("wsw", [4 * E, E], F16)
    wsp = nc.dram_tensor("wsp", [M, 128], F16)
    ws1 = nc.dram_tensor("ws1", [HID, E], F16)
    ws2 = nc.dram_tensor("ws2", [E, HID], F16)
    rmio = nc.dram_tensor("rmio", [128, 16], F32)
    rmo = nc.dram_tensor("rmo", [128, 16], F32)
    srow = nc.dram_tensor("srow", [128, 1], F32)

    identd = nc.inline_tensor(np.eye(128, dtype=np.float16), "identc")
    ident32d = nc.inline_tensor(np.eye(128, dtype=np.float32), "ident32c")
    ind_np = np.zeros((128, 2), dtype=np.float16)
    ind_np[0:64, 0] = 1.0
    ind_np[64:128, 1] = 1.0
    indd = nc.inline_tensor(ind_np, "indc")

    s = Sched()
    es = contextlib.ExitStack()
    with es:
        sbuf = lambda n_, sh, dt_: es.enter_context(nc.sbuf_tensor(n_, sh, dt_))
        WP = sbuf("WP", [128, 65536], F16)
        HP = sbuf("HP", [128, 12320], F32)
        TP = sbuf("TP", [128, 9472], F16)
        MS = sbuf("MS", [128, 872], F32)
        CXP = sbuf("CXP", [128, 2080], F16)
        BR = sbuf("BR", [128, 2193], F16)
        IDN = sbuf("IDN", [128, 128], F16)
        IDN32 = sbuf("IDN32", [128, 128], F32)
        IND = sbuf("IND", [128, 2], F16)
        PJ = sbuf("PJ", [128, 256], F16)
        KC = sbuf("KC", [128, 8], F32)

        PAB_A = es.enter_context(nc.psum_tensor("PAB_A", [128, 512], F32))
        PAB_B = es.enter_context(nc.psum_tensor("PAB_B", [128, 512], F32))
        DDP = es.enter_context(nc.psum_tensor("DDP", [128, 2048], F32))
        CTP = es.enter_context(nc.psum_tensor("CTP", [128, 1024], F32))
        PA16 = PAB_A.bitcast(F16)
        PB16 = PAB_B.bitcast(F16)

        sems = {}
        for nm in ("sync", "vector", "scalar", "tensor", "gpsimd",
                   "dxa", "dwt", "dwq", "dsb", "dst", "dgo", "dpr", "dscalar", "cc"):
            sems[nm] = es.enter_context(nc.semaphore(nm))

        TPf = TP.bitcast(F32)
        HPf = HP.bitcast(F16)

        RA, RB, RXL, RC, RD = 0, 8192, 16384, 49152, 57344
        KN = 4352

        def w2t_region(kc):
            return [RA, RB, RC, RD][kc // 8] + (kc % 8) * 1024

        m_bn = MS[:, 0:12]
        m_mv = MS[:, 16:24]
        m_sd = MS[:, 24:28]
        m_rstd = MS[:, 28:32]
        m_negm = MS[:, 32:36]
        m_cbk = MS[:, 36:44]
        m_cbq = MS[:, 44:52]
        m_b1h = MS[:, 52:84]
        m_RM = MS[:, 84:100]
        m_diag = MS[:, 100:164]
        m_asum = MS[:, 800:864]
        m_nst = MS[:, 164:228]
        m_edb = MS[:, 228:292]
        m_stab = MS[:, 292:293]
        m_cstab = MS[:, 293:294]
        m_red2 = MS[:, 296:298]
        m_red4 = MS[:, 294:296]
        m_dinv = MS[:, 300:304]
        m_wred = MS[:, 304:306]
        m_sca = MS[0:1, 306:316]
        m_rx1 = MS[:, 320:321]
        m_mx1 = MS[:, 321:322]
        m_s1c = MS[:, 322:323]
        m_s1i = MS[:, 324:328]
        m_rx2 = MS[:, 328:332]
        m_mx2 = MS[:, 332:336]
        m_s2c = MS[:, 336:340]
        m_s2i = MS[:, 340:344]
        m_f2 = MS[:, 344:348]
        m_sw1b = MS[:, 348:349]
        m_sw2b = MS[:, 349:350]
        m_mw1db = MS[:, 350:351]
        m_mw2db = MS[:, 351:352]
        o32row = MS[0:1, 416:544]
        m_bn2 = MS[:, 352:376]
        m_mv2 = MS[:, 376:380]
        m_rstd2 = MS[:, 380:382]
        m_sd2 = MS[:, 390:392]
        m_rx1b = MS[:, 382:384]
        m_mx1b = MS[:, 384:386]
        m_s1cb = MS[:, 386:388]
        m_mw1d = MS[:, 392:393]
        m_mw2d = MS[:, 393:394]
        m_L1col = MS[:, 394:395]
        m_L1 = MS[0:1, 395:396]
        m_cLt = MS[:, 396:397]
        m_mLt = MS[:, 397:398]
        m_eD = MS[:, 398:399]
        m_Lprev = MS[:, 399:400]
        m_mS = MS[:, 400:401]
        m_f = MS[:, 401:402]
        m_sx = MS[:, 402:410]
        m_dinv8 = MS[:, 864:872]
        m_srow = MS[0:1, 544:672]
        m_s1row = MS[0:1, 544:800]

        r_bv = BR[0:1, 0:1024]
        ones16row = BR[0:1, 1024:1152]
        r_bo = BR[32:33, 0:1024]
        ones16row32 = BR[32:33, 1024:1152]
        r_f2 = BR[64:65, 0:512]
        r_b2 = BR[64:65, 512:1536]
        r_eps = BR[32:33, 1152:2192]
        ones16col = BR[:, 2192:2193]
        k4096h = BR[0:1, 1536:1537]

        k1536 = KC[:, 0:1]
        kmC = KC[:, 1:2]
        kpC = KC[:, 2:3]
        kCQ = KC[:, 3:4]
        kmhalf = KC[:, 4:5]
        kones32 = KC[:, 5:6]
        kLNEPS = KC[:, 6:7]

        def dma1(eng, outap, inap, waits=(), stream=None, **kw):
            return s.dmas(eng, [lambda e, o=outap, i=inap, kw=kw:
                                e.dma_start(out=o, in_=i, **kw)], waits=waits,
                          stream=stream)

        s.dmas("sync", [
            lambda e: e.dma_start(out=IDN[:], in_=identd.ap()),
            lambda e: e.dma_start(out=IDN32[:], in_=ident32d.ap()),
            lambda e: e.dma_start(out=IND[:], in_=indd.ap()),
        ])
        st_ones = s.stage("vector", [
            lambda e: e.memset(KC[:, 0:1], 1536.0),
            lambda e: e.memset(KC[:, 1:2], -CSTAB),
            lambda e: e.memset(KC[:, 2:3], CSTAB),
            lambda e: e.memset(KC[:, 3:4], CQ),
            lambda e: e.memset(KC[:, 4:5], -0.5),
            lambda e: e.memset(KC[:, 5:6], 1.0),
            lambda e: e.memset(KC[:, 6:7], LNEPS),
            lambda e: e.memset(o32row, 1.0),
            lambda e: e.memset(ones16row, 1.0),
            lambda e: e.memset(ones16row32, 1.0),
            lambda e: e.memset(ones16col, 1.0),
            lambda e: e.memset(m_RM, -3.0e38),
            lambda e: e.memset(KC[:, 7:8], 4096.0),
            lambda e: e.memset(k4096h, 4096.0),
            lambda e: e.memset(m_Lprev, -1.0e30),
        ])

        row_ld = s.dmas("gpsimd", [
            lambda e: e.dma_start(out=r_bv, in_=bvd[None, :]),
            lambda e: e.dma_start(out=r_bo, in_=bod[None, :]),
        ], stream="dpr")
        wlist = [(Wk, RA), (Wv, RB), (Wq, RC), (Wo, RD)]
        cast_done = []
        for wi, (wap, dst) in enumerate(wlist):
            cast_done.append(dma1("gpsimd", wsw[wi * E:(wi + 1) * E, :], wap, stream="dpr"))
        proj_cast = s.dmas("gpsimd", [
            lambda e: e.dma_start(out=wsp[:, 0:64], in_=projd),
            lambda e: e.dma_start(out=wsp[:, 64:128], in_=projd),
        ], stream="dpr")
        b2_cast = dma1("gpsimd", r_b2, b2d[None, :], stream="dpr")
        wt_loads = []
        for wi, (wap, dst) in enumerate(wlist):
            fns = []
            for kc in range(EC):
                fns.append(lambda e, wi=wi, kc=kc, dst=dst: e.dma_start(
                    out=WP[:, dst + kc * 1024: dst + (kc + 1) * 1024],
                    in_=wsw[wi * E:(wi + 1) * E, kc * 128:(kc + 1) * 128],
                    transpose=True))
            wt_loads.append(s.dmas("sync", fns, waits=[cast_done[wi]]))
        pj_load = s.dmas("sync", [
            lambda e: e.dma_start(out=PJ[:, :], in_=wsp[:, :], transpose=True),
        ], waits=[proj_cast])

        bias_ld = s.dmas("sync", [
            lambda e: e.dma_start(out=m_cbk, in_=bkd.rearrange("(c p) -> p c", p=128),
                                  allow_slow_non_contiguous=True),
            lambda e: e.dma_start(out=m_cbq, in_=bqd.rearrange("(c p) -> p c", p=128),
                                  allow_slow_non_contiguous=True),
            lambda e: e.dma_start(out=m_b1h, in_=b1d.rearrange("(c p) -> p c", p=128),
                                  allow_slow_non_contiguous=True),
        ])
        sc_bias = s.stage("vector", [
            lambda e: e.tensor_scalar_mul(m_cbk, m_cbk, CQ),
            lambda e: e.tensor_scalar_mul(m_cbq, m_cbq, CQ),
        ], waits=[bias_ld])

        def emit_ln(src_of_tc, dst_of_tc, wait_load, dst_w=()):
            for tc in range(TC):
                s.stage("vector", [
                    (lambda e, tc=tc: e.bn_stats(m_bn[:, 0:6], src_of_tc(tc)[:, 0:512])),
                    (lambda e, tc=tc: e.bn_stats(m_bn[:, 6:12], src_of_tc(tc)[:, 512:1024])),
                ], waits=[wait_load] if tc == 0 else [])
                s.stage("vector", [
                    (lambda e, tc=tc: e.bn_aggr(m_mv[:, 2 * tc:2 * tc + 2], m_bn[:, 0:12])),
                ])
            veps = s.stage("vector", [
                lambda e: e.tensor_scalar_add(m_sd, m_mv[:, 1:8:2], LNEPS)])
            ssq = s.stage("scalar", [
                lambda e: e.activation(m_sd, m_sd, AF.Sqrt)], waits=[veps])
            s.stage("vector", [
                lambda e: e.reciprocal(m_rstd, m_sd)], waits=[ssq])
            vnm = s.stage("vector", [
                lambda e: e.scalar_tensor_tensor(m_negm, m_mv[:, 0:8:2], -1.0,
                                                 m_rstd, ALU.mult, ALU.mult)])
            last = None
            for tc in range(TC):
                last = s.stage("scalar", [
                    (lambda e, tc=tc: e.activation(
                        dst_of_tc(tc), src_of_tc(tc), AF.Identity,
                        bias=m_negm[:, tc:tc + 1], scale=m_rstd[:, tc:tc + 1])),
                ], waits=[vnm] + list(dst_w) if tc == 0 else [])
            return last

        def emit_tblock(src_of_k, dst_of_k, nk, wait_src, ps16, extra_w=()):
            last_ev = None
            for g0 in range(0, nk, 8):
                gn = min(8, nk - g0)
                w = [wait_src] + list(extra_w)
                if last_ev is not None:
                    w.append(last_ev)
                tp_ = s.stage("tensor", [
                    (lambda e, k=g0 + i, sl=i: e.transpose(
                        ps16[:, sl * 128:(sl + 1) * 128], src_of_k(k), IDN[:]))
                    for i in range(gn)
                ], waits=w)
                last_ev = s.stage("vector", [
                    (lambda e, k=g0 + i, sl=i: e.tensor_copy(
                        dst_of_k(k), ps16[:, sl * 128:(sl + 1) * 128]))
                    for i in range(gn)
                ], waits=[tp_])
            return last_ev

        def emit_proj(wreg, cb_col, t, dst_of_ec, waits):
            evs = []
            for ec in range(EC):
                pab = PAB_A if ec % 2 == 0 else PAB_B
                w = list(waits) if ec <= 1 else []
                if len(evs) >= 2:
                    w.append(evs[-2])
                mm = s.stage("tensor", [
                    (lambda e, kc=kc, ec=ec, pab=pab: e.matmul(
                        pab[:, 0:512],
                        WP[:, wreg + kc * 1024 + ec * 128: wreg + kc * 1024 + ec * 128 + 128],
                        WP[:, RXL + kc * 4096 + t * 512: RXL + kc * 4096 + (t + 1) * 512],
                        start=(kc == 0), stop=(kc == EC - 1)))
                    for kc in range(EC)
                ], waits=w)
                evs.append(s.stage("scalar", [
                    (lambda e, ec=ec, pab=pab: e.activation(
                        dst_of_ec(ec), pab[:, 0:512], AF.Identity,
                        bias=cb_col[:, ec:ec + 1], scale=kCQ)),
                ], waits=[mm]))
            return evs[-1]

        def emit_diag(src_off, wait_kn, extra_first=()):
            dmm = []
            for ec in range(EC):
                p = ec % 2
                wsq = [wait_kn] if ec == 0 else []
                if len(dmm) >= 2:
                    wsq.append(dmm[ec - 2])
                sq_ = s.stage("scalar", [
                    (lambda e, ec=ec, p=p: e.activation(
                        TP[:, 8448 + p * 512: 8448 + (p + 1) * 512],
                        TP[:, src_off + ec * 512: src_off + (ec + 1) * 512],
                        AF.Square)),
                ], waits=wsq)
                wm = [sq_]
                if ec == 0:
                    wm += list(extra_first)
                dmm.append(s.stage("tensor", [
                    (lambda e, ec=ec, tc=tc, p=p: e.matmul(
                        DDP[:, tc * 16 + 2 * ec: tc * 16 + 2 * ec + 2],
                        TP[:, 8448 + p * 512 + tc * 128: 8448 + p * 512 + tc * 128 + 128],
                        IND[:], start=(ec == 0 and tc == 0),
                        stop=(ec == EC - 1 and tc == TC - 1)))
                    for tc in range(TC)
                ], waits=wm))
            return s.stage("vector", [
                lambda e: e.tensor_copy(m_diag, DDP[:, 0:64])], waits=[dmm[-1]])

        wq_items = ([("w1", ch, None) for ch in range(HC)] +
                    [("w2", ch, ss) for ch in range(EC) for ss in range(4)])
        wq1 = {"i": 0, "last": None}
        wq2 = {"i": 0, "last": None, "mean": None}
        wq_stg = HP[:, 4096:5120]
        wq_ta = HP[:, 5120:6144]
        wq_tb = HP[:, 6144:7168]
        wq_o16 = HPf[:, 14336:15360]

        def wq_src(it):
            wname, ch, ss = it
            if wname == "w1":
                return w1d[ch * 128:(ch + 1) * 128, :], m_asum[:, ch:ch + 1], m_sw1b
            return (w2d[ch * 128:(ch + 1) * 128, ss * 1024:(ss + 1) * 1024],
                    m_asum[:, 32 + ch * 4 + ss: 33 + ch * 4 + ss], m_sw2b)

        def wq_dst(it):
            wname, ch, ss = it
            if wname == "w1":
                return ws1[ch * 128:(ch + 1) * 128, :]
            return ws2[ch * 128:(ch + 1) * 128, ss * 1024:(ss + 1) * 1024]

        def emit_wq_pass1(n):
            for _ in range(n):
                i = wq1["i"]
                if i >= len(wq_items):
                    return
                wq1["i"] += 1
                src_, acol, _ = wq_src(wq_items[i])
                w = [wq1["last"]] if wq1["last"] else []
                ld = dma1("sync", wq_stg, src_, waits=w, stream="dwq")
                wq1["last"] = s.stage("scalar", [
                    (lambda e, acol=acol: e.activation(
                        wq_stg, wq_stg, AF.Abs, accum_out=acol)),
                ], waits=[ld])

        def emit_wq_mean():
            red = s.stage("vector", [
                lambda e: e.reduce_sum(m_wred[:, 0:1], m_asum[:, 0:32], AX.X),
                lambda e: e.reduce_sum(m_wred[:, 1:2], m_asum[:, 32:64], AX.X),
            ], waits=[wq1["last"]])
            pm = s.stage("tensor", [
                lambda e: e.matmul(CTP[0:1, 0:1], kones32, m_wred[:, 0:1],
                                   start=True, stop=False),
                lambda e: e.matmul(CTP[0:1, 1:2], kones32, m_wred[:, 1:2],
                                   start=False, stop=True),
            ], waits=[red])
            s.stage("vector", [
                lambda e: e.tensor_scalar(m_sca[:, 0:2], CTP[0:1, 0:2],
                                          1.0 / (HID * E), 1e-5, ALU.mult, ALU.max),
            ], waits=[pm])
            sc = s.stage("vector", [
                lambda e: e.reciprocal(m_sca[:, 2:4], m_sca[:, 0:2]),
                lambda e: e.tensor_scalar_mul(m_sca[:, 4:6], m_sca[:, 0:2], 1.0 / 127.0),
            ])
            bc = s.stage("tensor", [
                lambda e: e.matmul(CTP[:, 0:1], o32row, m_sca[:, 2:3],
                                   start=True, stop=False),
                lambda e: e.matmul(CTP[:, 1:2], o32row, m_sca[:, 3:4],
                                   start=False, stop=False),
                lambda e: e.matmul(CTP[:, 2:3], o32row, m_sca[:, 4:5],
                                   start=False, stop=False),
                lambda e: e.matmul(CTP[:, 3:4], o32row, m_sca[:, 5:6],
                                   start=False, stop=False),
                lambda e: e.matmul(CTP[:, 4:5], o32row, m_sca[:, 0:1],
                                   start=False, stop=False),
                lambda e: e.matmul(CTP[:, 5:6], o32row, m_sca[:, 1:2],
                                   start=False, stop=True),
            ], waits=[sc])
            wq2["mean"] = s.stage("vector", [
                lambda e: e.tensor_copy(m_sw1b, CTP[:, 0:1]),
                lambda e: e.tensor_copy(m_sw2b, CTP[:, 1:2]),
                lambda e: e.tensor_copy(m_mw1db, CTP[:, 2:3]),
                lambda e: e.tensor_copy(m_mw2db, CTP[:, 3:4]),
                lambda e: e.tensor_copy(m_mw1d, CTP[:, 4:5]),
                lambda e: e.tensor_copy(m_mw2d, CTP[:, 5:6]),
            ], waits=[bc])

        def emit_wq_pass2(n):
            for _ in range(n):
                i = wq2["i"]
                if i >= len(wq_items):
                    return
                wq2["i"] += 1
                it = wq_items[i]
                src_, _, swb = wq_src(it)
                w = [wq2["mean"]]
                if wq2["last"]:
                    w.append(wq2["last"])
                ld = dma1("sync", wq_stg, src_, waits=w, stream="dwq")
                g1 = s.stage("vector", [
                    (lambda e, swb=swb: e.tensor_scalar(
                        wq_ta, wq_stg, swb[:, 0:1], 0.5, ALU.mult, ALU.is_gt)),
                    (lambda e, swb=swb: e.tensor_scalar(
                        wq_tb, wq_stg, swb[:, 0:1], -0.5, ALU.mult, ALU.is_lt)),
                ], waits=[ld] + ([wq2["last"]] if wq2["last"] else []))
                g2 = s.stage("vector", [
                    lambda e: e.tensor_sub(wq_o16, wq_ta, wq_tb),
                ])
                wq2["last"] = dma1("gpsimd", wq_dst(it), wq_o16, waits=[g2],
                                   stream="dgo")

        s.mark("prologue")
        prol_w = s.barrier()
        ctx_init = s.stage("vector", [
            lambda e: e.memset(HP[:, 8192:10272], 0.0),
        ], waits=prol_w)
        ctx_last = None
        last_xuse = None
        prev_trd = None
        prev_s1red = None
        for t in range(NT):
            w = list(prol_w) if t == 0 else []
            if last_xuse is not None:
                w.append(last_xuse)
            xld = s.dmas("sync", [
                (lambda e, t=t: e.dma_start(
                    out=HP[:, 0:4096].rearrange("p (a m) -> p a m", a=TC),
                    in_=xin[t * 512:(t + 1) * 512, :].rearrange("(a p) m -> p a m", p=128)))
            ], waits=w, stream="dxa")
            lnd = emit_ln(lambda tc: HP[:, tc * 1024:(tc + 1) * 1024],
                          lambda tc: HPf[:, 20544 + tc * 1024: 20544 + (tc + 1) * 1024],
                          xld, dst_w=[prev_trd] if prev_trd else [])
            last_xuse = lnd
            trd = emit_tblock(
                lambda k: HPf[:, 20544 + (k // EC) * 1024 + (k % EC) * 128:
                              20544 + (k // EC) * 1024 + (k % EC) * 128 + 128],
                lambda k, t=t: WP[:, RXL + (k % EC) * 4096 + t * 512 + (k // EC) * 128:
                                  RXL + (k % EC) * 4096 + t * 512 + (k // EC) * 128 + 128],
                TC * EC, lnd, PA16,
                extra_w=[prev_s1red] if prev_s1red else [])
            prev_trd = trd
            if t == 0:
                s.mark("p1tr")
            kev = emit_proj(RA, m_cbk, t,
                            lambda ec: TP[:, KN + ec * 512: KN + (ec + 1) * 512],
                            [trd, wt_loads[0], sc_bias, st_ones] if t == 0
                            else [trd])
            if t == 0:
                s.mark("p1kp")
            vev = []
            for tc in range(TC):
                for half in range(2):
                    pab = PAB_A if half == 0 else PAB_B
                    wv = [kev] if tc == 0 else []
                    if t == 0 and tc == 0 and half == 0:
                        wv += [wt_loads[1], row_ld]
                    if len(vev) >= 2:
                        wv.append(vev[-2])
                    mmv = s.stage("tensor", [
                        (lambda e, kc=kc, tc=tc, half=half, pab=pab, t=t: e.matmul(
                            pab[:, 0:512],
                            WP[:, RXL + kc * 4096 + t * 512 + tc * 128:
                               RXL + kc * 4096 + t * 512 + tc * 128 + 128],
                            WP[:, RB + kc * 1024 + half * 512: RB + kc * 1024 + (half + 1) * 512],
                            start=(kc == 0), stop=False))
                        for kc in range(EC)
                    ] + [
                        (lambda e, half=half, pab=pab: e.matmul(
                            pab[:, 0:512], ones16row,
                            BR[0:1, half * 512:(half + 1) * 512],
                            start=False, stop=True)),
                    ], waits=wv)
                    vev.append(s.stage("scalar", [
                        (lambda e, tc=tc, half=half, pab=pab: e.activation(
                            TP[:, tc * 1040 + half * 520: tc * 1040 + half * 520 + 520]
                            .rearrange("p (h d) -> p h d", d=65)[:, :, 0:64],
                            pab[:, 0:512].rearrange("p (h d) -> p h d", d=64),
                            AF.Copy)),
                    ], waits=[mmv] + ([ctx_last] if (tc == 0 and half == 0
                                                     and ctx_last) else [])))
            ones_v = s.stage("vector", [
                lambda e: e.memset(
                    TP[:, 0:4160].rearrange("p (a h d) -> p a h d", a=TC, d=65)
                    [:, :, :, 64:65], 1.0),
            ], waits=[vev[-1]])
            dgd = emit_diag(KN, kev,
                            extra_first=[] if ctx_last else list(prol_w))
            red_tok = {0: None, 1: None}
            for hp in range(8):
                par1 = hp % 2
                if hp == 0:
                    wm = [kev, dgd]
                elif hp == 1:
                    wm = [kev, dgd, vev[-1]]
                    if wq2["mean"] is not None:
                        wm.append(wq2["mean"])
                else:
                    wm = [red_tok[par1]]
                if t == 0 and hp == 0:
                    wm.append(pj_load)
                if par1 == 0:
                    zlo = lambda tc: DDP[:, tc * 256: tc * 256 + 256]
                    zhi = lambda tc: DDP[:, 1024 + tc * 256: 1024 + tc * 256 + 256]
                else:
                    zlo = lambda tc: (PAB_A[:, (tc % 2) * 256: (tc % 2) * 256 + 256]
                                      if tc < 2 else
                                      CTP[:, (tc % 2) * 256: (tc % 2) * 256 + 256])
                    zhi = lambda tc: (PAB_B[:, (tc % 2) * 256: (tc % 2) * 256 + 256]
                                      if tc < 2 else
                                      CTP[:, 512 + (tc % 2) * 256: 512 + (tc % 2) * 256 + 256])
                mm = s.stage("tensor", [
                    mmf
                    for tc in range(TC)
                    for mmf in [
                        (lambda e, hp=hp, tc=tc, z=zlo(tc): e.matmul(
                            z,
                            TP[0:64, KN + hp * 512 + tc * 128: KN + hp * 512 + tc * 128 + 128],
                            PJ[0:64, :], start=(tc % 2 == 0), stop=(tc % 2 == 1),
                            tile_position=(0, 0))),
                        (lambda e, hp=hp, tc=tc, z=zhi(tc): e.matmul(
                            z,
                            TP[64:128, KN + hp * 512 + tc * 128: KN + hp * 512 + tc * 128 + 128],
                            PJ[64:128, :], start=(tc % 2 == 0), stop=(tc % 2 == 1),
                            tile_position=(64, 0))),
                    ]
                ], waits=wm)
                if t == 0 and hp == 0:
                    s.mark("p1d1")
                if par1 == 0:
                    red_tok[0] = s.stage("vector", [
                        lambda e: e.reduce_max(
                            m_red2,
                            DDP[:, 0:2048].rearrange("p (h m) -> p h m", h=2),
                            AX.X),
                    ], waits=[mm])
                else:
                    red_tok[1] = s.stage("vector", [
                        lambda e: e.reduce_max(m_red4[:, 0:1], PAB_A[:, 0:512], AX.X),
                        lambda e: e.reduce_max(m_red4[:, 1:2], PAB_B[:, 0:512], AX.X),
                        lambda e: e.reduce_max(
                            m_red2,
                            CTP[:, 0:1024].rearrange("p (h m) -> p h m", h=2),
                            AX.X),
                    ], waits=[mm])
                    s.stage("vector", [
                        lambda e: e.tensor_max(m_red2, m_red2, m_red4),
                    ])
                    prev_s1red = red_tok[1]
                rm_done = s.stage("vector", [
                    (lambda e, hp=hp: e.tensor_max(
                        m_RM[:, 2 * hp:2 * hp + 2], m_RM[:, 2 * hp:2 * hp + 2],
                        m_red2)),
                ])
            if t == 0:
                s.mark("p1dd")
            ra = s.stage("vector", [
                lambda e: e.reduce_max(m_L1col, m_RM, AX.X)])
            ra_last = ra
            rowm = s.stage("tensor", [
                lambda e: e.matmul(CTP[0:1, 0:128], m_L1col, IDN32[:],
                                   start=True, stop=True)], waits=[ra])
            rL = s.stage("vector", [
                lambda e: e.reduce_max(m_L1, CTP[0:1, 0:128], AX.X)], waits=[rowm])
            bcm = s.stage("tensor", [
                lambda e: e.matmul(CTP[:, 0:1], o32row, m_L1,
                                   start=True, stop=True)], waits=[rL])
            vL = s.stage("vector", [
                lambda e: e.tensor_scalar(m_cLt, CTP[:, 0:1], -1.0, CSTAB,
                                          ALU.mult, ALU.add),
                lambda e: e.tensor_scalar_mul(m_mLt, CTP[:, 0:1], -1.0),
            ], waits=[bcm])
            eD = s.stage("scalar", [
                lambda e: e.activation(m_eD, m_Lprev, AF.Exp, bias=m_mLt[:, 0:1]),
            ], waits=[vL])
            s.stage("vector", [
                lambda e: e.tensor_scalar_mul(m_Lprev, m_mLt, -1.0),
            ], waits=[eD])
            edl = s.stage("vector", [
                lambda e: e.tensor_scalar(m_edb, m_diag, -0.5, m_mLt[:, 0:1],
                                          ALU.mult, ALU.add),
            ], waits=[dgd, vL])
            prev_ctx_ev = None
            prev_readers = [red_tok[0], red_tok[1]]
            last_cmmz = {0: None, 1: None}
            cmm = None
            for hp in range(8):
                wm = [kev] + prev_readers
                hmm = s.stage("tensor", [
                    mmf
                    for tc in range(TC)
                    for mmf in [
                        (lambda e, hp=hp, tc=tc: e.matmul(
                            DDP[:, tc * 256: tc * 256 + 256],
                            TP[0:64, KN + hp * 512 + tc * 128: KN + hp * 512 + tc * 128 + 128],
                            PJ[0:64, :], start=(tc % 2 == 0), stop=(tc % 2 == 1),
                            tile_position=(0, 0))),
                        (lambda e, hp=hp, tc=tc: e.matmul(
                            DDP[:, 1024 + tc * 256: 1024 + tc * 256 + 256],
                            TP[64:128, KN + hp * 512 + tc * 128: KN + hp * 512 + tc * 128 + 128],
                            PJ[64:128, :], start=(tc % 2 == 0), stop=(tc % 2 == 1),
                            tile_position=(64, 0))),
                    ]
                ], waits=wm)
                prev_readers = []
                for sub in range(2):
                    h = 2 * hp + sub
                    kz = TP[:, 8448:9472] if sub == 0 else HPf[:, 15360:16384]
                    kfw = [hmm, edl]
                    if last_cmmz.get(sub) is not None:
                        kfw.append(last_cmmz[sub])
                    kf = s.stage("scalar", [
                        (lambda e, sub=sub, h=h, tc=tc, kz=kz: e.activation(
                            kz[:, tc * 256:(tc + 1) * 256],
                            DDP[:, sub * 1024 + tc * 256:
                                sub * 1024 + (tc + 1) * 256],
                            AF.Exp, bias=m_edb[:, tc * 16 + h: tc * 16 + h + 1]))
                        for tc in range(TC)
                    ], waits=kfw)
                    prev_readers.append(kf)
                    zb = h % 2
                    wcm = [kf, ones_v]
                    if prev_ctx_ev is not None and zb == 0:
                        wcm.append(prev_ctx_ev)
                    cmm = s.stage("tensor", [
                        (lambda e, ms=ms, tc=tc, h=h, zb=zb, kz=kz: e.matmul(
                            CTP[:, zb * 512 + ms * 65: zb * 512 + ms * 65 + 65],
                            kz[:, tc * 256 + ms * 128: tc * 256 + ms * 128 + 128],
                            TP[:, tc * 1040 + h * 65: tc * 1040 + h * 65 + 65],
                            start=(ms == 0 and tc == 0), stop=(ms == 1 and tc == TC - 1)))
                        for ms in range(2) for tc in range(TC)
                    ], waits=wcm)
                    last_cmmz[sub] = cmm
                prev_ctx_ev = s.stage("vector", [
                    (lambda e, hp=hp: e.scalar_tensor_tensor(
                        HP[:, 8192 + hp * 260: 8192 + (hp + 1) * 260]
                        .rearrange("p (z c) -> p z c", c=130),
                        HP[:, 8192 + hp * 260: 8192 + (hp + 1) * 260]
                        .rearrange("p (z c) -> p z c", c=130),
                        m_eD[:, 0:1],
                        CTP[:, 0:1024].rearrange("p (z c) -> p z c", z=2)[:, :, 0:130],
                        ALU.mult, ALU.add)),
                ], waits=[cmm, ctx_init])
            ctx_last = prev_ctx_ev
            if t < 3:
                emit_wq_pass1(22)
            elif t == 3:
                emit_wq_pass1(len(wq_items))
                emit_wq_mean()
                emit_wq_pass2(13)
            else:
                emit_wq_pass2(13)
        emit_wq_pass2(len(wq_items))
        wq_final = wq2["last"]
        s.mark("p1")

        ar_o = dma1("gpsimd", rmio.ap(), m_RM, waits=[rm_done], stream="dgo")
        import os as _os
        if _os.environ.get("MK_NOCC"):
            cc1 = dma1("gpsimd", rmo.ap(), rmio.ap(), waits=[ar_o], stream="dgo")
        else:
            cc1 = s.collective(
                lambda e: e.collective_compute(
                    "AllReduce", ALU.max, replica_groups=[list(range(NCORES))],
                    ins=[rmio.ap().opt()], outs=[rmo.ap().opt()]),
                waits=[ar_o])
        ar_i = dma1("gpsimd", m_RM, rmo.ap(), waits=[cc1, ra_last], stream="dgo")
        rr = s.stage("vector", [
            lambda e: e.reduce_max(m_red2[:, 0:1], m_RM, AX.X)], waits=[ar_i])
        sb_o = dma1("gpsimd", srow.ap(), m_red2[:, 0:1], waits=[rr], stream="dgo")
        sb_i = dma1("sync", m_srow, srow.ap().rearrange("a b -> b a"),
                    waits=[sb_o], stream="dsb", allow_slow_non_contiguous=True)
        sr = s.stage("vector", [
            lambda e: e.reduce_max(m_sca[:, 6:7], m_srow, AX.X)], waits=[sb_i])
        sbm = s.stage("tensor", [
            lambda e: e.matmul(CTP[:, 0:1], o32row, m_sca[:, 6:7],
                               start=True, stop=True)], waits=[sr, ctx_last])
        sev = s.stage("vector", [
            lambda e: e.tensor_copy(m_stab, CTP[:, 0:1]),
            lambda e: e.tensor_scalar_mul(m_mS, CTP[:, 0:1], -1.0),
        ], waits=[sbm])
        fst = s.stage("scalar", [
            lambda e: e.activation(m_f, m_Lprev, AF.Exp, bias=m_mS[:, 0:1]),
        ], waits=[sev])
        s.mark("ar")

        sxr = s.stage("vector", [
            (lambda e, kc=kc: e.reduce_sum(
                m_sx[:, kc:kc + 1],
                WP[:, RXL + kc * 4096: RXL + (kc + 1) * 4096], AX.X))
            for kc in range(EC)
        ])
        sxc = s.stage("vector", [
            lambda e: e.tensor_copy(TP[:, 8448:8456], m_sx),
        ])
        vsm = s.stage("tensor", [
            (lambda e, kc=kc, half=half: e.matmul(
                CTP[0:1, half * 512:(half + 1) * 512],
                TP[:, 8448 + kc: 8449 + kc],
                WP[:, RB + kc * 1024 + half * 512: RB + kc * 1024 + (half + 1) * 512],
                start=(kc == 0), stop=False))
            for half in range(2) for kc in range(EC)
        ] + [
            (lambda e, half=half: e.matmul(
                CTP[0:1, half * 512:(half + 1) * 512],
                k4096h, r_bv[:, half * 512:(half + 1) * 512],
                start=False, stop=True))
            for half in range(2)
        ], waits=[sxc, ctx_last])
        vse = s.stage("vector", [
            lambda e: e.tensor_copy(HP[0:1, 0:1024], CTP[0:1, 0:1024]),
        ], waits=[vsm])
        v2a = s.stage("vector", [
            (lambda e, ms=ms: e.tensor_scalar_mul(
                HP[0:1, 1024:3104].rearrange("a (h m c) -> a h m c", m=2, c=65)
                [:, :, ms:ms + 1, 0:64],
                HP[0:1, 0:1024].rearrange("a (h one d) -> a h one d", one=1, d=64),
                KEPS))
            for ms in range(2)
        ] + [
            lambda e: e.memset(
                HP[0:1, 1024:3104].rearrange("a (h m c) -> a h m c", m=2, c=65)
                [:, :, :, 64:65], KEPS * 4096.0),
        ], waits=[vse])
        vsb = s.stage("tensor", [
            (lambda e, c=c: e.matmul(
                DDP[:, c * 512:(c + 1) * 512], o32row,
                HP[0:1, 1024 + c * 512: 1024 + (c + 1) * 512],
                start=True, stop=True))
            for c in range(4)
        ] + [
            lambda e: e.matmul(CTP[:, 0:32], o32row, HP[0:1, 3072:3104],
                               start=True, stop=True),
        ], waits=[v2a])
        vsc = s.stage("vector", [
            (lambda e, c=c: e.tensor_copy(HP[:, c * 512:(c + 1) * 512],
                                          DDP[:, c * 512:(c + 1) * 512]))
            for c in range(4)
        ] + [
            lambda e: e.tensor_copy(HP[:, 2048:2080], CTP[:, 0:32]),
        ], waits=[vsb])
        cxc = s.stage("vector", [
            lambda e: e.scalar_tensor_tensor(
                CXP[:], HP[:, 8192:10272], m_f[:, 0:1], HP[:, 0:2080],
                ALU.mult, ALU.add),
        ], waits=[vsc, fst])
        ebm = s.stage("tensor", [
            (lambda e, h=h, ms=ms: e.matmul(
                DDP[0:1, h * 128: h * 128 + 65], ones16col,
                CXP[:, h * 130 + ms * 65: h * 130 + ms * 65 + 65],
                start=(ms == 0 and h % 4 == 0), stop=(ms == 1 and h % 4 == 3)))
            for h in range(H) for ms in range(2)
        ], waits=[cxc])
        ebv = s.stage("vector", [
            lambda e: e.tensor_scalar_mul(
                r_eps.rearrange("a (h c) -> a h c", c=65),
                DDP[0:1, 0:2048].rearrange("p (h c) -> p h c", h=16)[:, :, 0:65], KEPS),
        ], waits=[ebm])

        w2ab = s.dmas("sync", [
            (lambda e, kc=kc: e.dma_start(
                out=WP[:, w2t_region(kc): w2t_region(kc) + 1024],
                in_=ws2[:, kc * 128:(kc + 1) * 128], transpose=True))
            for kc in range(16)
        ], waits=[wq_final, vsm])

        s.mark("p2")
        x2st = [None] * NT
        p3a_last = None
        for t in range(NT):
            w0 = [ctx_last] if t == 0 else [p3a_last, x2st[t - 1]]
            xl = s.dmas("sync", [
                (lambda e, t=t: e.dma_start(
                    out=HP[:, 0:4096].rearrange("p (a m) -> p a m", a=TC),
                    in_=xin[t * 512:(t + 1) * 512, :].rearrange("(a p) m -> p a m", p=128)))
            ], waits=w0 + ([cxc] if t == 0 else []), stream="dxa")
            qev = emit_proj(RC, m_cbq, t,
                            lambda ec: TP[:, KN + ec * 512: KN + (ec + 1) * 512], w0)
            dgd = emit_diag(KN, qev,
                            extra_first=w0 + ([ebv] if t == 0 else []))
            prev_out_ev = None
            prev_readers = [dgd]
            last_qf = {0: None, 1: None}
            last_eqf = {0: None, 1: None}
            last_ommz = {0: None, 1: None}
            for hp in range(8):
                wm = [qev] + prev_readers
                hmm = s.stage("tensor", [
                    mmf
                    for tc in range(TC)
                    for mmf in [
                        (lambda e, hp=hp, tc=tc: e.matmul(
                            DDP[:, tc * 256: tc * 256 + 256],
                            TP[0:64, KN + hp * 512 + tc * 128: KN + hp * 512 + tc * 128 + 128],
                            PJ[0:64, :], start=(tc % 2 == 0), stop=(tc % 2 == 1),
                            tile_position=(0, 0))),
                        (lambda e, hp=hp, tc=tc: e.matmul(
                            DDP[:, 1024 + tc * 256: 1024 + tc * 256 + 256],
                            TP[64:128, KN + hp * 512 + tc * 128: KN + hp * 512 + tc * 128 + 128],
                            PJ[64:128, :], start=(tc % 2 == 0), stop=(tc % 2 == 1),
                            tile_position=(64, 0))),
                    ]
                ], waits=wm)
                nst = s.stage("vector", [
                    (lambda e, sub=sub, tc=tc, hp=hp: e.reduce_max(
                        m_nst[:, tc * 16 + 2 * hp + sub: tc * 16 + 2 * hp + sub + 1],
                        DDP[:, sub * 1024 + tc * 256: sub * 1024 + (tc + 1) * 256],
                        AX.X, negate=True))
                    for sub in range(2) for tc in range(TC)
                ], waits=[hmm])
                edq = s.stage("vector", [
                    (lambda e, hp=hp: e.scalar_tensor_tensor(
                        m_edb.rearrange("p (a h) -> p a h", a=TC)[:, :, 2 * hp:2 * hp + 2],
                        m_diag.rearrange("p (a h) -> p a h", a=TC)[:, :, 2 * hp:2 * hp + 2],
                        -0.5,
                        m_nst.rearrange("p (a h) -> p a h", a=TC)[:, :, 2 * hp:2 * hp + 2],
                        ALU.mult, ALU.add)),
                ], waits=[nst, dgd])

                prev_readers = [nst]
                ozb = lambda sub, tc: (tc // 2) * 512 + sub * 256 + (tc % 2) * 128
                ozoneP = [[CTP[:, ozb(s_, tc): ozb(s_, tc) + 65] for tc in range(4)]
                          for s_ in range(2)]
                ozcolP = [[CTP[:, ozb(s_, tc) + 64: ozb(s_, tc) + 65] for tc in range(4)]
                          for s_ in range(2)]
                oznumP = [[CTP[:, ozb(s_, tc): ozb(s_, tc) + 64] for tc in range(4)]
                          for s_ in range(2)]
                omms = []
                qzs = [TP[:, 8448:9472], HPf[:, 16384:18432]]
                qfs = []
                for sub in range(2):
                    h = 2 * hp + sub
                    qz = qzs[sub]
                    qfw = [hmm, edq]
                    if last_ommz.get(sub) is not None:
                        qfw.append(last_ommz[sub])
                    qf = s.stage("scalar", [
                        (lambda e, sub=sub, tc=tc, h=h, qz=qz: e.activation(
                            qz[:, tc * 256:(tc + 1) * 256],
                            DDP[:, sub * 1024 + tc * 256:
                                sub * 1024 + (tc + 1) * 256],
                            AF.Exp, bias=m_edb[:, tc * 16 + h: tc * 16 + h + 1]))
                        for tc in range(TC)
                    ], waits=qfw)
                    last_qf[sub] = qf
                    prev_readers.append(qf)
                    qfs.append(qf)
                for sub in range(2):
                    h = 2 * hp + sub
                    par = sub
                    qz = qzs[sub]
                    pz = PA16 if par == 0 else PB16
                    ozone = ozoneP[par]
                    tqw = [qfs[sub]]
                    if last_eqf.get(par) is not None:
                        tqw.append(last_eqf[par])
                    tqf = s.stage("tensor", [
                        (lambda e, k=k, qz=qz, pz=pz: e.transpose(
                            pz[:, k * 128:(k + 1) * 128],
                            qz[:, k * 128:(k + 1) * 128],
                            IDN[:]))
                        for k in range(8)
                    ], waits=tqw)
                    eqf = s.stage("scalar" if par == 0 else "vector", [
                        (lambda e, k=k, qz=qz, pz=pz, par=par: (
                            e.activation(qz[:, k * 128:(k + 1) * 128],
                                         pz[:, k * 128:(k + 1) * 128], AF.Copy)
                            if par == 0 else
                            e.tensor_copy(qz[:, k * 128:(k + 1) * 128],
                                          pz[:, k * 128:(k + 1) * 128])))
                        for k in range(8)
                    ], waits=[tqf])
                    last_eqf[par] = eqf
                    ommw = [eqf, qfs[0], qfs[1], cxc, ebv]
                    if sub == 0 and prev_out_ev is not None:
                        ommw.append(prev_out_ev)
                    omm = s.stage("tensor", [
                        mmf
                        for tc in range(TC)
                        for mmf in [
                            (lambda e, tc=tc, h=h, qz=qz, ozone=ozone, sub=sub: e.matmul(
                                ozone[tc],
                                qz[:, tc * 256: tc * 256 + 128],
                                CXP[:, h * 130: h * 130 + 65],
                                start=(sub == 0 and tc % 2 == 0), stop=False)),
                            (lambda e, tc=tc, h=h, qz=qz, ozone=ozone: e.matmul(
                                ozone[tc],
                                qz[:, tc * 256 + 128: tc * 256 + 256],
                                CXP[:, h * 130 + 65: h * 130 + 130],
                                start=False, stop=False)),
                            (lambda e, tc=tc, h=h, ozone=ozone, sub=sub: e.matmul(
                                ozone[tc],
                                ones16row32, r_eps[:, h * 65: h * 65 + 65],
                                start=False, stop=(sub == 1 and tc % 2 == 1))),
                        ]
                    ], waits=ommw)
                    last_ommz[par] = omm
                    omms.append(omm)
                dvs = s.stage("vector", [
                    (lambda e, sub=sub, tc=tc: e.reciprocal(
                        m_dinv8[:, sub * 4 + tc: sub * 4 + tc + 1],
                        ozcolP[sub][tc]))
                    for sub in range(2) for tc in range(TC)
                ], waits=[omms[1]])
                prev_out_ev = s.stage("scalar", [
                    (lambda e, sub=sub, tc=tc, h=2 * hp + sub: e.activation(
                        TP[:, tc * 1024 + h * 64: tc * 1024 + h * 64 + 64],
                        oznumP[sub][tc], AF.Copy,
                        scale=m_dinv8[:, sub * 4 + tc: sub * 4 + tc + 1]))
                    for sub in range(2) for tc in range(TC)
                ], waits=[dvs])
            trd = emit_tblock(
                lambda k: TP[:, (k // EC) * 1024 + (k % EC) * 128:
                             (k // EC) * 1024 + (k % EC) * 128 + 128],
                lambda k: TP[:, KN + (k % EC) * 512 + (k // EC) * 128:
                             KN + (k % EC) * 512 + (k // EC) * 128 + 128],
                TC * EC, prev_out_ev, PB16)
            x2ev = []
            for tc in range(TC):
                for half in range(2):
                    pab = PAB_A if half == 0 else PAB_B
                    wv = [trd, xl] if tc == 0 else []
                    if len(x2ev) >= 2:
                        wv.append(x2ev[-2])
                    mmo = s.stage("tensor", [
                        (lambda e, kc=kc, tc=tc, half=half, pab=pab: e.matmul(
                            pab[:, 0:512],
                            TP[:, KN + kc * 512 + tc * 128: KN + kc * 512 + tc * 128 + 128],
                            WP[:, RD + kc * 1024 + half * 512: RD + kc * 1024 + (half + 1) * 512],
                            start=(kc == 0), stop=False))
                        for kc in range(EC)
                    ] + [
                        (lambda e, half=half, pab=pab: e.matmul(
                            pab[:, 0:512], ones16row32,
                            BR[32:33, half * 512:(half + 1) * 512],
                            start=False, stop=True)),
                    ], waits=wv)
                    x2ev.append(s.stage("vector", [
                        (lambda e, tc=tc, half=half, pab=pab: e.tensor_add(
                            HP[:, 4096 + tc * 1024 + half * 512:
                               4096 + tc * 1024 + (half + 1) * 512],
                            pab[:, 0:512],
                            HP[:, tc * 1024 + half * 512: tc * 1024 + (half + 1) * 512])),
                    ], waits=[mmo]))
            p3a_last = x2ev[-1]
            x2st[t] = s.dmas("gpsimd", [
                (lambda e, t=t, tc=tc: e.dma_start(
                    out=x2s[(t * TC + tc) * 128:(t * TC + tc + 1) * 128, :],
                    in_=HP[:, 4096 + tc * 1024: 4096 + (tc + 1) * 1024]))
                for tc in range(TC)
            ], waits=[x2ev[-1]])

        s.mark("p3a")
        p3ab = s.barrier()
        w1ld = s.dmas("sync", [
            (lambda e, fc=fc: e.dma_start(
                out=WP[:, RXL + fc * 4096: RXL + (fc + 1) * 4096],
                in_=ws1[:, fc * 128:(fc + 1) * 128], transpose=True))
            for fc in range(EC)
        ], waits=p3ab)
        w2cd = s.dmas("scalar", [
            (lambda e, kc=kc: e.dma_start(
                out=WP[:, w2t_region(kc): w2t_region(kc) + 1024],
                in_=ws2[:, kc * 128:(kc + 1) * 128], transpose=True))
            for kc in range(16, 32)
        ], waits=p3ab)
        gb_last = None
        for idx, vecd in enumerate([gammad, betad]):
            for hlf in range(4):
                gl = dma1("sync", m_s1row,
                          vecd[None, hlf * 256:(hlf + 1) * 256],
                          waits=list(p3ab) + ([gb_last] if gb_last else []))
                gm = s.stage("tensor", [
                    lambda e: e.matmul(CTP[:, 0:256], o32row, m_s1row,
                                       start=True, stop=True),
                ], waits=[gl] + ([gb_last] if gb_last else []))
                gb_last = s.stage("vector", [
                    (lambda e, idx=idx, hlf=hlf: e.tensor_copy(
                        CXP[:, idx * 1024 + hlf * 256: idx * 1024 + (hlf + 1) * 256],
                        CTP[:, 0:256])),
                ], waits=[gm])

        s.mark("bnd")
        T2 = 256
        NT2 = N // T2
        TC2 = 2
        p3bw = s.barrier()
        prev_store = None
        prev_oev = None
        for t2 in range(NT2):
            w0 = list(p3bw) if t2 == 0 else [prev_store]
            xq_done = []
            x2la = s.dmas("sync", [
                (lambda e, g=t2 * TC2: e.dma_start(
                    out=TPf[:, 2048:3072],
                    in_=x2s[g * 128:(g + 1) * 128, 0:1024])),
            ], waits=list(w0) + [x2st[t2 // 2]], stream="dxa")
            x2lb = s.dmas("sync", [
                (lambda e, g=t2 * TC2 + 1: e.dma_start(
                    out=TPf[:, 3072:4096],
                    in_=x2s[g * 128:(g + 1) * 128, 0:1024])),
            ], waits=[], stream="dxa")
            s.stage("vector", [
                lambda e: e.bn_stats(m_bn2[:, 0:6], TPf[:, 2048:2560]),
                lambda e: e.bn_stats(m_bn2[:, 6:12], TPf[:, 2560:3072]),
                lambda e: e.bn_stats(m_bn2[:, 12:18], TPf[:, 3072:3584]),
                lambda e: e.bn_stats(m_bn2[:, 18:24], TPf[:, 3584:4096]),
            ], waits=[x2la, x2lb])
            ag = s.stage("vector", [
                lambda e: e.bn_aggr(m_mv2[:, 0:2], m_bn2[:, 0:12]),
                lambda e: e.bn_aggr(m_mv2[:, 2:4], m_bn2[:, 12:24]),
            ])
            rs = s.stage("scalar", [
                lambda e: e.activation(m_sd2[:, 0:1], m_mv2[:, 1:2],
                                       AF.Sqrt, bias=kLNEPS),
                lambda e: e.activation(m_sd2[:, 1:2], m_mv2[:, 3:4],
                                       AF.Sqrt, bias=kLNEPS),
            ], waits=[ag])
            s.stage("vector", [
                (lambda e, tc=tc: e.scalar_tensor_tensor(
                    HP[:, tc * 1024:(tc + 1) * 1024],
                    TPf[:, 2048 + tc * 1024: 2048 + (tc + 1) * 1024],
                    m_mv2[:, 2 * tc: 2 * tc + 1],
                    CXP[:, 0:1024], ALU.subtract, ALU.mult))
                for tc in range(TC2)
            ])
            s.stage("vector", [
                lambda e: e.reciprocal(m_rstd2[:, 0:2], m_sd2[:, 0:2]),
            ], waits=[rs])
            s.stage("vector", [
                (lambda e, tc=tc: e.scalar_tensor_tensor(
                    HP[:, 2048 + tc * 1024: 2048 + (tc + 1) * 1024],
                    HP[:, tc * 1024:(tc + 1) * 1024],
                    m_rstd2[:, tc:tc + 1],
                    CXP[:, 1024:2048], ALU.mult, ALU.add))
                for tc in range(TC2)
            ])
            s.stage("vector", [
                (lambda e, tc=tc: e.reduce_max(
                    m_rx1b[:, tc:tc + 1],
                    HP[:, 2048 + tc * 1024: 2048 + (tc + 1) * 1024],
                    AX.X, apply_absolute_value=True))
                for tc in range(TC2)
            ])
            s.stage("vector", [
                lambda e: e.tensor_scalar(m_mx1b[:, 0:2], m_rx1b[:, 0:2],
                                          1e-5, 1.0 / 127.0, ALU.max, ALU.mult),
            ])
            rcp = s.stage("vector", [
                lambda e: e.reciprocal(m_s1cb[:, 0:2], m_mx1b[:, 0:2]),
                lambda e: e.tensor_single_scalar(m_s1i[:, 0:2], m_mx1b[:, 0:2],
                                                 m_mw1d[:, 0:1], ALU.mult),
            ])
            u1 = s.stage("scalar", [
                (lambda e, tc=tc: e.activation(
                    TP[:, tc * 1024:(tc + 1) * 1024],
                    HP[:, 2048 + tc * 1024: 2048 + (tc + 1) * 1024],
                    AF.Copy, bias=1536.0, scale=m_s1cb[:, tc:tc + 1]))
                for tc in range(TC2)
            ], waits=[rcp])
            for tc in range(TC2):
                tp_ = s.stage("tensor", [
                    (lambda e, fc=i, tc=tc: e.transpose(
                        PA16[:, fc * 128:(fc + 1) * 128],
                        TP[:, tc * 1024 + fc * 128: tc * 1024 + fc * 128 + 128],
                        IDN[:]))
                    for i in range(8)
                ], waits=[u1] + ([xq_done[-1]] if xq_done else []))
                xq_done.append(s.stage("vector", [
                    (lambda e, fc=i, tc=tc: e.tensor_scalar_add(
                        TP[:, 2048 + fc * 256 + tc * 128: 2048 + fc * 256 + tc * 128 + 128],
                        PA16[:, fc * 128:(fc + 1) * 128], -1536.0))
                    for i in range(8)
                ], waits=[tp_]))
            s1o = s.dmas("gpsimd", [
                lambda e: e.dma_start(out=rmio[:, 0:2], in_=m_s1i[:, 0:2]),
            ], waits=[xq_done[-1]], stream="dgo")
            s1i_ = s.dmas("sync", [
                lambda e: e.dma_start(out=m_s1row[:, 0:256].rearrange("a (c p) -> a c p", p=128),
                                      in_=rmio[:, 0:2].rearrange("p c -> c p")[None],
                                      allow_slow_non_contiguous=True),
            ], waits=[s1o], stream="dsb")
            sbm1 = s.stage("tensor", [
                (lambda e, tc=tc: e.matmul(
                    CTP[:, tc * 128:(tc + 1) * 128], o32row,
                    m_s1row[:, tc * 128:(tc + 1) * 128],
                    start=(tc == 0), stop=(tc == TC2 - 1)))
                for tc in range(TC2)
            ], waits=[s1i_])
            sbe1 = s.stage("vector", [
                lambda e: e.tensor_copy(TPf[:, 0:256], CTP[:, 0:256]),
            ], waits=[sbm1])
            sc1_hist = []
            m1_done = None
            for r in range(4):
                if r % 2 == 0:
                    zones = [DDP[:, 0:512], DDP[:, 512:1024],
                             DDP[:, 1024:1536], DDP[:, 1536:2048]]
                else:
                    zones = [PAB_A[:, 0:512], PAB_B[:, 0:512],
                             CTP[:, 0:512], CTP[:, 512:1024]]
                if r == 0:
                    wv = [xq_done[-1], w1ld]
                elif r == 1:
                    wv = [sbe1]
                else:
                    wv = [sc1_hist[r - 2]]
                mmh = s.stage("tensor", [
                    (lambda e, fc=fc, hc=r * 8 + j, z=zones[j // 2],
                     o=(j % 2) * 256,
                     st=(j % 2 == 0 and fc == 0),
                     sp=(j % 2 == 1 and fc == EC - 1): e.matmul(
                        z[:, o:o + 256],
                        WP[:, RXL + fc * 4096 + hc * 128: RXL + fc * 4096 + hc * 128 + 128],
                        TP[:, 2048 + fc * 256: 2048 + (fc + 1) * 256],
                        start=st, stop=sp))
                    for j in range(8)
                    for fc in range(EC)
                ], waits=wv)
                sc1 = s.stage("vector", [
                    (lambda e, hc=r * 8 + j, z=zones[j // 2], o=(j % 2) * 256:
                     e.tensor_mul(
                        HP[:, hc * 256:(hc + 1) * 256], z[:, o:o + 256],
                        TPf[:, 0:256]))
                    for j in range(8)
                ], waits=[mmh])
                sc1_hist.append(sc1)
                gl = s.stage("scalar", [
                    (lambda e, hc=r * 8 + j: e.activation(
                        HP[:, hc * 256:(hc + 1) * 256],
                        HP[:, hc * 256:(hc + 1) * 256],
                        AF.Gelu, bias=m_b1h[:, hc:hc + 1]))
                    for j in range(8)
                ], waits=[sc1])
                if r == 0:
                    m1_done = s.stage("vector", [
                        lambda e: e.tensor_copy(HP[:, 8192:10240], HP[:, 0:2048]),
                    ], waits=[gl])
                else:
                    m1_done = s.stage("vector", [
                        (lambda e, r=r: e.tensor_max(
                            HP[:, 8192:10240], HP[:, 8192:10240],
                            HP[:, r * 2048:(r + 1) * 2048])),
                    ], waits=[gl, m1_done])
            m1_done = s.stage("vector", [
                lambda e: e.tensor_max(HP[:, 8192:9216], HP[:, 8192:9216],
                                       HP[:, 9216:10240]),
            ], waits=[m1_done])
            m1_done = s.stage("vector", [
                lambda e: e.tensor_max(HP[:, 8192:8704], HP[:, 8192:8704],
                                       HP[:, 8704:9216]),
            ])
            m1_done = s.stage("vector", [
                lambda e: e.tensor_max(HP[:, 8192:8448], HP[:, 8192:8448],
                                       HP[:, 8448:8704]),
            ])
            m132 = s.stage("tensor", [
                (lambda e, tc=tc: e.transpose(
                    CTP[:, tc * 128:(tc + 1) * 128],
                    HP[:, 8192 + tc * 128: 8192 + (tc + 1) * 128], IDN32[:]))
                for tc in range(TC2)
            ], waits=[m1_done])
            s.stage("vector", [
                (lambda e, tc=tc: e.reduce_max(
                    m_rx2[:, tc:tc + 1], CTP[:, tc * 128:(tc + 1) * 128], AX.X))
                for tc in range(TC2)
            ], waits=[m132])
            s.stage("vector", [
                lambda e: e.tensor_scalar(m_mx2[:, 0:2], m_rx2[:, 0:2],
                                          1e-5, 1.0 / 127.0, ALU.max, ALU.mult),
            ])
            s.stage("vector", [
                lambda e: e.reciprocal(m_s2c[:, 0:2], m_mx2[:, 0:2]),
                lambda e: e.tensor_single_scalar(m_s2i[:, 0:2], m_mx2[:, 0:2],
                                                 m_mw2d[:, 0:1], ALU.mult),
            ])
            rx2 = s.stage("vector", [
                lambda e: e.tensor_single_scalar(m_f2[:, 0:2], m_s2c[:, 0:2],
                                                 m_sw2b[:, 0:1], ALU.mult),
            ])
            s2o = s.dmas("gpsimd", [
                lambda e: e.dma_start(out=rmio[:, 0:2], in_=m_s2c[:, 0:2]),
                lambda e: e.dma_start(out=rmio[:, 2:4], in_=m_f2[:, 0:2]),
            ], waits=[rx2], stream="dgo")
            s2i_ = s.dmas("sync", [
                lambda e: e.dma_start(out=m_s1row[:, 0:256].rearrange("a (c p) -> a c p", p=128),
                                      in_=rmio[:, 0:2].rearrange("p c -> c p")[None],
                                      allow_slow_non_contiguous=True),
            ], waits=[s2o], stream="dsb")
            f2i_ = s.dmas("gpsimd", [
                lambda e: e.dma_start(out=r_f2[:, 0:256].rearrange("a (c p) -> a c p", p=128),
                                      in_=rmio[:, 2:4].rearrange("p c -> c p")[None],
                                      allow_slow_non_contiguous=True),
            ], waits=[s2o], stream="dgo")
            sbm2 = s.stage("tensor", [
                (lambda e, tc=tc: e.matmul(
                    CTP[:, tc * 128:(tc + 1) * 128], o32row,
                    m_s1row[:, tc * 128:(tc + 1) * 128],
                    start=(tc == 0), stop=(tc == TC2 - 1)))
                for tc in range(TC2)
            ], waits=[s2i_, rx2])
            sbe2 = s.stage("vector", [
                lambda e: e.tensor_copy(TPf[:, 256:512], CTP[:, 0:256]),
            ], waits=[sbm2])
            hq_prev = None
            for k0 in range(0, HC, 8):
                wq_ = [sbe2]
                if hq_prev is not None:
                    wq_.append(hq_prev)
                mul = s.stage("vector", [
                    (lambda e, kc=k0 + i: e.tensor_mul(
                        HP[:, kc * 256:(kc + 1) * 256],
                        HP[:, kc * 256:(kc + 1) * 256], TPf[:, 256:512]))
                    for i in range(8)
                ], waits=wq_)
                def _adds(kcs):
                    return [
                        (lambda e, kc=kc: e.tensor_scalar_add(
                            HPf[:, kc * 256:(kc + 1) * 256],
                            HP[:, kc * 256:(kc + 1) * 256], 1536.0))
                        for kc in kcs
                    ]
                if k0 == 0:
                    s.stage("vector", [
                        lambda e: e.tensor_scalar_add(TP[:, 0:256], HP[:, 0:256], 1536.0),
                    ], waits=[mul])
                    s.stage("vector", _adds([1]) + [
                        lambda e: e.tensor_scalar_add(TP[:, 0:256], TP[:, 0:256], -1536.0),
                    ])
                    s.stage("vector", _adds([2, 3]))
                    hq_prev = s.stage("vector", _adds([4, 5, 6, 7]))
                else:
                    hq_prev = s.stage("vector", _adds(range(k0, k0 + 8)), waits=[mul])
            hq2 = s.stage("vector", [
                lambda e: e.tensor_scalar_add(HPf[:, 256:4096], HPf[:, 256:4096], -1536.0),
                lambda e: e.tensor_scalar_add(HPf[:, 4096:8192], HPf[:, 4096:8192], -1536.0),
            ], waits=[hq_prev])
            last_ev1 = None
            for tc in range(TC2):
                g = t2 * TC2 + tc
                x2l2 = s.dmas("sync", [
                    (lambda e, g=g: e.dma_start(
                        out=TPf[:, 2048:3072],
                        in_=x2s[g * 128:(g + 1) * 128, 0:1024])),
                ], waits=[hq2] if tc == 0 else [prev_store], stream="dxa")
                oev = []
                for half in range(2):
                    pab = PAB_A if half == 0 else PAB_B
                    wv = [hq2, w2ab, w2cd, f2i_, b2_cast] if tc == 0 else [hq2]
                    if last_ev1 is not None:
                        wv.append(last_ev1)
                    mmo = s.stage("tensor", [
                        (lambda e, kc=kc, tc=tc, half=half, pab=pab: e.matmul(
                            pab[:, 0:512],
                            (TP[:, tc * 128: tc * 128 + 128] if kc == 0 else
                             HPf[:, kc * 256 + tc * 128: kc * 256 + tc * 128 + 128]),
                            WP[:, w2t_region(kc) + half * 512: w2t_region(kc) + (half + 1) * 512],
                            start=(kc == 0), stop=False))
                        for kc in range(HC)
                    ] + [
                        (lambda e, tc=tc, half=half, pab=pab: e.matmul(
                            pab[:, 0:512],
                            r_f2[:, tc * 128:(tc + 1) * 128],
                            r_b2[:, half * 512:(half + 1) * 512],
                            start=False, stop=True, tile_position=(64, 0))),
                    ], waits=wv)
                    ev1w = [mmo]
                    if oev:
                        ev1w.append(oev[-1])
                    elif prev_oev is not None:
                        ev1w.append(prev_oev)
                    ev1 = s.stage("scalar", [
                        (lambda e, tc=tc, half=half, pab=pab: e.activation(
                            TPf[:, 1024 + half * 512: 1024 + (half + 1) * 512],
                            pab[:, 0:512], AF.Copy, scale=m_s2i[:, tc:tc + 1])),
                    ], waits=ev1w)
                    last_ev1 = ev1
                    oev.append(s.stage("vector", [
                        (lambda e, half=half: e.tensor_add(
                            TPf[:, 3072 + half * 512: 3072 + (half + 1) * 512],
                            TPf[:, 1024 + half * 512: 1024 + (half + 1) * 512],
                            TPf[:, 2048 + half * 512: 2048 + (half + 1) * 512])),
                    ], waits=[ev1, x2l2]))
                prev_oev = oev[-1]
                prev_store = s.dmas("gpsimd", [
                    (lambda e, g=g: e.dma_start(
                        out=outd[g * 128:(g + 1) * 128, :],
                        in_=TPf[:, 3072:4096]))
                ], waits=[oev[0], oev[1]])

        s.mark("all")
        import os
        trunc = os.environ.get("MK_TRUNCATE", "")
        if trunc:
            s.truncate(trunc)
        fin = s.barrier()
        s.prog["gpsimd"].append(("fin", fin, [], None))

        with nc.Block() as block:
            def mk(engname):
                def body(eng):
                    hw = {}
                    for kind, waits, fns, inc in s.prog[engname]:
                        for (nm, val) in waits:
                            if hw.get(nm, -1) >= val:
                                continue
                            hw[nm] = val
                            eng.wait_ge(sems[nm], val)
                        if kind == "fin":
                            continue
                        if kind == "ccstage":
                            last = None
                            for fn in fns:
                                last = fn(eng)
                            nm, v = inc
                            last.then_inc(sems[nm], 1)
                        elif kind == "pestage":
                            last = None
                            for fn in fns:
                                last = fn(eng)
                            nm, v = inc
                            last.then_inc(sems[nm], 1)
                        elif kind == "stage":
                            for fn in fns:
                                fn(eng)
                            nm, v = inc
                            eng.drain().then_inc(sems[nm], 1)
                        else:
                            nm, v = inc
                            for fn in fns:
                                fn(eng).then_inc(sems[nm], 16)
                return body
            block.sync(mk("sync"))
            block.vector(mk("vector"))
            block.scalar(mk("scalar"))
            block.tensor(mk("tensor"))
            block.gpsimd(mk("gpsimd"))
    return nc


_NC_CACHE = {}


def kernel(**inputs):
    if "nc" not in _NC_CACHE:
        _NC_CACHE["nc"] = build_program()
    nc = _NC_CACHE["nc"]
    x = np.ascontiguousarray(np.asarray(inputs["L0_attn_x"], dtype=np.float32))
    f32 = lambda k: np.ascontiguousarray(np.asarray(inputs[k], dtype=np.float32))
    shared = {
        "Wq": f32("L0_attn_Wq"), "Wk": f32("L0_attn_Wk"),
        "Wv": f32("L0_attn_Wv"), "Wo": f32("L0_attn_Wo"),
        "bq": f32("L0_attn_bq"), "bk": f32("L0_attn_bk"),
        "bv": f32("L0_attn_bv"), "bo": f32("L0_attn_bo"),
        "proj": f32("L0_attn_proj"),
        "w1": f32("L0_ffn_w1"), "b1": f32("L0_ffn_b1"),
        "w2": f32("L0_ffn_w2"), "b2": f32("L0_ffn_b2"),
        "gamma": f32("L0_ffn_gamma"), "beta": f32("L0_ffn_beta"),
    }
    in_maps = [dict(shared, x=x[c]) for c in range(NCORES)]
    res = run_bass_kernel_spmd(nc, in_maps, core_ids=list(range(NCORES)))
    return np.stack([np.asarray(res.results[c]["out"]) for c in range(NCORES)],
                    axis=0)


# version 13
# speedup vs baseline: 1.1042x; 1.0157x over previous
import contextlib
import numpy as np
import concourse.bass as bass
import concourse.mybir as mybir
from concourse.bass_utils import run_bass_kernel_spmd

F32 = mybir.dt.float32
F16 = mybir.dt.float16
AF = mybir.ActivationFunctionType
ALU = mybir.AluOpType
AX = mybir.AxisListType

NCORES = 8
N = 4096
E = 1024
H = 16
M = 256
HID = 4096
T = 512
NT = N // T
TC = T // 128
EC = E // 128
HC = HID // 128
CSTAB = 20.0
KEPS = 1e-4
LNEPS = 1e-5
CQ = float(64.0 ** -0.25)

ENGINES = ("sync", "vector", "scalar", "tensor", "gpsimd")


class Sched:
    DSTREAMS = ("dxa", "dwt", "dwq", "dsb", "dst", "dgo", "dpr", "dscalar")

    def __init__(self):
        self.prog = {e: [] for e in ENGINES}
        self.cnt = {e: 0 for e in ENGINES}
        self.dma = {k: 0 for k in self.DSTREAMS}
        self.cc = 0

    def _rw(self, waits):
        out = []
        for w in waits:
            if not w:
                continue
            nm, v = w
            if nm in self.dma:
                v = self.dma[nm]
                if v == 0:
                    continue
            out.append((nm, v))
        return out

    def stage(self, eng, fns, waits=()):
        w = self._rw(waits)
        if self.cnt[eng] > 0:
            w.append((eng, self.cnt[eng]))
        self.cnt[eng] += 1
        kind = "pestage"
        self.prog[eng].append((kind, w, fns, (eng, self.cnt[eng])))
        return (eng, self.cnt[eng])

    def dmas(self, eng, fns, waits=(), stream=None):
        nm = stream or ("dscalar" if eng == "scalar" else
                        ("dst" if eng == "gpsimd" else "dwt"))
        w = self._rw(waits)
        self.dma[nm] += 16 * len(fns)
        self.prog[eng].append(("dma", w, fns, (nm, self.dma[nm])))
        return (nm, self.dma[nm])

    def collective(self, fn, waits=()):
        self.cc += 1
        self.prog["gpsimd"].append(("ccstage", self._rw(waits), [fn],
                                    ("cc", self.cc)))
        return ("cc", self.cc)

    def mark(self, name):
        self.marks = getattr(self, "marks", {})
        self.marks[name] = {e: len(self.prog[e]) for e in ENGINES}

    def truncate(self, name):
        for e in ENGINES:
            self.prog[e] = self.prog[e][: self.marks[name][e]]
        self.cnt = {e: 0 for e in ENGINES}
        self.dma = {k: 0 for k in self.DSTREAMS}
        self.cc = 0
        for e in ENGINES:
            for kind, waits, fns, inc in self.prog[e]:
                if inc is None:
                    continue
                nm, v = inc
                if nm in self.dma:
                    self.dma[nm] = max(self.dma[nm], v)
                elif nm == "cc":
                    self.cc = max(self.cc, v)
                else:
                    self.cnt[nm] = max(self.cnt[nm], v)

    def barrier(self):
        w = [(e, self.cnt[e]) for e in ENGINES if self.cnt[e] > 0]
        w += [(k, v) for k, v in self.dma.items() if v > 0]
        if self.cc:
            w.append(("cc", self.cc))
        return w


def build_program():
    nc = bass.Bass("TRN2", target_bir_lowering=False, debug=False)

    xin = nc.dram_tensor("x", [N, E], F32, kind="ExternalInput").ap()
    Wq = nc.dram_tensor("Wq", [E, E], F32, kind="ExternalInput").ap()
    Wk = nc.dram_tensor("Wk", [E, E], F32, kind="ExternalInput").ap()
    Wv = nc.dram_tensor("Wv", [E, E], F32, kind="ExternalInput").ap()
    Wo = nc.dram_tensor("Wo", [E, E], F32, kind="ExternalInput").ap()
    bqd = nc.dram_tensor("bq", [E], F32, kind="ExternalInput").ap()
    bkd = nc.dram_tensor("bk", [E], F32, kind="ExternalInput").ap()
    bvd = nc.dram_tensor("bv", [E], F32, kind="ExternalInput").ap()
    bod = nc.dram_tensor("bo", [E], F32, kind="ExternalInput").ap()
    projd = nc.dram_tensor("proj", [M, 64], F32, kind="ExternalInput").ap()
    w1d = nc.dram_tensor("w1", [HID, E], F32, kind="ExternalInput").ap()
    b1d = nc.dram_tensor("b1", [HID], F32, kind="ExternalInput").ap()
    w2d = nc.dram_tensor("w2", [E, HID], F32, kind="ExternalInput").ap()
    b2d = nc.dram_tensor("b2", [E], F32, kind="ExternalInput").ap()
    gammad = nc.dram_tensor("gamma", [E], F32, kind="ExternalInput").ap()
    betad = nc.dram_tensor("beta", [E], F32, kind="ExternalInput").ap()
    outd = nc.dram_tensor("out", [N, E], F32, kind="ExternalOutput").ap()

    x2s = nc.dram_tensor("x2s", [N, E], F32)
    wsw = nc.dram_tensor("wsw", [4 * E, E], F16)
    wsp = nc.dram_tensor("wsp", [M, 128], F16)
    ws1 = nc.dram_tensor("ws1", [HID, E], F16)
    ws2 = nc.dram_tensor("ws2", [E, HID], F16)
    rmio = nc.dram_tensor("rmio", [128, 16], F32)
    rmo = nc.dram_tensor("rmo", [128, 16], F32)
    srow = nc.dram_tensor("srow", [128, 1], F32)

    identd = nc.inline_tensor(np.eye(128, dtype=np.float16), "identc")
    ident32d = nc.inline_tensor(np.eye(128, dtype=np.float32), "ident32c")
    ind_np = np.zeros((128, 2), dtype=np.float16)
    ind_np[0:64, 0] = 1.0
    ind_np[64:128, 1] = 1.0
    indd = nc.inline_tensor(ind_np, "indc")

    s = Sched()
    es = contextlib.ExitStack()
    with es:
        sbuf = lambda n_, sh, dt_: es.enter_context(nc.sbuf_tensor(n_, sh, dt_))
        WP = sbuf("WP", [128, 65536], F16)
        HP = sbuf("HP", [128, 12320], F32)
        TP = sbuf("TP", [128, 9472], F16)
        MS = sbuf("MS", [128, 872], F32)
        CXP = sbuf("CXP", [128, 2080], F16)
        BR = sbuf("BR", [128, 2193], F16)
        IDN = sbuf("IDN", [128, 128], F16)
        IDN32 = sbuf("IDN32", [128, 128], F32)
        IND = sbuf("IND", [128, 2], F16)
        PJ = sbuf("PJ", [128, 256], F16)
        KC = sbuf("KC", [128, 8], F32)

        PAB_A = es.enter_context(nc.psum_tensor("PAB_A", [128, 512], F32))
        PAB_B = es.enter_context(nc.psum_tensor("PAB_B", [128, 512], F32))
        DDP = es.enter_context(nc.psum_tensor("DDP", [128, 2048], F32))
        CTP = es.enter_context(nc.psum_tensor("CTP", [128, 1024], F32))
        PA16 = PAB_A.bitcast(F16)
        PB16 = PAB_B.bitcast(F16)

        sems = {}
        for nm in ("sync", "vector", "scalar", "tensor", "gpsimd",
                   "dxa", "dwt", "dwq", "dsb", "dst", "dgo", "dpr", "dscalar", "cc"):
            sems[nm] = es.enter_context(nc.semaphore(nm))

        TPf = TP.bitcast(F32)
        HPf = HP.bitcast(F16)

        RA, RB, RXL, RC, RD = 0, 8192, 16384, 49152, 57344
        KN = 4352

        def w2t_region(kc):
            return [RA, RB, RC, RD][kc // 8] + (kc % 8) * 1024

        m_bn = MS[:, 0:12]
        m_mv = MS[:, 16:24]
        m_sd = MS[:, 24:28]
        m_rstd = MS[:, 28:32]
        m_negm = MS[:, 32:36]
        m_cbk = MS[:, 36:44]
        m_cbq = MS[:, 44:52]
        m_b1h = MS[:, 52:84]
        m_RM = MS[:, 84:100]
        m_diag = MS[:, 100:164]
        m_asum = MS[:, 800:864]
        m_nst = MS[:, 164:228]
        m_edb = MS[:, 228:292]
        m_stab = MS[:, 292:293]
        m_cstab = MS[:, 293:294]
        m_red2 = MS[:, 296:298]
        m_red4 = MS[:, 294:296]
        m_dinv = MS[:, 300:304]
        m_wred = MS[:, 304:306]
        m_sca = MS[0:1, 306:316]
        m_rx1 = MS[:, 320:321]
        m_mx1 = MS[:, 321:322]
        m_s1c = MS[:, 322:323]
        m_s1i = MS[:, 324:328]
        m_rx2 = MS[:, 328:332]
        m_mx2 = MS[:, 332:336]
        m_s2c = MS[:, 336:340]
        m_s2i = MS[:, 340:344]
        m_f2 = MS[:, 344:348]
        m_sw1b = MS[:, 348:349]
        m_sw2b = MS[:, 349:350]
        m_mw1db = MS[:, 350:351]
        m_mw2db = MS[:, 351:352]
        o32row = MS[0:1, 416:544]
        m_bn2 = MS[:, 352:376]
        m_mv2 = MS[:, 376:380]
        m_rstd2 = MS[:, 380:382]
        m_sd2 = MS[:, 390:392]
        m_rx1b = MS[:, 382:384]
        m_mx1b = MS[:, 384:386]
        m_s1cb = MS[:, 386:388]
        m_mw1d = MS[:, 392:393]
        m_mw2d = MS[:, 393:394]
        m_L1col = MS[:, 394:395]
        m_L1 = MS[0:1, 395:396]
        m_cLt = MS[:, 396:397]
        m_mLt = MS[:, 397:398]
        m_eD = MS[:, 398:399]
        m_Lprev = MS[:, 399:400]
        m_mS = MS[:, 400:401]
        m_f = MS[:, 401:402]
        m_sx = MS[:, 402:410]
        m_dinv8 = MS[:, 864:872]
        m_srow = MS[0:1, 544:672]
        m_s1row = MS[0:1, 544:800]

        r_bv = BR[0:1, 0:1024]
        ones16row = BR[0:1, 1024:1152]
        r_bo = BR[32:33, 0:1024]
        ones16row32 = BR[32:33, 1024:1152]
        r_f2 = BR[64:65, 0:512]
        r_b2 = BR[64:65, 512:1536]
        r_eps = BR[32:33, 1152:2192]
        ones16col = BR[:, 2192:2193]
        k4096h = BR[0:1, 1536:1537]

        k1536 = KC[:, 0:1]
        kmC = KC[:, 1:2]
        kpC = KC[:, 2:3]
        kCQ = KC[:, 3:4]
        kmhalf = KC[:, 4:5]
        kones32 = KC[:, 5:6]
        kLNEPS = KC[:, 6:7]

        def dma1(eng, outap, inap, waits=(), stream=None, **kw):
            return s.dmas(eng, [lambda e, o=outap, i=inap, kw=kw:
                                e.dma_start(out=o, in_=i, **kw)], waits=waits,
                          stream=stream)

        s.dmas("sync", [
            lambda e: e.dma_start(out=IDN[:], in_=identd.ap()),
            lambda e: e.dma_start(out=IDN32[:], in_=ident32d.ap()),
            lambda e: e.dma_start(out=IND[:], in_=indd.ap()),
        ])
        st_ones = s.stage("vector", [
            lambda e: e.memset(KC[:, 0:1], 1536.0),
            lambda e: e.memset(KC[:, 1:2], -CSTAB),
            lambda e: e.memset(KC[:, 2:3], CSTAB),
            lambda e: e.memset(KC[:, 3:4], CQ),
            lambda e: e.memset(KC[:, 4:5], -0.5),
            lambda e: e.memset(KC[:, 5:6], 1.0),
            lambda e: e.memset(KC[:, 6:7], LNEPS),
            lambda e: e.memset(o32row, 1.0),
            lambda e: e.memset(ones16row, 1.0),
            lambda e: e.memset(ones16row32, 1.0),
            lambda e: e.memset(ones16col, 1.0),
            lambda e: e.memset(m_RM, -3.0e38),
            lambda e: e.memset(KC[:, 7:8], 4096.0),
            lambda e: e.memset(k4096h, 4096.0),
            lambda e: e.memset(m_Lprev, -1.0e30),
        ])

        row_ld = s.dmas("gpsimd", [
            lambda e: e.dma_start(out=r_bv, in_=bvd[None, :]),
            lambda e: e.dma_start(out=r_bo, in_=bod[None, :]),
        ], stream="dpr")
        wlist = [(Wk, RA), (Wv, RB), (Wq, RC), (Wo, RD)]
        cast_done = []
        for wi, (wap, dst) in enumerate(wlist):
            cast_done.append(dma1("gpsimd", wsw[wi * E:(wi + 1) * E, :], wap, stream="dpr"))
        proj_cast = s.dmas("gpsimd", [
            lambda e: e.dma_start(out=wsp[:, 0:64], in_=projd),
            lambda e: e.dma_start(out=wsp[:, 64:128], in_=projd),
        ], stream="dpr")
        b2_cast = dma1("gpsimd", r_b2, b2d[None, :], stream="dpr")
        wt_loads = []
        for wi, (wap, dst) in enumerate(wlist):
            fns = []
            for kc in range(EC):
                fns.append(lambda e, wi=wi, kc=kc, dst=dst: e.dma_start(
                    out=WP[:, dst + kc * 1024: dst + (kc + 1) * 1024],
                    in_=wsw[wi * E:(wi + 1) * E, kc * 128:(kc + 1) * 128],
                    transpose=True))
            wt_loads.append(s.dmas("sync", fns, waits=[cast_done[wi]]))
        pj_load = s.dmas("sync", [
            lambda e: e.dma_start(out=PJ[:, :], in_=wsp[:, :], transpose=True),
        ], waits=[proj_cast])

        bias_ld = s.dmas("sync", [
            lambda e: e.dma_start(out=m_cbk, in_=bkd.rearrange("(c p) -> p c", p=128),
                                  allow_slow_non_contiguous=True),
            lambda e: e.dma_start(out=m_cbq, in_=bqd.rearrange("(c p) -> p c", p=128),
                                  allow_slow_non_contiguous=True),
            lambda e: e.dma_start(out=m_b1h, in_=b1d.rearrange("(c p) -> p c", p=128),
                                  allow_slow_non_contiguous=True),
        ])
        sc_bias = s.stage("vector", [
            lambda e: e.tensor_scalar_mul(m_cbk, m_cbk, CQ),
            lambda e: e.tensor_scalar_mul(m_cbq, m_cbq, CQ),
        ], waits=[bias_ld])

        def emit_ln(src_of_tc, dst_of_tc, wait_load, dst_w=()):
            for tc in range(TC):
                s.stage("vector", [
                    (lambda e, tc=tc: e.bn_stats(m_bn[:, 0:6], src_of_tc(tc)[:, 0:512])),
                    (lambda e, tc=tc: e.bn_stats(m_bn[:, 6:12], src_of_tc(tc)[:, 512:1024])),
                ], waits=[wait_load] if tc == 0 else [])
                s.stage("vector", [
                    (lambda e, tc=tc: e.bn_aggr(m_mv[:, 2 * tc:2 * tc + 2], m_bn[:, 0:12])),
                ])
            veps = s.stage("vector", [
                lambda e: e.tensor_scalar_add(m_sd, m_mv[:, 1:8:2], LNEPS)])
            ssq = s.stage("scalar", [
                lambda e: e.activation(m_sd, m_sd, AF.Sqrt)], waits=[veps])
            s.stage("vector", [
                lambda e: e.reciprocal(m_rstd, m_sd)], waits=[ssq])
            vnm = s.stage("vector", [
                lambda e: e.scalar_tensor_tensor(m_negm, m_mv[:, 0:8:2], -1.0,
                                                 m_rstd, ALU.mult, ALU.mult)])
            last = None
            for tc in range(TC):
                last = s.stage("scalar", [
                    (lambda e, tc=tc: e.activation(
                        dst_of_tc(tc), src_of_tc(tc), AF.Identity,
                        bias=m_negm[:, tc:tc + 1], scale=m_rstd[:, tc:tc + 1])),
                ], waits=[vnm] + list(dst_w) if tc == 0 else [])
            return last

        def emit_tblock(src_of_k, dst_of_k, nk, wait_src, ps16, extra_w=()):
            last_ev = None
            for g0 in range(0, nk, 8):
                gn = min(8, nk - g0)
                w = [wait_src] + list(extra_w)
                if last_ev is not None:
                    w.append(last_ev)
                tp_ = s.stage("tensor", [
                    (lambda e, k=g0 + i, sl=i: e.transpose(
                        ps16[:, sl * 128:(sl + 1) * 128], src_of_k(k), IDN[:]))
                    for i in range(gn)
                ], waits=w)
                last_ev = s.stage("vector", [
                    (lambda e, k=g0 + i, sl=i: e.tensor_copy(
                        dst_of_k(k), ps16[:, sl * 128:(sl + 1) * 128]))
                    for i in range(gn)
                ], waits=[tp_])
            return last_ev

        def emit_proj(wreg, cb_col, t, dst_of_ec, waits):
            evs = []
            for ec in range(EC):
                pab = PAB_A if ec % 2 == 0 else PAB_B
                w = list(waits) if ec <= 1 else []
                if len(evs) >= 2:
                    w.append(evs[-2])
                mm = s.stage("tensor", [
                    (lambda e, kc=kc, ec=ec, pab=pab: e.matmul(
                        pab[:, 0:512],
                        WP[:, wreg + kc * 1024 + ec * 128: wreg + kc * 1024 + ec * 128 + 128],
                        WP[:, RXL + kc * 4096 + t * 512: RXL + kc * 4096 + (t + 1) * 512],
                        start=(kc == 0), stop=(kc == EC - 1)))
                    for kc in range(EC)
                ], waits=w)
                evs.append(s.stage("vector", [
                    (lambda e, ec=ec, pab=pab: e.tensor_scalar(
                        dst_of_ec(ec), pab[:, 0:512], CQ,
                        cb_col[:, ec:ec + 1], ALU.mult, ALU.add)),
                ], waits=[mm]))
            return evs[-1]

        def emit_diag(src_off, wait_kn, extra_first=()):
            dmm = []
            for ec in range(EC):
                p = ec % 2
                wsq = [wait_kn] if ec == 0 else []
                if len(dmm) >= 2:
                    wsq.append(dmm[ec - 2])
                sq_ = s.stage("vector", [
                    (lambda e, ec=ec, p=p: e.tensor_mul(
                        TP[:, 8448 + p * 512: 8448 + (p + 1) * 512],
                        TP[:, src_off + ec * 512: src_off + (ec + 1) * 512],
                        TP[:, src_off + ec * 512: src_off + (ec + 1) * 512])),
                ], waits=wsq)
                wm = [sq_]
                if ec == 0:
                    wm += list(extra_first)
                dmm.append(s.stage("tensor", [
                    (lambda e, ec=ec, tc=tc, p=p: e.matmul(
                        DDP[:, tc * 16 + 2 * ec: tc * 16 + 2 * ec + 2],
                        TP[:, 8448 + p * 512 + tc * 128: 8448 + p * 512 + tc * 128 + 128],
                        IND[:], start=(ec == 0 and tc == 0),
                        stop=(ec == EC - 1 and tc == TC - 1)))
                    for tc in range(TC)
                ], waits=wm))
            return s.stage("vector", [
                lambda e: e.tensor_copy(m_diag, DDP[:, 0:64])], waits=[dmm[-1]])

        wq_items = ([("w1", ch, None) for ch in range(HC)] +
                    [("w2", ch, ss) for ch in range(EC) for ss in range(4)])
        wq1 = {"i": 0, "last": None}
        wq2 = {"i": 0, "last": None, "mean": None}
        wq_stg = HP[:, 4096:5120]
        wq_ta = HP[:, 5120:6144]
        wq_tb = HP[:, 6144:7168]
        wq_o16 = HPf[:, 14336:15360]

        def wq_src(it):
            wname, ch, ss = it
            if wname == "w1":
                return w1d[ch * 128:(ch + 1) * 128, :], m_asum[:, ch:ch + 1], m_sw1b
            return (w2d[ch * 128:(ch + 1) * 128, ss * 1024:(ss + 1) * 1024],
                    m_asum[:, 32 + ch * 4 + ss: 33 + ch * 4 + ss], m_sw2b)

        def wq_dst(it):
            wname, ch, ss = it
            if wname == "w1":
                return ws1[ch * 128:(ch + 1) * 128, :]
            return ws2[ch * 128:(ch + 1) * 128, ss * 1024:(ss + 1) * 1024]

        def emit_wq_pass1(n):
            for _ in range(n):
                i = wq1["i"]
                if i >= len(wq_items):
                    return
                wq1["i"] += 1
                src_, acol, _ = wq_src(wq_items[i])
                w = [wq1["last"]] if wq1["last"] else []
                ld = dma1("sync", wq_stg, src_, waits=w, stream="dwq")
                wq1["last"] = s.stage("scalar", [
                    (lambda e, acol=acol: e.activation(
                        wq_stg, wq_stg, AF.Abs, accum_out=acol)),
                ], waits=[ld])

        def emit_wq_mean():
            red = s.stage("vector", [
                lambda e: e.reduce_sum(m_wred[:, 0:1], m_asum[:, 0:32], AX.X),
                lambda e: e.reduce_sum(m_wred[:, 1:2], m_asum[:, 32:64], AX.X),
            ], waits=[wq1["last"]])
            pm = s.stage("tensor", [
                lambda e: e.matmul(CTP[0:1, 0:1], kones32, m_wred[:, 0:1],
                                   start=True, stop=False),
                lambda e: e.matmul(CTP[0:1, 1:2], kones32, m_wred[:, 1:2],
                                   start=False, stop=True),
            ], waits=[red])
            s.stage("vector", [
                lambda e: e.tensor_scalar(m_sca[:, 0:2], CTP[0:1, 0:2],
                                          1.0 / (HID * E), 1e-5, ALU.mult, ALU.max),
            ], waits=[pm])
            sc = s.stage("vector", [
                lambda e: e.reciprocal(m_sca[:, 2:4], m_sca[:, 0:2]),
                lambda e: e.tensor_scalar_mul(m_sca[:, 4:6], m_sca[:, 0:2], 1.0 / 127.0),
            ])
            bc = s.stage("tensor", [
                lambda e: e.matmul(CTP[:, 0:1], o32row, m_sca[:, 2:3],
                                   start=True, stop=False),
                lambda e: e.matmul(CTP[:, 1:2], o32row, m_sca[:, 3:4],
                                   start=False, stop=False),
                lambda e: e.matmul(CTP[:, 2:3], o32row, m_sca[:, 4:5],
                                   start=False, stop=False),
                lambda e: e.matmul(CTP[:, 3:4], o32row, m_sca[:, 5:6],
                                   start=False, stop=False),
                lambda e: e.matmul(CTP[:, 4:5], o32row, m_sca[:, 0:1],
                                   start=False, stop=False),
                lambda e: e.matmul(CTP[:, 5:6], o32row, m_sca[:, 1:2],
                                   start=False, stop=True),
            ], waits=[sc])
            wq2["mean"] = s.stage("vector", [
                lambda e: e.tensor_copy(m_sw1b, CTP[:, 0:1]),
                lambda e: e.tensor_copy(m_sw2b, CTP[:, 1:2]),
                lambda e: e.tensor_copy(m_mw1db, CTP[:, 2:3]),
                lambda e: e.tensor_copy(m_mw2db, CTP[:, 3:4]),
                lambda e: e.tensor_copy(m_mw1d, CTP[:, 4:5]),
                lambda e: e.tensor_copy(m_mw2d, CTP[:, 5:6]),
            ], waits=[bc])

        def emit_wq_pass2(n):
            for _ in range(n):
                i = wq2["i"]
                if i >= len(wq_items):
                    return
                wq2["i"] += 1
                it = wq_items[i]
                src_, _, swb = wq_src(it)
                w = [wq2["mean"]]
                if wq2["last"]:
                    w.append(wq2["last"])
                ld = dma1("sync", wq_stg, src_, waits=w, stream="dwq")
                g1 = s.stage("vector", [
                    (lambda e, swb=swb: e.tensor_scalar(
                        wq_ta, wq_stg, swb[:, 0:1], 0.5, ALU.mult, ALU.is_gt)),
                    (lambda e, swb=swb: e.tensor_scalar(
                        wq_tb, wq_stg, swb[:, 0:1], -0.5, ALU.mult, ALU.is_lt)),
                ], waits=[ld] + ([wq2["last"]] if wq2["last"] else []))
                g2 = s.stage("vector", [
                    lambda e: e.tensor_sub(wq_o16, wq_ta, wq_tb),
                ])
                wq2["last"] = dma1("gpsimd", wq_dst(it), wq_o16, waits=[g2],
                                   stream="dgo")

        s.mark("prologue")
        prol_w = s.barrier()
        ctx_init = s.stage("vector", [
            lambda e: e.memset(HP[:, 8192:10272], 0.0),
        ], waits=prol_w)
        ctx_last = None
        last_xuse = None
        prev_trd = None
        prev_s1red = None
        for t in range(NT):
            w = list(prol_w) if t == 0 else []
            if last_xuse is not None:
                w.append(last_xuse)
            xld = s.dmas("sync", [
                (lambda e, t=t: e.dma_start(
                    out=HP[:, 0:4096].rearrange("p (a m) -> p a m", a=TC),
                    in_=xin[t * 512:(t + 1) * 512, :].rearrange("(a p) m -> p a m", p=128)))
            ], waits=w, stream="dxa")
            lnd = emit_ln(lambda tc: HP[:, tc * 1024:(tc + 1) * 1024],
                          lambda tc: HPf[:, 20544 + tc * 1024: 20544 + (tc + 1) * 1024],
                          xld, dst_w=[prev_trd] if prev_trd else [])
            last_xuse = lnd
            trd = emit_tblock(
                lambda k: HPf[:, 20544 + (k // EC) * 1024 + (k % EC) * 128:
                              20544 + (k // EC) * 1024 + (k % EC) * 128 + 128],
                lambda k, t=t: WP[:, RXL + (k % EC) * 4096 + t * 512 + (k // EC) * 128:
                                  RXL + (k % EC) * 4096 + t * 512 + (k // EC) * 128 + 128],
                TC * EC, lnd, PA16,
                extra_w=[prev_s1red] if prev_s1red else [])
            prev_trd = trd
            if t == 0:
                s.mark("p1tr")
            kev = emit_proj(RA, m_cbk, t,
                            lambda ec: TP[:, KN + ec * 512: KN + (ec + 1) * 512],
                            [trd, wt_loads[0], sc_bias, st_ones] if t == 0
                            else [trd])
            if t == 0:
                s.mark("p1kp")
            vev = []
            for tc in range(TC):
                for half in range(2):
                    pab = PAB_A if half == 0 else PAB_B
                    wv = [kev] if tc == 0 else []
                    if t == 0 and tc == 0 and half == 0:
                        wv += [wt_loads[1], row_ld]
                    if len(vev) >= 2:
                        wv.append(vev[-2])
                    mmv = s.stage("tensor", [
                        (lambda e, kc=kc, tc=tc, half=half, pab=pab, t=t: e.matmul(
                            pab[:, 0:512],
                            WP[:, RXL + kc * 4096 + t * 512 + tc * 128:
                               RXL + kc * 4096 + t * 512 + tc * 128 + 128],
                            WP[:, RB + kc * 1024 + half * 512: RB + kc * 1024 + (half + 1) * 512],
                            start=(kc == 0), stop=False))
                        for kc in range(EC)
                    ] + [
                        (lambda e, half=half, pab=pab: e.matmul(
                            pab[:, 0:512], ones16row,
                            BR[0:1, half * 512:(half + 1) * 512],
                            start=False, stop=True)),
                    ], waits=wv)
                    vev.append(s.stage("vector", [
                        (lambda e, tc=tc, half=half, pab=pab: e.tensor_copy(
                            TP[:, tc * 1040 + half * 520: tc * 1040 + half * 520 + 520]
                            .rearrange("p (h d) -> p h d", d=65)[:, :, 0:64],
                            pab[:, 0:512].rearrange("p (h d) -> p h d", d=64))),
                    ], waits=[mmv] + ([ctx_last] if (tc == 0 and half == 0
                                                     and ctx_last) else [])))
            ones_v = s.stage("vector", [
                lambda e: e.memset(
                    TP[:, 0:4160].rearrange("p (a h d) -> p a h d", a=TC, d=65)
                    [:, :, :, 64:65], 1.0),
            ], waits=[vev[-1]])
            dgd = emit_diag(KN, kev,
                            extra_first=[] if ctx_last else list(prol_w))
            red_tok = {0: None, 1: None}
            for hp in range(8):
                par1 = hp % 2
                if hp == 0:
                    wm = [kev, dgd]
                elif hp == 1:
                    wm = [kev, dgd, vev[-1]]
                    if wq2["mean"] is not None:
                        wm.append(wq2["mean"])
                else:
                    wm = [red_tok[par1]]
                if t == 0 and hp == 0:
                    wm.append(pj_load)
                if par1 == 0:
                    zlo = lambda tc: DDP[:, tc * 256: tc * 256 + 256]
                    zhi = lambda tc: DDP[:, 1024 + tc * 256: 1024 + tc * 256 + 256]
                else:
                    zlo = lambda tc: (PAB_A[:, (tc % 2) * 256: (tc % 2) * 256 + 256]
                                      if tc < 2 else
                                      CTP[:, (tc % 2) * 256: (tc % 2) * 256 + 256])
                    zhi = lambda tc: (PAB_B[:, (tc % 2) * 256: (tc % 2) * 256 + 256]
                                      if tc < 2 else
                                      CTP[:, 512 + (tc % 2) * 256: 512 + (tc % 2) * 256 + 256])
                mm = s.stage("tensor", [
                    mmf
                    for tc in range(TC)
                    for mmf in [
                        (lambda e, hp=hp, tc=tc, z=zlo(tc): e.matmul(
                            z,
                            TP[0:64, KN + hp * 512 + tc * 128: KN + hp * 512 + tc * 128 + 128],
                            PJ[0:64, :], start=(tc % 2 == 0), stop=(tc % 2 == 1),
                            tile_position=(0, 0))),
                        (lambda e, hp=hp, tc=tc, z=zhi(tc): e.matmul(
                            z,
                            TP[64:128, KN + hp * 512 + tc * 128: KN + hp * 512 + tc * 128 + 128],
                            PJ[64:128, :], start=(tc % 2 == 0), stop=(tc % 2 == 1),
                            tile_position=(64, 0))),
                    ]
                ], waits=wm)
                if t == 0 and hp == 0:
                    s.mark("p1d1")
                if par1 == 0:
                    red_tok[0] = s.stage("vector", [
                        lambda e: e.reduce_max(
                            m_red2,
                            DDP[:, 0:2048].rearrange("p (h m) -> p h m", h=2),
                            AX.X),
                    ], waits=[mm])
                else:
                    red_tok[1] = s.stage("vector", [
                        lambda e: e.reduce_max(m_red4[:, 0:1], PAB_A[:, 0:512], AX.X),
                        lambda e: e.reduce_max(m_red4[:, 1:2], PAB_B[:, 0:512], AX.X),
                        lambda e: e.reduce_max(
                            m_red2,
                            CTP[:, 0:1024].rearrange("p (h m) -> p h m", h=2),
                            AX.X),
                    ], waits=[mm])
                    s.stage("vector", [
                        lambda e: e.tensor_max(m_red2, m_red2, m_red4),
                    ])
                    prev_s1red = red_tok[1]
                rm_done = s.stage("vector", [
                    (lambda e, hp=hp: e.tensor_max(
                        m_RM[:, 2 * hp:2 * hp + 2], m_RM[:, 2 * hp:2 * hp + 2],
                        m_red2)),
                ])
            if t == 0:
                s.mark("p1dd")
            ra = s.stage("vector", [
                lambda e: e.reduce_max(m_L1col, m_RM, AX.X)])
            ra_last = ra
            lch = {}

            def emit_lchain(dgd=dgd, ra=ra):
                rowm = s.stage("tensor", [
                    lambda e: e.matmul(CTP[0:1, 0:128], m_L1col, IDN32[:],
                                       start=True, stop=True)], waits=[ra])
                rL = s.stage("vector", [
                    lambda e: e.reduce_max(m_L1, CTP[0:1, 0:128], AX.X)],
                    waits=[rowm])
                bcm = s.stage("tensor", [
                    lambda e: e.matmul(CTP[:, 0:1], o32row, m_L1,
                                       start=True, stop=True)], waits=[rL])
                vL = s.stage("vector", [
                    lambda e: e.tensor_scalar(m_cLt, CTP[:, 0:1], -1.0, CSTAB,
                                              ALU.mult, ALU.add),
                    lambda e: e.tensor_scalar_mul(m_mLt, CTP[:, 0:1], -1.0),
                ], waits=[bcm])
                eD = s.stage("scalar", [
                    lambda e: e.activation(m_eD, m_Lprev, AF.Exp,
                                           bias=m_mLt[:, 0:1]),
                ], waits=[vL])
                s.stage("vector", [
                    lambda e: e.tensor_scalar_mul(m_Lprev, m_mLt, -1.0),
                ], waits=[eD])
                lch["edl"] = s.stage("vector", [
                    lambda e: e.tensor_scalar(m_edb, m_diag, -0.5,
                                              m_mLt[:, 0:1],
                                              ALU.mult, ALU.add),
                ], waits=[dgd, vL])

            prev_ctx_ev = None
            prev_readers = [red_tok[0], red_tok[1]]
            last_cmmz = {0: None, 1: None}
            cmm = None
            for hp in range(8):
                wm = [kev] + prev_readers
                hmm = s.stage("tensor", [
                    mmf
                    for tc in range(TC)
                    for mmf in [
                        (lambda e, hp=hp, tc=tc: e.matmul(
                            DDP[:, tc * 256: tc * 256 + 256],
                            TP[0:64, KN + hp * 512 + tc * 128: KN + hp * 512 + tc * 128 + 128],
                            PJ[0:64, :], start=(tc % 2 == 0), stop=(tc % 2 == 1),
                            tile_position=(0, 0))),
                        (lambda e, hp=hp, tc=tc: e.matmul(
                            DDP[:, 1024 + tc * 256: 1024 + tc * 256 + 256],
                            TP[64:128, KN + hp * 512 + tc * 128: KN + hp * 512 + tc * 128 + 128],
                            PJ[64:128, :], start=(tc % 2 == 0), stop=(tc % 2 == 1),
                            tile_position=(64, 0))),
                    ]
                ], waits=wm)
                if hp == 0:
                    emit_lchain()
                prev_readers = []
                for sub in range(2):
                    h = 2 * hp + sub
                    kz = TP[:, 8448:9472] if sub == 0 else HPf[:, 15360:16384]
                    kfw = [hmm, lch["edl"]]
                    if last_cmmz.get(sub) is not None:
                        kfw.append(last_cmmz[sub])
                    kf = s.stage("scalar", [
                        (lambda e, sub=sub, h=h, tc=tc, kz=kz: e.activation(
                            kz[:, tc * 256:(tc + 1) * 256],
                            DDP[:, sub * 1024 + tc * 256:
                                sub * 1024 + (tc + 1) * 256],
                            AF.Exp, bias=m_edb[:, tc * 16 + h: tc * 16 + h + 1]))
                        for tc in range(TC)
                    ], waits=kfw)
                    prev_readers.append(kf)
                    zb = h % 2
                    wcm = [kf, ones_v]
                    if prev_ctx_ev is not None and zb == 0:
                        wcm.append(prev_ctx_ev)
                    cmm = s.stage("tensor", [
                        (lambda e, ms=ms, tc=tc, h=h, zb=zb, kz=kz: e.matmul(
                            CTP[:, zb * 512 + ms * 65: zb * 512 + ms * 65 + 65],
                            kz[:, tc * 256 + ms * 128: tc * 256 + ms * 128 + 128],
                            TP[:, tc * 1040 + h * 65: tc * 1040 + h * 65 + 65],
                            start=(ms == 0 and tc == 0), stop=(ms == 1 and tc == TC - 1)))
                        for ms in range(2) for tc in range(TC)
                    ], waits=wcm)
                    last_cmmz[sub] = cmm
                prev_ctx_ev = s.stage("vector", [
                    (lambda e, hp=hp: e.scalar_tensor_tensor(
                        HP[:, 8192 + hp * 260: 8192 + (hp + 1) * 260]
                        .rearrange("p (z c) -> p z c", c=130),
                        HP[:, 8192 + hp * 260: 8192 + (hp + 1) * 260]
                        .rearrange("p (z c) -> p z c", c=130),
                        m_eD[:, 0:1],
                        CTP[:, 0:1024].rearrange("p (z c) -> p z c", z=2)[:, :, 0:130],
                        ALU.mult, ALU.add)),
                ], waits=[cmm, ctx_init])
            ctx_last = prev_ctx_ev
            if t < 3:
                emit_wq_pass1(22)
            elif t == 3:
                emit_wq_pass1(len(wq_items))
                emit_wq_mean()
                emit_wq_pass2(13)
            else:
                emit_wq_pass2(13)
        emit_wq_pass2(len(wq_items))
        wq_final = wq2["last"]
        s.mark("p1")

        ar_o = dma1("gpsimd", rmio.ap(), m_RM, waits=[rm_done], stream="dgo")
        import os as _os
        if _os.environ.get("MK_NOCC"):
            cc1 = dma1("gpsimd", rmo.ap(), rmio.ap(), waits=[ar_o], stream="dgo")
        else:
            cc1 = s.collective(
                lambda e: e.collective_compute(
                    "AllReduce", ALU.max, replica_groups=[list(range(NCORES))],
                    ins=[rmio.ap().opt()], outs=[rmo.ap().opt()]),
                waits=[ar_o])
        ar_i = dma1("gpsimd", m_RM, rmo.ap(), waits=[cc1, ra_last], stream="dgo")
        rr = s.stage("vector", [
            lambda e: e.reduce_max(m_red2[:, 0:1], m_RM, AX.X)], waits=[ar_i])
        sb_o = dma1("gpsimd", srow.ap(), m_red2[:, 0:1], waits=[rr], stream="dgo")
        sb_i = dma1("sync", m_srow, srow.ap().rearrange("a b -> b a"),
                    waits=[sb_o], stream="dsb", allow_slow_non_contiguous=True)
        sr = s.stage("vector", [
            lambda e: e.reduce_max(m_sca[:, 6:7], m_srow, AX.X)], waits=[sb_i])
        sbm = s.stage("tensor", [
            lambda e: e.matmul(CTP[:, 0:1], o32row, m_sca[:, 6:7],
                               start=True, stop=True)], waits=[sr, ctx_last])
        sev = s.stage("vector", [
            lambda e: e.tensor_copy(m_stab, CTP[:, 0:1]),
            lambda e: e.tensor_scalar_mul(m_mS, CTP[:, 0:1], -1.0),
        ], waits=[sbm])
        fst = s.stage("scalar", [
            lambda e: e.activation(m_f, m_Lprev, AF.Exp, bias=m_mS[:, 0:1]),
        ], waits=[sev])
        s.mark("ar")

        sxr = s.stage("vector", [
            (lambda e, kc=kc: e.reduce_sum(
                m_sx[:, kc:kc + 1],
                WP[:, RXL + kc * 4096: RXL + (kc + 1) * 4096], AX.X))
            for kc in range(EC)
        ])
        sxc = s.stage("vector", [
            lambda e: e.tensor_copy(TP[:, 8448:8456], m_sx),
        ])
        vsm = s.stage("tensor", [
            (lambda e, kc=kc, half=half: e.matmul(
                CTP[0:1, half * 512:(half + 1) * 512],
                TP[:, 8448 + kc: 8449 + kc],
                WP[:, RB + kc * 1024 + half * 512: RB + kc * 1024 + (half + 1) * 512],
                start=(kc == 0), stop=False))
            for half in range(2) for kc in range(EC)
        ] + [
            (lambda e, half=half: e.matmul(
                CTP[0:1, half * 512:(half + 1) * 512],
                k4096h, r_bv[:, half * 512:(half + 1) * 512],
                start=False, stop=True))
            for half in range(2)
        ], waits=[sxc, ctx_last])
        vse = s.stage("vector", [
            lambda e: e.tensor_copy(HP[0:1, 0:1024], CTP[0:1, 0:1024]),
        ], waits=[vsm])
        v2a = s.stage("vector", [
            (lambda e, ms=ms: e.tensor_scalar_mul(
                HP[0:1, 1024:3104].rearrange("a (h m c) -> a h m c", m=2, c=65)
                [:, :, ms:ms + 1, 0:64],
                HP[0:1, 0:1024].rearrange("a (h one d) -> a h one d", one=1, d=64),
                KEPS))
            for ms in range(2)
        ] + [
            lambda e: e.memset(
                HP[0:1, 1024:3104].rearrange("a (h m c) -> a h m c", m=2, c=65)
                [:, :, :, 64:65], KEPS * 4096.0),
        ], waits=[vse])
        vsb = s.stage("tensor", [
            (lambda e, c=c: e.matmul(
                DDP[:, c * 512:(c + 1) * 512], o32row,
                HP[0:1, 1024 + c * 512: 1024 + (c + 1) * 512],
                start=True, stop=True))
            for c in range(4)
        ] + [
            lambda e: e.matmul(CTP[:, 0:32], o32row, HP[0:1, 3072:3104],
                               start=True, stop=True),
        ], waits=[v2a])
        vsc = s.stage("vector", [
            (lambda e, c=c: e.tensor_copy(HP[:, c * 512:(c + 1) * 512],
                                          DDP[:, c * 512:(c + 1) * 512]))
            for c in range(4)
        ] + [
            lambda e: e.tensor_copy(HP[:, 2048:2080], CTP[:, 0:32]),
        ], waits=[vsb])
        cxc = s.stage("vector", [
            lambda e: e.scalar_tensor_tensor(
                CXP[:], HP[:, 8192:10272], m_f[:, 0:1], HP[:, 0:2080],
                ALU.mult, ALU.add),
        ], waits=[vsc, fst])
        ebm = s.stage("tensor", [
            (lambda e, h=h, ms=ms: e.matmul(
                DDP[0:1, h * 128: h * 128 + 65], ones16col,
                CXP[:, h * 130 + ms * 65: h * 130 + ms * 65 + 65],
                start=(ms == 0 and h % 4 == 0), stop=(ms == 1 and h % 4 == 3)))
            for h in range(H) for ms in range(2)
        ], waits=[cxc])
        ebv = s.stage("vector", [
            lambda e: e.tensor_scalar_mul(
                r_eps.rearrange("a (h c) -> a h c", c=65),
                DDP[0:1, 0:2048].rearrange("p (h c) -> p h c", h=16)[:, :, 0:65], KEPS),
        ], waits=[ebm])

        w2ab = s.dmas("sync", [
            (lambda e, kc=kc: e.dma_start(
                out=WP[:, w2t_region(kc): w2t_region(kc) + 1024],
                in_=ws2[:, kc * 128:(kc + 1) * 128], transpose=True))
            for kc in range(16)
        ], waits=[wq_final, vsm])

        s.mark("p2")
        x2st = [None] * NT
        p3a_last = None
        for t in range(NT):
            w0 = [ctx_last] if t == 0 else [p3a_last, x2st[t - 1]]
            xl = s.dmas("sync", [
                (lambda e, t=t: e.dma_start(
                    out=HP[:, 0:4096].rearrange("p (a m) -> p a m", a=TC),
                    in_=xin[t * 512:(t + 1) * 512, :].rearrange("(a p) m -> p a m", p=128)))
            ], waits=w0 + ([cxc] if t == 0 else []), stream="dxa")
            qev = emit_proj(RC, m_cbq, t,
                            lambda ec: TP[:, KN + ec * 512: KN + (ec + 1) * 512], w0)
            dgd = emit_diag(KN, qev,
                            extra_first=w0 + ([ebv] if t == 0 else []))
            prev_out_ev = None
            prev_readers = [dgd]
            last_qf = {0: None, 1: None}
            last_eqf = {0: None, 1: None}
            last_ommz = {0: None, 1: None}
            for hp in range(8):
                wm = [qev] + prev_readers
                hmm = s.stage("tensor", [
                    mmf
                    for tc in range(TC)
                    for mmf in [
                        (lambda e, hp=hp, tc=tc: e.matmul(
                            DDP[:, tc * 256: tc * 256 + 256],
                            TP[0:64, KN + hp * 512 + tc * 128: KN + hp * 512 + tc * 128 + 128],
                            PJ[0:64, :], start=(tc % 2 == 0), stop=(tc % 2 == 1),
                            tile_position=(0, 0))),
                        (lambda e, hp=hp, tc=tc: e.matmul(
                            DDP[:, 1024 + tc * 256: 1024 + tc * 256 + 256],
                            TP[64:128, KN + hp * 512 + tc * 128: KN + hp * 512 + tc * 128 + 128],
                            PJ[64:128, :], start=(tc % 2 == 0), stop=(tc % 2 == 1),
                            tile_position=(64, 0))),
                    ]
                ], waits=wm)
                nst = s.stage("vector", [
                    (lambda e, sub=sub, tc=tc, hp=hp: e.reduce_max(
                        m_nst[:, tc * 16 + 2 * hp + sub: tc * 16 + 2 * hp + sub + 1],
                        DDP[:, sub * 1024 + tc * 256: sub * 1024 + (tc + 1) * 256],
                        AX.X, negate=True))
                    for sub in range(2) for tc in range(TC)
                ], waits=[hmm])
                edq = s.stage("vector", [
                    (lambda e, hp=hp: e.scalar_tensor_tensor(
                        m_edb.rearrange("p (a h) -> p a h", a=TC)[:, :, 2 * hp:2 * hp + 2],
                        m_diag.rearrange("p (a h) -> p a h", a=TC)[:, :, 2 * hp:2 * hp + 2],
                        -0.5,
                        m_nst.rearrange("p (a h) -> p a h", a=TC)[:, :, 2 * hp:2 * hp + 2],
                        ALU.mult, ALU.add)),
                ], waits=[nst, dgd])

                prev_readers = [nst]
                ozb = lambda sub, tc: (tc // 2) * 512 + sub * 256 + (tc % 2) * 128
                ozoneP = [[CTP[:, ozb(s_, tc): ozb(s_, tc) + 65] for tc in range(4)]
                          for s_ in range(2)]
                ozcolP = [[CTP[:, ozb(s_, tc) + 64: ozb(s_, tc) + 65] for tc in range(4)]
                          for s_ in range(2)]
                oznumP = [[CTP[:, ozb(s_, tc): ozb(s_, tc) + 64] for tc in range(4)]
                          for s_ in range(2)]
                omms = []
                qzs = [TP[:, 8448:9472], HPf[:, 16384:18432]]
                qfs = []
                for sub in range(2):
                    h = 2 * hp + sub
                    qz = qzs[sub]
                    qfw = [hmm, edq]
                    if last_ommz.get(sub) is not None:
                        qfw.append(last_ommz[sub])
                    qf = s.stage("scalar", [
                        (lambda e, sub=sub, tc=tc, h=h, qz=qz: e.activation(
                            qz[:, tc * 256:(tc + 1) * 256],
                            DDP[:, sub * 1024 + tc * 256:
                                sub * 1024 + (tc + 1) * 256],
                            AF.Exp, bias=m_edb[:, tc * 16 + h: tc * 16 + h + 1]))
                        for tc in range(TC)
                    ], waits=qfw)
                    last_qf[sub] = qf
                    prev_readers.append(qf)
                    qfs.append(qf)
                for sub in range(2):
                    h = 2 * hp + sub
                    par = sub
                    qz = qzs[sub]
                    pz = PA16 if par == 0 else PB16
                    ozone = ozoneP[par]
                    tqw = [qfs[sub]]
                    if last_eqf.get(par) is not None:
                        tqw.append(last_eqf[par])
                    tqf = s.stage("tensor", [
                        (lambda e, k=k, qz=qz, pz=pz: e.transpose(
                            pz[:, k * 128:(k + 1) * 128],
                            qz[:, k * 128:(k + 1) * 128],
                            IDN[:]))
                        for k in range(8)
                    ], waits=tqw)
                    eqf = s.stage("vector", [
                        (lambda e, k=k, qz=qz, pz=pz: e.tensor_copy(
                            qz[:, k * 128:(k + 1) * 128],
                            pz[:, k * 128:(k + 1) * 128]))
                        for k in range(8)
                    ], waits=[tqf])
                    last_eqf[par] = eqf
                    ommw = [eqf, qfs[0], qfs[1], cxc, ebv]
                    if sub == 0 and prev_out_ev is not None:
                        ommw.append(prev_out_ev)
                    omm = s.stage("tensor", [
                        mmf
                        for tc in range(TC)
                        for mmf in [
                            (lambda e, tc=tc, h=h, qz=qz, ozone=ozone, sub=sub: e.matmul(
                                ozone[tc],
                                qz[:, tc * 256: tc * 256 + 128],
                                CXP[:, h * 130: h * 130 + 65],
                                start=(sub == 0 and tc % 2 == 0), stop=False)),
                            (lambda e, tc=tc, h=h, qz=qz, ozone=ozone: e.matmul(
                                ozone[tc],
                                qz[:, tc * 256 + 128: tc * 256 + 256],
                                CXP[:, h * 130 + 65: h * 130 + 130],
                                start=False, stop=False)),
                            (lambda e, tc=tc, h=h, ozone=ozone, sub=sub: e.matmul(
                                ozone[tc],
                                ones16row32, r_eps[:, h * 65: h * 65 + 65],
                                start=False, stop=(sub == 1 and tc % 2 == 1))),
                        ]
                    ], waits=ommw)
                    last_ommz[par] = omm
                    omms.append(omm)
                dvs = s.stage("vector", [
                    (lambda e, sub=sub, tc=tc: e.reciprocal(
                        m_dinv8[:, sub * 4 + tc: sub * 4 + tc + 1],
                        ozcolP[sub][tc]))
                    for sub in range(2) for tc in range(TC)
                ], waits=[omms[1]])
                prev_out_ev = s.stage("scalar", [
                    (lambda e, sub=sub, tc=tc, h=2 * hp + sub: e.activation(
                        TP[:, tc * 1024 + h * 64: tc * 1024 + h * 64 + 64],
                        oznumP[sub][tc], AF.Copy,
                        scale=m_dinv8[:, sub * 4 + tc: sub * 4 + tc + 1]))
                    for sub in range(2) for tc in range(TC)
                ], waits=[dvs])
            trd = emit_tblock(
                lambda k: TP[:, (k // EC) * 1024 + (k % EC) * 128:
                             (k // EC) * 1024 + (k % EC) * 128 + 128],
                lambda k: TP[:, KN + (k % EC) * 512 + (k // EC) * 128:
                             KN + (k % EC) * 512 + (k // EC) * 128 + 128],
                TC * EC, prev_out_ev, PB16)
            x2ev = []
            for tc in range(TC):
                for half in range(2):
                    pab = PAB_A if half == 0 else PAB_B
                    wv = [trd, xl] if tc == 0 else []
                    if len(x2ev) >= 2:
                        wv.append(x2ev[-2])
                    mmo = s.stage("tensor", [
                        (lambda e, kc=kc, tc=tc, half=half, pab=pab: e.matmul(
                            pab[:, 0:512],
                            TP[:, KN + kc * 512 + tc * 128: KN + kc * 512 + tc * 128 + 128],
                            WP[:, RD + kc * 1024 + half * 512: RD + kc * 1024 + (half + 1) * 512],
                            start=(kc == 0), stop=False))
                        for kc in range(EC)
                    ] + [
                        (lambda e, half=half, pab=pab: e.matmul(
                            pab[:, 0:512], ones16row32,
                            BR[32:33, half * 512:(half + 1) * 512],
                            start=False, stop=True)),
                    ], waits=wv)
                    x2ev.append(s.stage("vector", [
                        (lambda e, tc=tc, half=half, pab=pab: e.tensor_add(
                            HP[:, 4096 + tc * 1024 + half * 512:
                               4096 + tc * 1024 + (half + 1) * 512],
                            pab[:, 0:512],
                            HP[:, tc * 1024 + half * 512: tc * 1024 + (half + 1) * 512])),
                    ], waits=[mmo]))
            p3a_last = x2ev[-1]
            x2st[t] = s.dmas("gpsimd", [
                (lambda e, t=t, tc=tc: e.dma_start(
                    out=x2s[(t * TC + tc) * 128:(t * TC + tc + 1) * 128, :],
                    in_=HP[:, 4096 + tc * 1024: 4096 + (tc + 1) * 1024]))
                for tc in range(TC)
            ], waits=[x2ev[-1]])

        s.mark("p3a")
        p3ab = s.barrier()
        w1ld = s.dmas("sync", [
            (lambda e, fc=fc: e.dma_start(
                out=WP[:, RXL + fc * 4096: RXL + (fc + 1) * 4096],
                in_=ws1[:, fc * 128:(fc + 1) * 128], transpose=True))
            for fc in range(EC)
        ], waits=p3ab)
        w2cd = s.dmas("scalar", [
            (lambda e, kc=kc: e.dma_start(
                out=WP[:, w2t_region(kc): w2t_region(kc) + 1024],
                in_=ws2[:, kc * 128:(kc + 1) * 128], transpose=True))
            for kc in range(16, 32)
        ], waits=p3ab)
        gb_last = None
        for idx, vecd in enumerate([gammad, betad]):
            for hlf in range(4):
                gl = dma1("sync", m_s1row,
                          vecd[None, hlf * 256:(hlf + 1) * 256],
                          waits=list(p3ab) + ([gb_last] if gb_last else []))
                gm = s.stage("tensor", [
                    lambda e: e.matmul(CTP[:, 0:256], o32row, m_s1row,
                                       start=True, stop=True),
                ], waits=[gl] + ([gb_last] if gb_last else []))
                gb_last = s.stage("vector", [
                    (lambda e, idx=idx, hlf=hlf: e.tensor_copy(
                        CXP[:, idx * 1024 + hlf * 256: idx * 1024 + (hlf + 1) * 256],
                        CTP[:, 0:256])),
                ], waits=[gm])

        s.mark("bnd")
        T2 = 256
        NT2 = N // T2
        TC2 = 2
        p3bw = s.barrier()
        prev_store = None
        prev_oev = None
        for t2 in range(NT2):
            w0 = list(p3bw) if t2 == 0 else [prev_store]
            xq_done = []
            x2la = s.dmas("sync", [
                (lambda e, g=t2 * TC2: e.dma_start(
                    out=TPf[:, 2048:3072],
                    in_=x2s[g * 128:(g + 1) * 128, 0:1024])),
            ], waits=list(w0) + [x2st[t2 // 2]], stream="dxa")
            x2lb = s.dmas("sync", [
                (lambda e, g=t2 * TC2 + 1: e.dma_start(
                    out=TPf[:, 3072:4096],
                    in_=x2s[g * 128:(g + 1) * 128, 0:1024])),
            ], waits=[], stream="dxa")
            s.stage("vector", [
                lambda e: e.bn_stats(m_bn2[:, 0:6], TPf[:, 2048:2560]),
                lambda e: e.bn_stats(m_bn2[:, 6:12], TPf[:, 2560:3072]),
                lambda e: e.bn_stats(m_bn2[:, 12:18], TPf[:, 3072:3584]),
                lambda e: e.bn_stats(m_bn2[:, 18:24], TPf[:, 3584:4096]),
            ], waits=[x2la, x2lb])
            ag = s.stage("vector", [
                lambda e: e.bn_aggr(m_mv2[:, 0:2], m_bn2[:, 0:12]),
                lambda e: e.bn_aggr(m_mv2[:, 2:4], m_bn2[:, 12:24]),
            ])
            rs = s.stage("scalar", [
                lambda e: e.activation(m_sd2[:, 0:1], m_mv2[:, 1:2],
                                       AF.Sqrt, bias=kLNEPS),
                lambda e: e.activation(m_sd2[:, 1:2], m_mv2[:, 3:4],
                                       AF.Sqrt, bias=kLNEPS),
            ], waits=[ag])
            s.stage("vector", [
                (lambda e, tc=tc: e.scalar_tensor_tensor(
                    HP[:, tc * 1024:(tc + 1) * 1024],
                    TPf[:, 2048 + tc * 1024: 2048 + (tc + 1) * 1024],
                    m_mv2[:, 2 * tc: 2 * tc + 1],
                    CXP[:, 0:1024], ALU.subtract, ALU.mult))
                for tc in range(TC2)
            ])
            s.stage("vector", [
                lambda e: e.reciprocal(m_rstd2[:, 0:2], m_sd2[:, 0:2]),
            ], waits=[rs])
            s.stage("vector", [
                (lambda e, tc=tc: e.scalar_tensor_tensor(
                    HP[:, 2048 + tc * 1024: 2048 + (tc + 1) * 1024],
                    HP[:, tc * 1024:(tc + 1) * 1024],
                    m_rstd2[:, tc:tc + 1],
                    CXP[:, 1024:2048], ALU.mult, ALU.add))
                for tc in range(TC2)
            ])
            s.stage("vector", [
                (lambda e, tc=tc: e.reduce_max(
                    m_rx1b[:, tc:tc + 1],
                    HP[:, 2048 + tc * 1024: 2048 + (tc + 1) * 1024],
                    AX.X, apply_absolute_value=True))
                for tc in range(TC2)
            ])
            s.stage("vector", [
                lambda e: e.tensor_scalar(m_mx1b[:, 0:2], m_rx1b[:, 0:2],
                                          1e-5, 1.0 / 127.0, ALU.max, ALU.mult),
            ])
            rcp = s.stage("vector", [
                lambda e: e.reciprocal(m_s1cb[:, 0:2], m_mx1b[:, 0:2]),
                lambda e: e.tensor_single_scalar(m_s1i[:, 0:2], m_mx1b[:, 0:2],
                                                 m_mw1d[:, 0:1], ALU.mult),
            ])
            u1 = s.stage("scalar", [
                (lambda e, tc=tc: e.activation(
                    TP[:, tc * 1024:(tc + 1) * 1024],
                    HP[:, 2048 + tc * 1024: 2048 + (tc + 1) * 1024],
                    AF.Copy, bias=1536.0, scale=m_s1cb[:, tc:tc + 1]))
                for tc in range(TC2)
            ], waits=[rcp])
            for tc in range(TC2):
                tp_ = s.stage("tensor", [
                    (lambda e, fc=i, tc=tc: e.transpose(
                        PA16[:, fc * 128:(fc + 1) * 128],
                        TP[:, tc * 1024 + fc * 128: tc * 1024 + fc * 128 + 128],
                        IDN[:]))
                    for i in range(8)
                ], waits=[u1] + ([xq_done[-1]] if xq_done else []))
                xq_done.append(s.stage("vector", [
                    (lambda e, fc=i, tc=tc: e.tensor_scalar_add(
                        TP[:, 2048 + fc * 256 + tc * 128: 2048 + fc * 256 + tc * 128 + 128],
                        PA16[:, fc * 128:(fc + 1) * 128], -1536.0))
                    for i in range(8)
                ], waits=[tp_]))
            s1o = s.dmas("gpsimd", [
                lambda e: e.dma_start(out=rmio[:, 0:2], in_=m_s1i[:, 0:2]),
            ], waits=[xq_done[-1]], stream="dgo")
            s1i_ = s.dmas("sync", [
                lambda e: e.dma_start(out=m_s1row[:, 0:256].rearrange("a (c p) -> a c p", p=128),
                                      in_=rmio[:, 0:2].rearrange("p c -> c p")[None],
                                      allow_slow_non_contiguous=True),
            ], waits=[s1o], stream="dsb")
            sbm1 = s.stage("tensor", [
                (lambda e, tc=tc: e.matmul(
                    CTP[:, tc * 128:(tc + 1) * 128], o32row,
                    m_s1row[:, tc * 128:(tc + 1) * 128],
                    start=(tc == 0), stop=(tc == TC2 - 1)))
                for tc in range(TC2)
            ], waits=[s1i_])
            sbe1 = s.stage("vector", [
                lambda e: e.tensor_copy(TPf[:, 0:256], CTP[:, 0:256]),
            ], waits=[sbm1])
            sc1_hist = []
            m1_done = None
            for r in range(4):
                if r % 2 == 0:
                    zones = [DDP[:, 0:512], DDP[:, 512:1024],
                             DDP[:, 1024:1536], DDP[:, 1536:2048]]
                else:
                    zones = [PAB_A[:, 0:512], PAB_B[:, 0:512],
                             CTP[:, 0:512], CTP[:, 512:1024]]
                if r == 0:
                    wv = [xq_done[-1], w1ld]
                elif r == 1:
                    wv = [sbe1]
                else:
                    wv = [sc1_hist[r - 2]]
                mmh = s.stage("tensor", [
                    (lambda e, fc=fc, hc=r * 8 + j, z=zones[j // 2],
                     o=(j % 2) * 256,
                     st=(j % 2 == 0 and fc == 0),
                     sp=(j % 2 == 1 and fc == EC - 1): e.matmul(
                        z[:, o:o + 256],
                        WP[:, RXL + fc * 4096 + hc * 128: RXL + fc * 4096 + hc * 128 + 128],
                        TP[:, 2048 + fc * 256: 2048 + (fc + 1) * 256],
                        start=st, stop=sp))
                    for j in range(8)
                    for fc in range(EC)
                ], waits=wv)
                sc1 = s.stage("vector", [
                    (lambda e, hc=r * 8 + j, z=zones[j // 2], o=(j % 2) * 256:
                     e.tensor_mul(
                        HP[:, hc * 256:(hc + 1) * 256], z[:, o:o + 256],
                        TPf[:, 0:256]))
                    for j in range(8)
                ], waits=[mmh])
                sc1_hist.append(sc1)
                gl = s.stage("scalar", [
                    (lambda e, hc=r * 8 + j: e.activation(
                        HP[:, hc * 256:(hc + 1) * 256],
                        HP[:, hc * 256:(hc + 1) * 256],
                        AF.Gelu, bias=m_b1h[:, hc:hc + 1]))
                    for j in range(8)
                ], waits=[sc1])
                if r == 0:
                    m1_done = s.stage("vector", [
                        lambda e: e.tensor_copy(HP[:, 8192:10240], HP[:, 0:2048]),
                    ], waits=[gl])
                else:
                    m1_done = s.stage("vector", [
                        (lambda e, r=r: e.tensor_max(
                            HP[:, 8192:10240], HP[:, 8192:10240],
                            HP[:, r * 2048:(r + 1) * 2048])),
                    ], waits=[gl, m1_done])
            m1_done = s.stage("vector", [
                lambda e: e.tensor_max(HP[:, 8192:9216], HP[:, 8192:9216],
                                       HP[:, 9216:10240]),
            ], waits=[m1_done])
            m1_done = s.stage("vector", [
                lambda e: e.tensor_max(HP[:, 8192:8704], HP[:, 8192:8704],
                                       HP[:, 8704:9216]),
            ])
            m1_done = s.stage("vector", [
                lambda e: e.tensor_max(HP[:, 8192:8448], HP[:, 8192:8448],
                                       HP[:, 8448:8704]),
            ])
            m132 = s.stage("tensor", [
                (lambda e, tc=tc: e.transpose(
                    CTP[:, tc * 128:(tc + 1) * 128],
                    HP[:, 8192 + tc * 128: 8192 + (tc + 1) * 128], IDN32[:]))
                for tc in range(TC2)
            ], waits=[m1_done])
            s.stage("vector", [
                (lambda e, tc=tc: e.reduce_max(
                    m_rx2[:, tc:tc + 1], CTP[:, tc * 128:(tc + 1) * 128], AX.X))
                for tc in range(TC2)
            ], waits=[m132])
            s.stage("vector", [
                lambda e: e.tensor_scalar(m_mx2[:, 0:2], m_rx2[:, 0:2],
                                          1e-5, 1.0 / 127.0, ALU.max, ALU.mult),
            ])
            s.stage("vector", [
                lambda e: e.reciprocal(m_s2c[:, 0:2], m_mx2[:, 0:2]),
                lambda e: e.tensor_single_scalar(m_s2i[:, 0:2], m_mx2[:, 0:2],
                                                 m_mw2d[:, 0:1], ALU.mult),
            ])
            rx2 = s.stage("vector", [
                lambda e: e.tensor_single_scalar(m_f2[:, 0:2], m_s2c[:, 0:2],
                                                 m_sw2b[:, 0:1], ALU.mult),
            ])
            s2o = s.dmas("gpsimd", [
                lambda e: e.dma_start(out=rmio[:, 0:2], in_=m_s2c[:, 0:2]),
                lambda e: e.dma_start(out=rmio[:, 2:4], in_=m_f2[:, 0:2]),
            ], waits=[rx2], stream="dgo")
            s2i_ = s.dmas("sync", [
                lambda e: e.dma_start(out=m_s1row[:, 0:256].rearrange("a (c p) -> a c p", p=128),
                                      in_=rmio[:, 0:2].rearrange("p c -> c p")[None],
                                      allow_slow_non_contiguous=True),
            ], waits=[s2o], stream="dsb")
            f2i_ = s.dmas("gpsimd", [
                lambda e: e.dma_start(out=r_f2[:, 0:256].rearrange("a (c p) -> a c p", p=128),
                                      in_=rmio[:, 2:4].rearrange("p c -> c p")[None],
                                      allow_slow_non_contiguous=True),
            ], waits=[s2o], stream="dgo")
            sbm2 = s.stage("tensor", [
                (lambda e, tc=tc: e.matmul(
                    CTP[:, tc * 128:(tc + 1) * 128], o32row,
                    m_s1row[:, tc * 128:(tc + 1) * 128],
                    start=(tc == 0), stop=(tc == TC2 - 1)))
                for tc in range(TC2)
            ], waits=[s2i_, rx2])
            sbe2 = s.stage("vector", [
                lambda e: e.tensor_copy(TPf[:, 256:512], CTP[:, 0:256]),
            ], waits=[sbm2])
            hq_prev = None
            for k0 in range(0, HC, 8):
                wq_ = [sbe2]
                if hq_prev is not None:
                    wq_.append(hq_prev)
                mul = s.stage("vector", [
                    (lambda e, kc=k0 + i: e.tensor_mul(
                        HP[:, kc * 256:(kc + 1) * 256],
                        HP[:, kc * 256:(kc + 1) * 256], TPf[:, 256:512]))
                    for i in range(8)
                ], waits=wq_)
                def _adds(kcs):
                    return [
                        (lambda e, kc=kc: e.tensor_scalar_add(
                            HPf[:, kc * 256:(kc + 1) * 256],
                            HP[:, kc * 256:(kc + 1) * 256], 1536.0))
                        for kc in kcs
                    ]
                if k0 == 0:
                    s.stage("vector", [
                        lambda e: e.tensor_scalar_add(TP[:, 0:256], HP[:, 0:256], 1536.0),
                    ], waits=[mul])
                    s.stage("vector", _adds([1]) + [
                        lambda e: e.tensor_scalar_add(TP[:, 0:256], TP[:, 0:256], -1536.0),
                    ])
                    s.stage("vector", _adds([2, 3]))
                    hq_prev = s.stage("vector", _adds([4, 5, 6, 7]))
                else:
                    hq_prev = s.stage("vector", _adds(range(k0, k0 + 8)), waits=[mul])
            hq2 = s.stage("vector", [
                lambda e: e.tensor_scalar_add(HPf[:, 256:4096], HPf[:, 256:4096], -1536.0),
                lambda e: e.tensor_scalar_add(HPf[:, 4096:8192], HPf[:, 4096:8192], -1536.0),
            ], waits=[hq_prev])
            last_ev1 = None
            for tc in range(TC2):
                g = t2 * TC2 + tc
                x2l2 = s.dmas("sync", [
                    (lambda e, g=g: e.dma_start(
                        out=TPf[:, 2048:3072],
                        in_=x2s[g * 128:(g + 1) * 128, 0:1024])),
                ], waits=[hq2] if tc == 0 else [prev_store], stream="dxa")
                oev = []
                for half in range(2):
                    pab = PAB_A if half == 0 else PAB_B
                    wv = [hq2, w2ab, w2cd, f2i_, b2_cast] if tc == 0 else [hq2]
                    if last_ev1 is not None:
                        wv.append(last_ev1)
                    mmo = s.stage("tensor", [
                        (lambda e, kc=kc, tc=tc, half=half, pab=pab: e.matmul(
                            pab[:, 0:512],
                            (TP[:, tc * 128: tc * 128 + 128] if kc == 0 else
                             HPf[:, kc * 256 + tc * 128: kc * 256 + tc * 128 + 128]),
                            WP[:, w2t_region(kc) + half * 512: w2t_region(kc) + (half + 1) * 512],
                            start=(kc == 0), stop=False))
                        for kc in range(HC)
                    ] + [
                        (lambda e, tc=tc, half=half, pab=pab: e.matmul(
                            pab[:, 0:512],
                            r_f2[:, tc * 128:(tc + 1) * 128],
                            r_b2[:, half * 512:(half + 1) * 512],
                            start=False, stop=True, tile_position=(64, 0))),
                    ], waits=wv)
                    ev1w = [mmo]
                    if oev:
                        ev1w.append(oev[-1])
                    elif prev_oev is not None:
                        ev1w.append(prev_oev)
                    ev1 = s.stage("scalar", [
                        (lambda e, tc=tc, half=half, pab=pab: e.activation(
                            TPf[:, 1024 + half * 512: 1024 + (half + 1) * 512],
                            pab[:, 0:512], AF.Copy, scale=m_s2i[:, tc:tc + 1])),
                    ], waits=ev1w)
                    last_ev1 = ev1
                    oev.append(s.stage("vector", [
                        (lambda e, half=half: e.tensor_add(
                            TPf[:, 3072 + half * 512: 3072 + (half + 1) * 512],
                            TPf[:, 1024 + half * 512: 1024 + (half + 1) * 512],
                            TPf[:, 2048 + half * 512: 2048 + (half + 1) * 512])),
                    ], waits=[ev1, x2l2]))
                prev_oev = oev[-1]
                prev_store = s.dmas("gpsimd", [
                    (lambda e, g=g: e.dma_start(
                        out=outd[g * 128:(g + 1) * 128, :],
                        in_=TPf[:, 3072:4096]))
                ], waits=[oev[0], oev[1]])

        s.mark("all")
        import os
        trunc = os.environ.get("MK_TRUNCATE", "")
        if trunc:
            s.truncate(trunc)
        fin = s.barrier()
        s.prog["gpsimd"].append(("fin", fin, [], None))

        with nc.Block() as block:
            def mk(engname):
                def body(eng):
                    hw = {}
                    for kind, waits, fns, inc in s.prog[engname]:
                        for (nm, val) in waits:
                            if hw.get(nm, -1) >= val:
                                continue
                            hw[nm] = val
                            eng.wait_ge(sems[nm], val)
                        if kind == "fin":
                            continue
                        if kind == "ccstage":
                            last = None
                            for fn in fns:
                                last = fn(eng)
                            nm, v = inc
                            last.then_inc(sems[nm], 1)
                        elif kind == "pestage":
                            last = None
                            for fn in fns:
                                last = fn(eng)
                            nm, v = inc
                            last.then_inc(sems[nm], 1)
                        elif kind == "stage":
                            for fn in fns:
                                fn(eng)
                            nm, v = inc
                            eng.drain().then_inc(sems[nm], 1)
                        else:
                            nm, v = inc
                            for fn in fns:
                                fn(eng).then_inc(sems[nm], 16)
                return body
            block.sync(mk("sync"))
            block.vector(mk("vector"))
            block.scalar(mk("scalar"))
            block.tensor(mk("tensor"))
            block.gpsimd(mk("gpsimd"))
    return nc


_NC_CACHE = {}


def kernel(**inputs):
    if "nc" not in _NC_CACHE:
        _NC_CACHE["nc"] = build_program()
    nc = _NC_CACHE["nc"]
    x = np.ascontiguousarray(np.asarray(inputs["L0_attn_x"], dtype=np.float32))
    f32 = lambda k: np.ascontiguousarray(np.asarray(inputs[k], dtype=np.float32))
    shared = {
        "Wq": f32("L0_attn_Wq"), "Wk": f32("L0_attn_Wk"),
        "Wv": f32("L0_attn_Wv"), "Wo": f32("L0_attn_Wo"),
        "bq": f32("L0_attn_bq"), "bk": f32("L0_attn_bk"),
        "bv": f32("L0_attn_bv"), "bo": f32("L0_attn_bo"),
        "proj": f32("L0_attn_proj"),
        "w1": f32("L0_ffn_w1"), "b1": f32("L0_ffn_b1"),
        "w2": f32("L0_ffn_w2"), "b2": f32("L0_ffn_b2"),
        "gamma": f32("L0_ffn_gamma"), "beta": f32("L0_ffn_beta"),
    }
    in_maps = [dict(shared, x=x[c]) for c in range(NCORES)]
    res = run_bass_kernel_spmd(nc, in_maps, core_ids=list(range(NCORES)))
    return np.stack([np.asarray(res.results[c]["out"]) for c in range(NCORES)],
                    axis=0)
